# version 29
# speedup vs baseline: 1.1956x; 1.1457x over previous
import numpy as np
import ml_dtypes
import concourse.bass as bass
import concourse.mybir as mybir
from concourse.bass_utils import run_bass_kernel_spmd
from contextlib import ExitStack

F32 = mybir.dt.float32
BF16 = mybir.dt.bfloat16
AF = mybir.ActivationFunctionType
ALU = mybir.AluOpType
AX = mybir.AxisListType

S, D = 4096, 2048
NCORE = 8
EPS_A = 1e-8
LN_EPS = 1e-5

ALL_PHASES = ("P0", "A1", "MID", "A2", "F1", "B1", "B2", "F2")
DBG = {"no_out": False, "no_otr": False, "no_oev_wait": False}
ASYNC_F = False
LAST_CTRS = (0, 0)
GSEM = [None, [0]]


class Tag:
    def __init__(self, eng, sem, ctr, dsem, dctr):
        self._eng = eng
        self._sem = sem
        self._ctr = ctr
        self._dsem = dsem
        self._dctr = dctr

    def ldma(self, out, in_, **k):
        self._eng.dma_start(out=out, in_=in_, **k).then_inc(self._dsem, 16)
        self._dctr[0] += 16
        return self._dctr[0]

    def gdma(self, out, in_, **k):
        self._eng.dma_start(out=out, in_=in_, **k).then_inc(GSEM[0], 16)
        GSEM[1][0] += 16
        return GSEM[1][0]

    def gwait(self, val):
        if val is not None:
            self._eng.wait_ge(GSEM[0], val)

    def dwait(self, val):
        if val is not None:
            self._eng.wait_ge(self._dsem, val)

    def __getattr__(self, name):
        f = getattr(self._eng, name)

        def g(*a, inc=1, **k):
            inst = f(*a, **k)
            inst.then_inc(self._sem, inc)
            self._ctr[0] += inc
            return inst

        return g


class Stepper:

    def __init__(self, block, sem, ctr, dsem=None, dctr=None):
        self.block = block
        self.sem = sem
        self.ctr = ctr
        self.dsem = dsem
        self.dctr = dctr

    def step(self, **eng_fns):
        base = self.ctr[0]
        for ename, fn in eng_fns.items():
            def body(eng, fn=fn, base=base):
                eng.wait_ge(self.sem, base)
                fn(Tag(eng, self.sem, self.ctr, self.dsem, self.dctr), eng)

            getattr(self.block, ename)(body)

    def drain_lazy(self, ename="sync"):
        base = self.ctr[0]
        tgt = self.dctr[0]

        def body(eng, base=base, tgt=tgt):
            eng.wait_ge(self.sem, base)
            eng.wait_ge(self.dsem, tgt)
            eng.nop().then_inc(self.sem, 1)

        getattr(self.block, ename)(body)
        self.ctr[0] += 1


def _col_layout(b):
    n = b.shape[0] // 128
    return np.ascontiguousarray(b.reshape(n, 128).T.astype(np.float32))


def host_prep(inputs):
    f = np.float32
    W = {}
    W["wqT"] = np.ascontiguousarray(np.asarray(inputs["L0_attn_a_Wq"], f).T)
    W["wkT"] = np.ascontiguousarray(np.asarray(inputs["L0_attn_a_Wk"], f).T)
    W["wvT"] = np.ascontiguousarray(np.asarray(inputs["L0_attn_a_Wv"], f).T)
    W["wgT"] = np.ascontiguousarray(np.asarray(inputs["L0_attn_a_Wg"], f).T)
    W["wmbT"] = np.ascontiguousarray(np.asarray(inputs["L0_attn_a_Wmb"], f).T)
    W["bq_c"] = _col_layout(np.asarray(inputs["L0_attn_a_bq"], f))
    W["bk_c"] = _col_layout(np.asarray(inputs["L0_attn_a_bk"], f))
    W["bv_c"] = _col_layout(np.asarray(inputs["L0_attn_a_bv"], f))
    W["bg_c"] = _col_layout(np.asarray(inputs["L0_attn_a_bg"], f))
    W["bmb_c"] = _col_layout(np.asarray(inputs["L0_attn_a_bmb"], f))
    W["mem"] = np.ascontiguousarray(np.asarray(inputs["L0_attn_a_memory"], f))
    W["normT"] = np.ascontiguousarray(np.asarray(inputs["L0_attn_a_norm"], f).T)

    rp = np.concatenate([np.arange(0, 1024, 2), np.arange(1, 1024, 2), np.arange(1024, 2048)])
    win = np.asarray(inputs["L0_attn_b_W_in"], f)
    perm6k = np.concatenate([rp, 2048 + rp, 4096 + rp])
    W["winT"] = np.ascontiguousarray(win[perm6k, :].T)
    W["bin_c"] = _col_layout(np.asarray(inputs["L0_attn_b_b_in"], f)[perm6k])
    W["woutT"] = np.ascontiguousarray(np.asarray(inputs["L0_attn_b_W_out"], f).T[rp, :])
    W["bout_c"] = _col_layout(np.asarray(inputs["L0_attn_b_b_out"], f))

    gc = np.arange(16)[:, None, None]
    j = np.arange(4)[None, :, None]
    p = np.arange(128)[None, None, :]
    permF = (4 * (gc * 128 + p) + j).reshape(-1)

    def slab_major(a, nk, nm):
        K2, M2 = a.shape
        ms = M2 // nm
        return np.ascontiguousarray(
            a.reshape(nk, 128, nm, ms).transpose(2, 1, 0, 3).reshape(nm * 128, nk * ms))

    W["wg_p"] = slab_major(np.asarray(inputs["L0_ffn_w_gate"], f)[:, permF], 16, 16)
    W["wu_p"] = slab_major(np.asarray(inputs["L0_ffn_w_up"], f)[:, permF], 16, 16)
    W["wd_p"] = slab_major(np.asarray(inputs["L0_ffn_w_down"], f)[permF, :], 64, 16)
    W["winT"] = slab_major(W["winT"], 16, 48)
    W["wq_sl"] = slab_major(W["wqT"], 16, 16)
    W["wgt_sl"] = slab_major(W["wgT"], 16, 16)
    W["wmb_sl"] = slab_major(W["wmbT"], 16, 16)
    W["wout_sl"] = slab_major(W["woutT"], 16, 16)

    rot = D // 2
    inv_freq = 1.0 / (10000.0 ** (np.arange(0, rot, 2, dtype=np.float64) / rot))
    freqs = np.arange(S, dtype=np.float64)[:, None] * inv_freq[None, :]
    W["costab"] = np.ascontiguousarray(np.cos(freqs).T.astype(f))
    W["sintab"] = np.ascontiguousarray(np.sin(freqs).T.astype(f))

    W["ident"] = np.eye(128, dtype=f)
    W["identb"] = np.eye(128, dtype=ml_dtypes.bfloat16)
    W["ones"] = np.ones((128, 128), dtype=f)
    oh = np.zeros((4, 512), dtype=f)
    for r in range(4):
        oh[r, r * 128:(r + 1) * 128] = 1.0
    W["onehot4"] = oh
    return W


def build_nc(taps=(), phases=ALL_PHASES):
    nc = bass.Bass("TRN2", target_bir_lowering=False)
    g = {}

    def din(name, shape, dtype=F32):
        g[name] = nc.dram_tensor(name, shape, dtype, kind="ExternalInput")

    din("x", [S, D])
    din("wkT", [D, 512]); din("wvT", [D, 512])
    din("bq_c", [128, 16]); din("bk_c", [128, 4]); din("bv_c", [128, 4])
    din("bg_c", [128, 16]); din("bmb_c", [128, 16])
    din("mem", [16, 768, 128]); din("normT", [768, 16])
    din("winT", [48 * 128, D]); din("bin_c", [128, 48])
    din("bout_c", [128, 16])
    din("wq_sl", [D, D]); din("wgt_sl", [D, D]); din("wmb_sl", [D, D])
    din("wout_sl", [D, D])
    din("wg_p", [D, 8192]); din("wu_p", [D, 8192]); din("wd_p", [D, 8192])
    din("costab", [512, S]); din("sintab", [512, S])
    din("ident", [128, 128]); din("identb", [128, 128], BF16)
    din("ones", [128, 128]); din("onehot4", [4, 512])

    out = nc.dram_tensor("out", [S, D], F32, kind="ExternalOutput")

    XN1 = nc.dram_tensor("XN1", [16, 128, S], BF16)
    XN2B = nc.dram_tensor("XN2B", [16, 128, S], BF16)
    QR = nc.dram_tensor("QR", [16, 128, S], BF16)
    XR1 = nc.dram_tensor("XR1", [16, 128, S], F32)
    XR2 = nc.dram_tensor("XR2", [16, 128, S], F32)
    XR3 = nc.dram_tensor("XR3", [16, 128, S], F32)
    ACC = nc.dram_tensor("ACC", [24, 128, 516], F32)
    ACC8 = nc.dram_tensor("ACC8", [8, 24, 128, 516], F32)
    ACC_R = nc.dram_tensor("ACC_R", [24, 128, 516], F32)
    CWG = nc.dram_tensor("CWG", [D, 8192], BF16)
    CWU = nc.dram_tensor("CWU", [D, 8192], BF16)
    CWD = nc.dram_tensor("CWD", [D, 8192], BF16)
    CWIN = nc.dram_tensor("CWIN", [48 * 128, D], BF16)
    CWQ = nc.dram_tensor("CWQ", [D, D], BF16)
    CWGT = nc.dram_tensor("CWGT", [D, D], BF16)
    CWMB = nc.dram_tensor("CWMB", [D, D], BF16)
    CWOUT = nc.dram_tensor("CWOUT", [D, D], BF16)
    scrmap = {"XN1": XN1, "XN2B": XN2B, "QR": QR, "XR1": XR1, "XR2": XR2,
              "XR3": XR3, "ACC": ACC, "ACC_R": ACC_R, "CWQ": CWQ}

    ctr = [0]
    dctr = [0]
    es = ExitStack()
    sem = es.enter_context(nc.semaphore("seq"))
    dsem = es.enter_context(nc.semaphore("dseq"))
    gsem = es.enter_context(nc.semaphore("gseq"))
    GSEM[0] = gsem
    GSEM[1] = [0]
    slots = [(es.enter_context(nc.semaphore("slot%d" % i)), [0]) for i in range(22)]

    def sput(eng, i, out, in_):
        h, cc = slots[i]
        eng.dma_start(out=out, in_=in_).then_inc(h, 16)
        cc[0] += 16
        return cc[0]

    def swt(eng, i, val):
        if val:
            eng.wait_ge(slots[i][0], val)

    def drain_all(eng):
        for h, cc in slots:
            if cc[0]:
                eng.wait_ge(h, cc[0])
        eng.wait_ge(dsem, dctr[0])
        if GSEM[1][0]:
            eng.wait_ge(gsem, GSEM[1][0])

    ident = es.enter_context(nc.sbuf_tensor("c_ident", [128, 128], F32))
    identb = es.enter_context(nc.sbuf_tensor("c_identb", [128, 128], BF16))
    ones = es.enter_context(nc.sbuf_tensor("c_ones", [128, 128], F32))
    oh4 = es.enter_context(nc.sbuf_tensor("c_oh4", [4, 512], F32))
    bias = es.enter_context(nc.sbuf_tensor("c_bias", [128, 96], F32))
    binc = es.enter_context(nc.sbuf_tensor("c_binc", [128, 48], F32))
    BQ, BK, BV, BG, BMB, BOUT, NBK, NBQ = 0, 16, 20, 24, 40, 56, 72, 76

    with nc.Block() as blk:
        st = Stepper(blk, sem, ctr, dsem, dctr)

        def ld_const(t, e):
            t.dma_start(out=ident[:], in_=g["ident"].ap(), inc=16)
            t.dma_start(out=identb[:], in_=g["identb"].ap(), inc=16)
            t.dma_start(out=ones[:], in_=g["ones"].ap(), inc=16)
            t.dma_start(out=oh4[:], in_=g["onehot4"].ap(), inc=16)
            t.dma_start(out=bias[:, BQ:BQ + 16], in_=g["bq_c"].ap(), inc=16)
            t.dma_start(out=bias[:, BK:BK + 4], in_=g["bk_c"].ap(), inc=16)
            t.dma_start(out=bias[:, BV:BV + 4], in_=g["bv_c"].ap(), inc=16)
            t.dma_start(out=bias[:, BG:BG + 16], in_=g["bg_c"].ap(), inc=16)
            t.dma_start(out=bias[:, BMB:BMB + 16], in_=g["bmb_c"].ap(), inc=16)
            t.dma_start(out=bias[:, BOUT:BOUT + 16], in_=g["bout_c"].ap(), inc=16)
            t.dma_start(out=binc[:], in_=g["bin_c"].ap(), inc=16)

        st.step(sync=ld_const)

        def negb(t, e):
            t.tensor_scalar_mul(bias[:, NBK:NBK + 4], bias[:, BK:BK + 4], -1.0)
            t.tensor_scalar_mul(bias[:, NBQ:NBQ + 16], bias[:, BQ:BQ + 16], -1.0)

        st.step(vector=negb)

    if "P0" in phases:
        with nc.Block() as blk, nc.sbuf_tensor("cbuf", [128, 2 * 8192], BF16) as cbuf:
            st = Stepper(blk, sem, ctr, dsem, dctr)
            jobs = [(g["wg_p"], CWG, D, 8192, 2), (g["wu_p"], CWU, D, 8192, 2),
                    (g["wd_p"], CWD, D, 8192, 2), (g["winT"], CWIN, 48 * 128, D, 8),
                    (g["wq_sl"], CWQ, D, D, 8), (g["wgt_sl"], CWGT, D, D, 8),
                    (g["wmb_sl"], CWMB, D, D, 8), (g["wout_sl"], CWOUT, D, D, 8)]
            for src, dst, rows_, cols, rblk in jobs:
                for r0 in range(0, rows_, 128 * rblk):
                    nb = rblk * cols

                    gv = [None]

                    def ld(t, e, src=src, r0=r0, cols=cols, rblk=rblk, nb=nb, gv=gv):
                        gv[0] = t.gdma(cbuf.ap()[:, 0:nb].rearrange("p (r c) -> p r c", r=rblk),
                                       src.ap()[r0:r0 + 128 * rblk, :]
                                       .rearrange("(r p) c -> p r c", p=128))

                    def sv(t, e, dst=dst, r0=r0, cols=cols, rblk=rblk, nb=nb, gv=gv):
                        t.gwait(gv[0])
                        t.dma_start(out=dst.ap()[r0:r0 + 128 * rblk, :]
                                    .rearrange("(r p) c -> p r c", p=128),
                                    in_=cbuf.ap()[:, 0:nb].rearrange("p (r c) -> p r c", r=rblk),
                                    inc=16)

                    st.step(gpsimd=ld)
                    st.step(sync=sv)

    ases = ExitStack()
    memsb = ases.enter_context(nc.sbuf_tensor("memsb", [128, 16 * 6 * 128], BF16))
    normsb = ases.enter_context(nc.sbuf_tensor("normsb", [128, 6 * 16], BF16))

    def mem_tile(h, fc):
        o = (h * 6 + fc) * 128
        return memsb[:, o:o + 128]

    with nc.Block() as blk:
        st = Stepper(blk, sem, ctr, dsem, dctr)

        gvm = [None]

        def ld_mem(t, e):
            t.gdma(memsb.ap().rearrange("p (h fc d) -> p h fc d", h=16, fc=6),
                   g["mem"].ap().rearrange("h (fc p) d -> p h fc d", p=128))
            gvm[0] = t.gdma(normsb.ap().rearrange("p (fc h) -> p fc h", fc=6),
                            g["normT"].ap().rearrange("(fc p) h -> p fc h", p=128))

        st.step(gpsimd=ld_mem)

        def ld_mem_drain(t, e):
            t.gwait(gvm[0])
            t.nop()

        st.step(gpsimd=ld_mem_drain)

    TC = 512
    if "A1" in phases:
        with ExitStack() as ph:
            x_tm = ph.enter_context(nc.sbuf_tensor("a1_xtm", [128, 4 * D], F32))
            scr2k = ph.enter_context(nc.sbuf_tensor("a1_scr2k", [128, 2 * D], F32))
            stats = ph.enter_context(nc.sbuf_tensor("a1_stats", [128, 16], F32))
            xn_tm = ph.enter_context(nc.sbuf_tensor("a1_xntm", [128, 4 * D], BF16))
            xn_fm = ph.enter_context(nc.sbuf_tensor("a1_xnfm", [128, 16 * TC], BF16))
            wk_sb = ph.enter_context(nc.sbuf_tensor("a1_wk", [128, 16 * 512], BF16))
            wv_sb = ph.enter_context(nc.sbuf_tensor("a1_wv", [128, 16 * 512], BF16))
            wmb_sl = ph.enter_context(nc.sbuf_tensor("a1_wmb", [128, 2 * 2048], BF16))
            k_fm = ph.enter_context(nc.sbuf_tensor("a1_kfm", [128, 4 * TC], F32))
            v_fm = ph.enter_context(nc.sbuf_tensor("a1_vfm", [128, 4 * TC], F32))
            mb_fm = xn_tm
            k_tm = ph.enter_context(nc.sbuf_tensor("a1_ktm", [128, 4 * 128], F32))
            xx = ph.enter_context(nc.sbuf_tensor("a1_xx", [128, 4 * 260], F32))
            u_tm = ph.enter_context(nc.sbuf_tensor("a1_utm", [128, 4 * 768], BF16))
            u_fm = ph.enter_context(nc.sbuf_tensor("a1_ufm", [128, 6 * TC], BF16))
            ss6 = ph.enter_context(nc.sbuf_tensor("a1_ss6", [128, 4 * 8], F32))
            num_sb = ph.enter_context(nc.sbuf_tensor("a1_num", [128, TC], F32))
            vninv = ph.enter_context(nc.sbuf_tensor("a1_vninv", [128, TC], F32))
            p1 = ph.enter_context(nc.sbuf_tensor("a1_p1", [128, TC], F32))
            wmv_fm = ph.enter_context(nc.sbuf_tensor("a1_wmvfm", [128, TC], BF16))
            rtm = ph.enter_context(nc.sbuf_tensor("a1_rtm", [128, 4 * 520], BF16))
            astg = ph.enter_context(nc.sbuf_tensor("a1_astg", [128, 2 * 516], F32))

            pT0 = ph.enter_context(nc.psum_tensor("a1_pT0", [128, 1024], BF16))
            pT1 = ph.enter_context(nc.psum_tensor("a1_pT1", [128, 1024], BF16))
            pP0 = ph.enter_context(nc.psum_tensor("a1_pP0", [128, 512], F32))
            pP1 = ph.enter_context(nc.psum_tensor("a1_pP1", [128, 512], F32))
            pN0 = ph.enter_context(nc.psum_tensor("a1_pN0", [128, 512], F32))
            pN1 = ph.enter_context(nc.psum_tensor("a1_pN1", [128, 512], F32))
            pSm = ph.enter_context(nc.psum_tensor("a1_pSm", [128, 512], F32))
            pBC = ph.enter_context(nc.psum_tensor("a1_pBC", [128, 512], F32))

            def R(r):
                return scr2k[0:1, r * 512:(r + 1) * 512]

            def R4(r):
                return scr2k[0:4, r * 512:(r + 1) * 512]

            vload = {}
            vspill = {}
            for c in range(S // TC):
                with nc.Block() as blk:
                    st = Stepper(blk, sem, ctr, dsem, dctr)

                    def ld_x(t, e, c=c):
                        vload["x"] = sput(e, 0, x_tm.ap().rearrange("p (tt d) -> p tt d", tt=4),
                                          g["x"].ap()[c * TC:(c + 1) * TC, :]
                                          .rearrange("(tt p) d -> p tt d", p=128))
                        vload["wmb0"] = sput(e, 1, wmb_sl.ap()[:, 0:2048], CWMB.ap()[0:128, :])

                    def ld_kv(t, e, c=c):
                        if c == 0:
                            t.gdma(wk_sb.ap().rearrange("p (kc m) -> p kc m", kc=16),
                                   g["wkT"].ap().rearrange("(kc p) m -> p kc m", p=128))
                            vload["kv"] = t.gdma(wv_sb.ap().rearrange("p (kc m) -> p kc m", kc=16),
                                                 g["wvT"].ap().rearrange("(kc p) m -> p kc m", p=128))

                    st.step(sync=ld_x, gpsimd=ld_kv)

                    def ln_sums(t, e):
                        swt(e, 0, vload["x"])
                        for tt in range(4):
                            t.reduce_sum(stats[:, tt:tt + 1], x_tm[:, tt * D:(tt + 1) * D], axis=AX.X)

                    def ln_sq2(t, e):
                        swt(e, 0, vload["x"])
                        for tt in range(4):
                            t.activation(xn_tm[:, tt * D:(tt + 1) * D],
                                         x_tm[:, tt * D:(tt + 1) * D], AF.Square,
                                         accum_out=stats[:, 4 + tt:5 + tt])

                    st.step(vector=ln_sums, scalar=ln_sq2)

                    def ln_mv(t, e):
                        t.tensor_scalar_mul(stats[:, 8:12], stats[:, 0:4], 1.0 / D)
                        t.tensor_scalar_mul(stats[:, 4:8], stats[:, 4:8], 1.0 / D)

                    st.step(vector=ln_mv)

                    def ln_m2(t, e):
                        t.tensor_mul(stats[:, 12:16], stats[:, 8:12], stats[:, 8:12])

                    st.step(vector=ln_m2)

                    def ln_var(t, e):
                        t.tensor_sub(stats[:, 4:8], stats[:, 4:8], stats[:, 12:16])

                    st.step(vector=ln_var)

                    def ln_eps(t, e):
                        t.tensor_scalar_add(stats[:, 4:8], stats[:, 4:8], LN_EPS)

                    st.step(vector=ln_eps)

                    def ln_rc(t, e):
                        t.reciprocal(stats[:, 12:16], stats[:, 4:8])

                    st.step(vector=ln_rc)

                    def ln_sq(t, e):
                        t.activation(stats[:, 12:16], stats[:, 12:16], AF.Sqrt)

                    st.step(scalar=ln_sq)

                    def ln_norm(t, e):
                        for tt in range(4):
                            t.tensor_scalar(xn_tm[:, tt * D:(tt + 1) * D], x_tm[:, tt * D:(tt + 1) * D],
                                            stats[:, 8 + tt:9 + tt], stats[:, 12 + tt:13 + tt],
                                            ALU.subtract, ALU.mult)

                    st.step(vector=ln_norm)

                    for grp in range(4):
                        def tr(t, e, grp=grp):
                            for k in range(4):
                                fc = grp * 4 + k
                                pt = (pT0, pT1)[k // 2]
                                for tt in range(4):
                                    mmi = e.transpose(
                                        pt[:, (k % 2) * 512 + tt * 128: (k % 2) * 512 + (tt + 1) * 128],
                                        xn_tm[:, tt * D + fc * 128: tt * D + (fc + 1) * 128],
                                        identb[:])
                            mmi.then_inc(sem, 1)
                            ctr[0] += 1

                        def ev(t, e, grp=grp):
                            swt(e, 3, vspill.get("xn"))
                            t.tensor_copy(xn_fm[:, (grp * 4) * TC:(grp * 4 + 2) * TC], pT0[:])
                            t.tensor_copy(xn_fm[:, (grp * 4 + 2) * TC:(grp * 4 + 4) * TC], pT1[:])

                        st.step(tensor=tr)
                        st.step(vector=ev)

                    def sp_xn(t, e, c=c):
                        vspill["xn"] = sput(e, 3, XN1.ap()[:, :, c * TC:(c + 1) * TC]
                                            .rearrange("fc p t -> p fc t"),
                                            xn_fm.ap().rearrange("p (fc t) -> p fc t", fc=16))

                    st.step(sync=sp_xn)

                    for name, wsb, nco, dst, bcol in (("k", wk_sb, 4, k_fm, BK),
                                                      ("v", wv_sb, 4, v_fm, BV)):
                        for oc in range(nco):
                            pp = (pP0, pP1)[oc % 2]

                            def mm(t, e, wsb=wsb, oc=oc, pp=pp):
                                t.gwait(vload.get("kv"))
                                for kc in range(16):
                                    mmi = e.matmul(
                                        pp[:], wsb[:, kc * 512 + oc * 128: kc * 512 + (oc + 1) * 128],
                                        xn_fm[:, kc * TC:(kc + 1) * TC],
                                        start=(kc == 0), stop=(kc == 15))
                                mmi.then_inc(sem, 1)
                                ctr[0] += 1

                            def ev(t, e, oc=oc, pp=pp, dst=dst, bcol=bcol):
                                t.activation(dst[:, oc * TC:(oc + 1) * TC], pp[:], AF.Identity,
                                             bias=bias[:, bcol + oc:bcol + oc + 1])

                            st.step(tensor=mm)
                            st.step(scalar=ev)

                    for oc in range(16):
                        sl = wmb_sl[:, (oc % 2) * 2048:(oc % 2 + 1) * 2048]
                        pp = (pP0, pP1)[oc % 2]

                        def mm(t, e, sl=sl, pp=pp, oc=oc):
                            swt(e, 1 + (oc % 2), vload["wmb" + str(oc)] if oc else vload["wmb0"])
                            for kc in range(16):
                                mmi = e.matmul(pp[:], sl[:, kc * 128:(kc + 1) * 128],
                                                      xn_fm[:, kc * TC:(kc + 1) * TC],
                                                      start=(kc == 0), stop=(kc == 15))
                            mmi.then_inc(sem, 1)
                            ctr[0] += 1

                        def pf(t, e, oc=oc):
                            if oc + 1 < 16:
                                vload["wmb" + str(oc + 1)] = sput(
                                    e, 1 + ((oc + 1) % 2),
                                    wmb_sl.ap()[:, ((oc + 1) % 2) * 2048:((oc + 1) % 2 + 1) * 2048],
                                    CWMB.ap()[(oc + 1) * 128:(oc + 2) * 128, :])

                        def ev(t, e, oc=oc, pp=pp):
                            t.activation(mb_fm[:, oc * TC:(oc + 1) * TC], pp[:], AF.Sigmoid,
                                         bias=bias[:, BMB + oc:BMB + oc + 1])

                        st.step(tensor=mm, sync=pf)
                        st.step(scalar=ev)

                    for gi in range(4):
                        def ktr(t, e, gi=gi):
                            for tt in range(4):
                                mmi = e.transpose(
                                    pSm[:, tt * 128:(tt + 1) * 128],
                                    k_fm[:, gi * TC + tt * 128: gi * TC + (tt + 1) * 128],
                                    ident[:])
                            mmi.then_inc(sem, 1)
                            ctr[0] += 1

                        def kev(t, e):
                            t.tensor_copy(k_tm[:], pSm[:])

                        st.step(tensor=ktr)
                        st.step(vector=kev)

                        def relu12(t, e):
                            t.activation(xx.ap().rearrange("p (tt cc) -> p tt cc", tt=4)[:, :, 3:131],
                                         k_tm.ap().rearrange("p (tt cc) -> p tt cc", tt=4), AF.Relu)
                            t.activation(xx.ap().rearrange("p (tt cc) -> p tt cc", tt=4)[:, :, 131:259],
                                         k_tm.ap().rearrange("p (tt cc) -> p tt cc", tt=4), AF.Relu,
                                         scale=-1.0)

                        st.step(scalar=relu12)

                        def wrap(t, e):
                            t.tensor_copy(xx.ap().rearrange("p (tt cc) -> p tt cc", tt=4)[:, :, 0:3],
                                          xx.ap().rearrange("p (tt cc) -> p tt cc", tt=4)[:, :, 256:259])

                        st.step(vector=wrap)

                        def dp(t, e):
                            for tt in range(4):
                                xo = tt * 260
                                for jj, j in enumerate((1, 2, 3)):
                                    for cc in range(2):
                                        fo = tt * 768 + (jj * 2 + cc) * 128
                                        t.tensor_mul(
                                            u_tm[:, fo:fo + 128],
                                            xx[:, xo + 3 + cc * 128: xo + 3 + (cc + 1) * 128],
                                            xx[:, xo + 3 + cc * 128 - j: xo + 3 + (cc + 1) * 128 - j])

                        st.step(vector=dp)

                        def ssq(t, e):
                            for tt in range(4):
                                t.activation(u_fm[:, tt * 768:(tt + 1) * 768],
                                             u_tm[:, tt * 768:(tt + 1) * 768],
                                             AF.Square, accum_out=ss6[:, tt * 8 + 6:tt * 8 + 7])

                        st.step(scalar=ssq)

                        def sstr(t, e):
                            for tt in range(4):
                                mmi = e.transpose(pSm[0:1, tt * 128:(tt + 1) * 128],
                                                         ss6[:, tt * 8 + 6:tt * 8 + 7], ident[:])
                            mmi.then_inc(sem, 1)
                            ctr[0] += 1

                        def ssev(t, e):
                            t.tensor_copy(R(0), pSm[0:1, 0:512])

                        st.step(tensor=sstr)
                        st.step(vector=ssev)

                        def nrow(t, e):
                            t.activation(R(1), R(0), AF.Sqrt)

                        st.step(scalar=nrow)

                        def ninv1(t, e):
                            t.tensor_scalar_max(R(1), R(1), 1e-12)

                        st.step(vector=ninv1)

                        def ninv2(t, e):
                            t.reciprocal(R(2), R(1))

                        st.step(vector=ninv2)

                        for grp in range(2):
                            nfc = 4 if grp == 0 else 2

                            def utr(t, e, grp=grp, nfc=nfc):
                                for k in range(nfc):
                                    fc = grp * 4 + k
                                    pt = (pT0, pT1)[k // 2]
                                    for tt in range(4):
                                        mmi = e.transpose(
                                            pt[:, (k % 2) * 512 + tt * 128: (k % 2) * 512 + (tt + 1) * 128],
                                            u_tm[:, tt * 768 + fc * 128: tt * 768 + (fc + 1) * 128],
                                            identb[:])
                                mmi.then_inc(sem, 1)
                                ctr[0] += 1

                            def uev(t, e, grp=grp, nfc=nfc):
                                t.tensor_copy(u_fm[:, (grp * 4) * TC:(grp * 4 + 2) * TC], pT0[:])
                                if nfc == 4:
                                    t.tensor_copy(u_fm[:, (grp * 4 + 2) * TC:(grp * 4 + 4) * TC], pT1[:])

                            st.step(tensor=utr)
                            st.step(vector=uev)

                        def dnm(t, e, gi=gi):
                            for fc in range(6):
                                mmi = e.matmul(
                                    pSm[0:4, :], normsb[:, fc * 16 + gi * 4: fc * 16 + gi * 4 + 4],
                                    u_fm[:, fc * TC:(fc + 1) * TC],
                                    start=(fc == 0), stop=(fc == 5))
                            mmi.then_inc(sem, 1)
                            ctr[0] += 1

                        def dnmev(t, e):
                            t.tensor_copy(R4(4), pSm[0:4, :])

                        st.step(tensor=dnm)
                        st.step(vector=dnmev)

                        def bcn(t, e):
                            t.matmul(pBC[0:4, :], ones[0:1, 0:4], R(2), start=True, stop=True)

                        st.step(tensor=bcn)

                        def den1(t, e):
                            t.tensor_mul(R4(5), R4(4), pBC[0:4, :])

                        st.step(vector=den1)

                        def den2(t, e):
                            t.tensor_scalar_add(R4(5), R4(5), EPS_A)

                        st.step(vector=den2)

                        def den3(t, e):
                            t.reciprocal(R4(6), R4(5))

                        st.step(vector=den3)

                        def aa1(t, e):
                            t.tensor_mul(R4(6), R4(6), pBC[0:4, :])

                        st.step(vector=aa1)

                        def aa2(t, e):
                            t.tensor_mul(R4(6), R4(6), pBC[0:4, :])

                        st.step(vector=aa2)

                        def cf2(t, e):
                            t.tensor_scalar(R4(7), R4(5), 1.0, -1.0, ALU.subtract, ALU.mult)

                        st.step(vector=cf2)

                        def cf3(t, e):
                            t.tensor_scalar(R4(7), R4(7), 0.0, 1.0, ALU.max, ALU.min)

                        st.step(vector=cf3)

                        def bcn2(t, e):
                            t.matmul(pBC[0:4, :], ones[0:1, 0:4], R(2), start=True, stop=True)

                        st.step(tensor=bcn2)

                        def cp1(t, e):
                            t.tensor_mul(R4(7), R4(7), pBC[0:4, :])

                        st.step(vector=cp1)

                        def cptr(t, e):
                            for tt in range(4):
                                mmi = e.transpose(pSm[:, 448 + tt * 4:448 + (tt + 1) * 4],
                                                  R4(7)[:, tt * 128:(tt + 1) * 128], ident[0:4, 0:4])
                            mmi.then_inc(sem, 1)
                            ctr[0] += 1

                        def cpev(t, e):
                            for tt in range(4):
                                t.tensor_copy(rtm[:, tt * 520 + 512: tt * 520 + 516],
                                              pSm[:, 448 + tt * 4:448 + (tt + 1) * 4])

                        st.step(tensor=cptr)
                        st.step(vector=cpev)

                        def bcv(t, e):
                            t.matmul(pBC[:], ones[0:1, :], R(2), start=True, stop=True)

                        st.step(tensor=bcv)

                        def vn(t, e, gi=gi):
                            t.tensor_mul(vninv[:], v_fm[:, gi * TC:(gi + 1) * TC], pBC[:])

                        st.step(vector=vn)

                        for r in range(4):
                            h = gi * 4 + r
                            pn = (pN0, pN1)[r % 2]

                            def nmm(t, e, h=h, pn=pn):
                                for fc in range(6):
                                    mmi = e.matmul(pn[:], mem_tile(h, fc),
                                                          u_fm[:, fc * TC:(fc + 1) * TC],
                                                          start=(fc == 0), stop=(fc == 5))
                                mmi.then_inc(sem, 1)
                                ctr[0] += 1

                            def nev(t, e, r=r, pn=pn):
                                t.copy(num_sb[:], pn[:])

                            st.step(tensor=nmm)
                            st.step(scalar=nev)

                            def bca(t, e, r=r):
                                t.matmul(pBC[:], oh4[:, r * 128:(r + 1) * 128], R4(6),
                                         start=True, stop=True)

                            st.step(tensor=bca)

                            def pp1(t, e, r=r):
                                t.tensor_mul(p1[:], num_sb[:], pBC[:])

                            st.step(vector=pp1)

                            def pp3(t, e, r=r):
                                t.tensor_sub(p1[:], vninv[:], p1[:])

                            st.step(vector=pp3)

                            def wm(t, e, h=h, r=r):
                                t.tensor_mul(wmv_fm[:], p1[:], mb_fm[:, h * TC:(h + 1) * TC])

                            st.step(vector=wm)

                            pt = (pT0, pT1)[r % 2]

                            def wtr(t, e, r=r, pt=pt):
                                for tt in range(4):
                                    mmi = e.transpose(
                                        pt[:, tt * 128:(tt + 1) * 128],
                                        wmv_fm[:, tt * 128:(tt + 1) * 128],
                                        identb[:])
                                mmi.then_inc(sem, 1)
                                ctr[0] += 1

                            def wev(t, e, r=r, pt=pt):
                                for tt in range(4):
                                    t.tensor_copy(rtm[:, tt * 520 + r * 128: tt * 520 + (r + 1) * 128],
                                                  pt[:, tt * 128:(tt + 1) * 128])

                            st.step(tensor=wtr)
                            st.step(vector=wev)

                        for fc in range(6):
                            def dmm(t, e, fc=fc):
                                for tt in range(4):
                                    lt = u_tm[:, tt * 768 + fc * 128: tt * 768 + (fc + 1) * 128]
                                    mmi = e.matmul(pP0[:], lt, rtm[:, tt * 520: tt * 520 + 512],
                                                          start=(tt == 0), stop=(tt == 3))
                                for tt in range(4):
                                    lt = u_tm[:, tt * 768 + fc * 128: tt * 768 + (fc + 1) * 128]
                                    mmi = e.matmul(pP1[:, 0:4], lt,
                                                          rtm[:, tt * 520 + 512: tt * 520 + 516],
                                                          start=(tt == 0), stop=(tt == 3))
                                mmi.then_inc(sem, 1)
                                ctr[0] += 1

                            i6 = gi * 6 + fc

                            def dev(t, e, i6=i6):
                                o = (i6 % 2) * 516
                                swt(e, 4 + (i6 % 2), vspill.get("dlast" + str(i6 % 2)))
                                t.tensor_copy(astg[:, o:o + 512], pP0[:])
                                t.tensor_copy(astg[:, o + 512:o + 516], pP1[:, 0:4])

                            def dwr(t, e, i6=i6, c=c):
                                o = (i6 % 2) * 516
                                vspill["dlast" + str(i6 % 2)] = sput(e, 4 + (i6 % 2),
                                                                     ACC8.ap()[c, i6, :, :],
                                                                     astg[:, o:o + 516])

                            st.step(tensor=dmm)
                            st.step(vector=dev)
                            st.step(sync=dwr)

    if "MID" in phases:
        with ExitStack() as ph:
            s0 = ph.enter_context(nc.sbuf_tensor("mid_s0", [128, 24 * 516], F32))
            s1 = ph.enter_context(nc.sbuf_tensor("mid_s1", [128, 24 * 516], F32))
            with nc.Block() as blk:
                st = Stepper(blk, sem, ctr, dsem, dctr)

                def l0(t, e):
                    drain_all(e)
                    t.dma_start(out=s0.ap().rearrange("p (i cc2) -> p i cc2", i=24),
                                in_=ACC8.ap()[0].rearrange("i p cc2 -> p i cc2"), inc=16)

                st.step(sync=l0)
                for c in range(1, 8):
                    def lc(t, e, c=c):
                        t.dma_start(out=s1.ap().rearrange("p (i cc2) -> p i cc2", i=24),
                                    in_=ACC8.ap()[c].rearrange("i p cc2 -> p i cc2"), inc=16)

                    st.step(scalar=lc)

                    def ac(t, e):
                        t.tensor_add(s0[:], s0[:], s1[:])

                    st.step(vector=ac)

                def sv(t, e):
                    t.dma_start(out=ACC.ap().rearrange("i p cc2 -> p i cc2"),
                                in_=s0.ap().rearrange("p (i cc2) -> p i cc2", i=24), inc=16)

                st.step(sync=sv)

        with nc.Block() as blk:
            st = Stepper(blk, sem, ctr, dsem, dctr)

            def cc(t, e):
                t.collective_compute(
                    "AllReduce", ALU.add,
                    replica_groups=[list(range(NCORE))],
                    ins=[ACC.ap().opt()], outs=[ACC_R.ap().opt()])

            st.step(gpsimd=cc)

        with ExitStack() as ph:
            accsb = ph.enter_context(nc.sbuf_tensor("mid_acc", [128, 24 * 516], F32))
            memf = ph.enter_context(nc.sbuf_tensor("mid_memf", [128, 16 * 6 * 128], F32))
            normf = ph.enter_context(nc.sbuf_tensor("mid_normf", [128, 96], F32))
            with nc.Block() as blk:
                st = Stepper(blk, sem, ctr, dsem, dctr)

                def ld(t, e):
                    t.dma_start(out=accsb.ap().rearrange("p (i cc) -> p i cc", i=24),
                                in_=ACC_R.ap().rearrange("i p cc -> p i cc"), inc=16)
                    t.dma_start(out=memf.ap().rearrange("p (h fc d) -> p h fc d", h=16, fc=6),
                                in_=g["mem"].ap().rearrange("h (fc p) d -> p h fc d", p=128), inc=16)
                    t.dma_start(out=normf.ap().rearrange("p (fc h) -> p fc h", fc=6),
                                in_=g["normT"].ap().rearrange("(fc p) h -> p fc h", p=128), inc=16)

                st.step(sync=ld)

                def addm(t, e):
                    for gi in range(4):
                        for fc in range(6):
                            i = gi * 6 + fc
                            memv = memf.ap().rearrange("p (h f2 d) -> p h f2 d", h=16, f2=6)[:, 4 * gi:4 * gi + 4, fc, :]
                            outv = memsb.ap().rearrange("p (h f2 d) -> p h f2 d", h=16, f2=6)[:, 4 * gi:4 * gi + 4, fc, :]
                            accv = accsb.ap()[:, i * 516:i * 516 + 512].rearrange("p (r d) -> p r d", r=4)
                            t.tensor_add(outv, memv, accv)
                    for gi in range(4):
                        accn = accsb.ap().rearrange("p (i cc) -> p i cc", i=24)[:, 6 * gi:6 * gi + 6, 512:516]
                        normv = normf.ap().rearrange("p (fc h) -> p fc h", fc=6)[:, :, 4 * gi:4 * gi + 4]
                        outn = normsb.ap().rearrange("p (fc h) -> p fc h", fc=6)[:, :, 4 * gi:4 * gi + 4]
                        t.tensor_add(outn, normv, accn)

                st.step(vector=addm)

    if "A2" in phases:
        with ExitStack() as ph:
            x_tm = ph.enter_context(nc.sbuf_tensor("a2_xtm", [128, 4 * D], F32))
            x_fm = ph.enter_context(nc.sbuf_tensor("a2_xfm", [128, 16 * TC], F32))
            xn1 = ph.enter_context(nc.sbuf_tensor("a2_xn1", [128, 16 * TC], BF16))
            scr2k = ph.enter_context(nc.sbuf_tensor("a2_scr", [128, 2 * D], F32))
            wq_sl = ph.enter_context(nc.sbuf_tensor("a2_wq", [128, 2 * 2048], BF16))
            wg_sl = ph.enter_context(nc.sbuf_tensor("a2_wg", [128, 2 * 2048], BF16))
            q_fm = ph.enter_context(nc.sbuf_tensor("a2_qfm", [128, TC], F32))
            q_tm = ph.enter_context(nc.sbuf_tensor("a2_qtm", [128, 4 * 128], F32))
            xx = ph.enter_context(nc.sbuf_tensor("a2_xx", [128, 4 * 260], F32))
            u_tm = ph.enter_context(nc.sbuf_tensor("a2_utm", [128, 4 * 768], BF16))
            u_fm = ph.enter_context(nc.sbuf_tensor("a2_ufm", [128, 6 * TC], BF16))
            ss6 = ph.enter_context(nc.sbuf_tensor("a2_ss6", [128, 4 * 8], F32))
            num_sb = ph.enter_context(nc.sbuf_tensor("a2_num", [128, TC], F32))
            gate_all = ph.enter_context(nc.sbuf_tensor("a2_gate", [128, 16 * TC], BF16))
            o_sb = ph.enter_context(nc.sbuf_tensor("a2_o", [128, TC], F32))

            pF0 = ph.enter_context(nc.psum_tensor("a2_pF0", [128, 512], F32))
            pF1 = ph.enter_context(nc.psum_tensor("a2_pF1", [128, 512], F32))
            pB0 = ph.enter_context(nc.psum_tensor("a2_pB0", [128, 1024], BF16))
            pB1 = ph.enter_context(nc.psum_tensor("a2_pB1", [128, 1024], BF16))
            pN0 = ph.enter_context(nc.psum_tensor("a2_pN0", [128, 512], F32))
            pN1 = ph.enter_context(nc.psum_tensor("a2_pN1", [128, 512], F32))
            pSm = ph.enter_context(nc.psum_tensor("a2_pSm", [128, 512], F32))
            pBC = ph.enter_context(nc.psum_tensor("a2_pBC", [128, 512], F32))

            def R(r):
                return scr2k[0:1, r * 512:(r + 1) * 512]

            va2 = {}
            for c in range(S // TC):
                with nc.Block() as blk:
                    st = Stepper(blk, sem, ctr, dsem, dctr)

                    def ld_x(t, e, c=c):
                        if c == 0:
                            drain_all(e)
                        va2["x"] = sput(e, 0, x_tm.ap().rearrange("p (tt d) -> p tt d", tt=4),
                                        g["x"].ap()[c * TC:(c + 1) * TC, :]
                                        .rearrange("(tt p) d -> p tt d", p=128))
                        va2["xn1"] = sput(e, 6, xn1.ap().rearrange("p (fc t2) -> p fc t2", fc=16),
                                          XN1.ap()[:, :, c * TC:(c + 1) * TC]
                                          .rearrange("fc p t2 -> p fc t2"))
                        va2["wg0"] = sput(e, 7, wg_sl.ap()[:, 0:2048], CWGT.ap()[0:128, :])

                    st.step(sync=ld_x)

                    for fc2 in range(8):
                        def tr(t, e, fc2=fc2):
                            swt(e, 0, va2["x"])
                            for half in range(2):
                                fc = fc2 * 2 + half
                                pt = (pF0, pF1)[half]
                                for tt in range(4):
                                    mmi = e.transpose(
                                        pt[:, tt * 128:(tt + 1) * 128],
                                        x_tm[:, tt * D + fc * 128: tt * D + (fc + 1) * 128],
                                        ident[:])
                                mmi.then_inc(sem, 1)
                                ctr[0] += 1

                        def ev(t, e, fc2=fc2):
                            for half in range(2):
                                fc = fc2 * 2 + half
                                pt = (pF0, pF1)[half]
                                t.tensor_copy(x_fm[:, fc * TC:(fc + 1) * TC], pt[:])

                        st.step(tensor=tr)
                        st.step(vector=ev)

                    for h2 in range(16):
                        gsl = wg_sl[:, (h2 % 2) * 2048:(h2 % 2 + 1) * 2048]
                        pg = (pN0, pN1)[h2 % 2]

                        def gmm(t, e, gsl=gsl, pg=pg, h2=h2):
                            swt(e, 7 + (h2 % 2), va2["wg" + str(h2)])
                            if h2 == 0:
                                swt(e, 6, va2["xn1"])
                            for kc in range(16):
                                mmi = e.matmul(pg[:], gsl[:, kc * 128:(kc + 1) * 128],
                                               xn1[:, kc * TC:(kc + 1) * TC],
                                               start=(kc == 0), stop=(kc == 15))
                            mmi.then_inc(sem, 1)
                            ctr[0] += 1

                        def gpf(t, e, h2=h2):
                            if h2 + 1 < 16:
                                va2["wg" + str(h2 + 1)] = sput(
                                    e, 7 + ((h2 + 1) % 2),
                                    wg_sl.ap()[:, ((h2 + 1) % 2) * 2048:((h2 + 1) % 2 + 1) * 2048],
                                    CWGT.ap()[(h2 + 1) * 128:(h2 + 2) * 128, :])
                            if h2 == 15:
                                va2["wq0"] = sput(e, 9, wq_sl.ap()[:, 0:2048], CWQ.ap()[0:128, :])

                        st.step(tensor=gmm, sync=gpf)

                        def gev(t, e, h2=h2, pg=pg):
                            t.activation(gate_all[:, h2 * TC:(h2 + 1) * TC], pg[:], AF.Sigmoid,
                                         bias=bias[:, BG + h2:BG + h2 + 1])

                        st.step(scalar=gev)

                    for h in range(16):
                        qsl = wq_sl[:, (h % 2) * 2048:(h % 2 + 1) * 2048]

                        def qmm(t, e, qsl=qsl, h=h):
                            swt(e, 9 + (h % 2), va2["wq" + str(h)])
                            for kc in range(16):
                                mmi = e.matmul(pN0[:], qsl[:, kc * 128:(kc + 1) * 128],
                                               xn1[:, kc * TC:(kc + 1) * TC],
                                               start=(kc == 0), stop=(kc == 15))
                            mmi.then_inc(sem, 1)
                            ctr[0] += 1

                        def pf(t, e, h=h):
                            if h + 1 < 16:
                                va2["wq" + str(h + 1)] = sput(
                                    e, 9 + ((h + 1) % 2),
                                    wq_sl.ap()[:, ((h + 1) % 2) * 2048:((h + 1) % 2 + 1) * 2048],
                                    CWQ.ap()[(h + 1) * 128:(h + 2) * 128, :])

                        st.step(tensor=qmm, sync=pf)

                        def qev(t, e, h=h):
                            t.activation(q_fm[:], pN0[:], AF.Identity,
                                         bias=bias[:, BQ + h:BQ + h + 1])

                        st.step(scalar=qev)

                        def qtr(t, e):
                            for tt in range(4):
                                mmi = e.transpose(pSm[:, tt * 128:(tt + 1) * 128],
                                                  q_fm[:, tt * 128:(tt + 1) * 128], ident[:])
                            mmi.then_inc(sem, 1)
                            ctr[0] += 1

                        def qevt(t, e):
                            t.tensor_copy(q_tm[:], pSm[:])

                        st.step(tensor=qtr)
                        st.step(vector=qevt)

                        def relu12(t, e):
                            t.activation(xx.ap().rearrange("p (tt cc) -> p tt cc", tt=4)[:, :, 3:131],
                                         q_tm.ap().rearrange("p (tt cc) -> p tt cc", tt=4), AF.Relu)
                            t.activation(xx.ap().rearrange("p (tt cc) -> p tt cc", tt=4)[:, :, 131:259],
                                         q_tm.ap().rearrange("p (tt cc) -> p tt cc", tt=4), AF.Relu,
                                         scale=-1.0)

                        st.step(scalar=relu12)

                        def wrap(t, e):
                            t.tensor_copy(xx.ap().rearrange("p (tt cc) -> p tt cc", tt=4)[:, :, 0:3],
                                          xx.ap().rearrange("p (tt cc) -> p tt cc", tt=4)[:, :, 256:259])

                        st.step(vector=wrap)

                        def dp(t, e):
                            for tt in range(4):
                                xo = tt * 260
                                for jj, j in enumerate((1, 2, 3)):
                                    for cc in range(2):
                                        fo = tt * 768 + (jj * 2 + cc) * 128
                                        t.tensor_mul(
                                            u_tm[:, fo:fo + 128],
                                            xx[:, xo + 3 + cc * 128: xo + 3 + (cc + 1) * 128],
                                            xx[:, xo + 3 + cc * 128 - j: xo + 3 + (cc + 1) * 128 - j])

                        st.step(vector=dp)

                        def ssq(t, e):
                            for tt in range(4):
                                t.activation(u_fm[:, tt * 768:(tt + 1) * 768],
                                             u_tm[:, tt * 768:(tt + 1) * 768],
                                             AF.Square, accum_out=ss6[:, tt * 8 + 6:tt * 8 + 7])

                        st.step(scalar=ssq)

                        def sstr(t, e):
                            for tt in range(4):
                                mmi = e.transpose(pSm[0:1, tt * 128:(tt + 1) * 128],
                                                  ss6[:, tt * 8 + 6:tt * 8 + 7], ident[:])
                            mmi.then_inc(sem, 1)
                            ctr[0] += 1

                        def ssev(t, e):
                            t.tensor_copy(R(0), pSm[0:1, 0:512])

                        st.step(tensor=sstr)
                        st.step(vector=ssev)

                        def nrow(t, e):
                            t.activation(R(1), R(0), AF.Sqrt)

                        st.step(scalar=nrow)

                        def ninv1(t, e):
                            t.tensor_scalar_max(R(1), R(1), 1e-12)

                        st.step(vector=ninv1)

                        def ninv2(t, e):
                            t.reciprocal(R(2), R(1))

                        st.step(vector=ninv2)

                        for grp in range(2):
                            nfc = 4 if grp == 0 else 2

                            def utr(t, e, grp=grp, nfc=nfc):
                                for k in range(nfc):
                                    fc = grp * 4 + k
                                    pt = (pB0, pB1)[k // 2]
                                    for tt in range(4):
                                        mmi = e.transpose(
                                            pt[:, (k % 2) * 512 + tt * 128: (k % 2) * 512 + (tt + 1) * 128],
                                            u_tm[:, tt * 768 + fc * 128: tt * 768 + (fc + 1) * 128],
                                            identb[:])
                                mmi.then_inc(sem, 1)
                                ctr[0] += 1

                            def uev(t, e, grp=grp, nfc=nfc):
                                t.tensor_copy(u_fm[:, (grp * 4) * TC:(grp * 4 + 2) * TC], pB0[:])
                                if nfc == 4:
                                    t.tensor_copy(u_fm[:, (grp * 4 + 2) * TC:(grp * 4 + 4) * TC], pB1[:])

                            st.step(tensor=utr)
                            st.step(vector=uev)

                        def nmm(t, e, h=h):
                            for fc in range(6):
                                mmi = e.matmul(pN0[:], mem_tile(h, fc),
                                               u_fm[:, fc * TC:(fc + 1) * TC],
                                               start=(fc == 0), stop=(fc == 5))
                            for fc in range(6):
                                mmi = e.matmul(pSm[0:1, :], normsb[:, fc * 16 + h: fc * 16 + h + 1],
                                               u_fm[:, fc * TC:(fc + 1) * TC],
                                               start=(fc == 0), stop=(fc == 5))
                            mmi.then_inc(sem, 1)
                            ctr[0] += 1

                        st.step(tensor=nmm)

                        def nev(t, e):
                            t.copy(num_sb[:], pN0[:])

                        st.step(scalar=nev)

                        def den1(t, e):
                            t.tensor_mul(R(3), pSm[0:1, :], R(2))

                        st.step(vector=den1)

                        def den2(t, e):
                            t.tensor_scalar_add(R(3), R(3), EPS_A)

                        st.step(vector=den2)

                        def den3(t, e):
                            t.reciprocal(R(3), R(3))

                        st.step(vector=den3)

                        def den4(t, e):
                            t.tensor_mul(R(3), R(3), R(2))

                        st.step(vector=den4)

                        def bcb(t, e):
                            t.matmul(pBC[:], ones[0:1, :], R(3), start=True, stop=True)

                        st.step(tensor=bcb)

                        def mh1(t, e):
                            t.tensor_mul(num_sb[:], num_sb[:], pBC[:])

                        st.step(vector=mh1)

                        def mh2(t, e, h=h):
                            swt(e, 11, va2.get("o" + str((h - 1) % 16)))
                            t.tensor_mul(o_sb[:], gate_all[:, h * TC:(h + 1) * TC], num_sb[:])

                        st.step(vector=mh2)

                        def mh3(t, e, h=h):
                            t.tensor_add(o_sb[:], o_sb[:], x_fm[:, h * TC:(h + 1) * TC])

                        st.step(vector=mh3)

                        def mh4(t, e, h=h):
                            t.tensor_add(o_sb[:], o_sb[:], xn1[:, h * TC:(h + 1) * TC])

                        st.step(vector=mh4)

                        def owr(t, e, h=h, c=c):
                            va2["o" + str(h)] = sput(e, 11, XR1.ap()[h, :, c * TC:(c + 1) * TC], o_sb[:])

                        st.step(sync=owr)

    ases.close()

    TCF = 512

    def w1buf2(w, i):
        return w[:, (i % 2) * 8192:(i % 2 + 1) * 8192]

    evMM = es.enter_context(nc.semaphore("F_evMM"))
    evMM2 = es.enter_context(nc.semaphore("F_evMM2"))
    evSilu = es.enter_context(nc.semaphore("F_evSilu"))
    evAbs = es.enter_context(nc.semaphore("F_evAbs"))
    evHm = es.enter_context(nc.semaphore("F_evHm"))
    evHS = es.enter_context(nc.semaphore("F_evHS"))
    evDev = es.enter_context(nc.semaphore("F_evDev"))
    evOtr = es.enter_context(nc.semaphore("F_evOtr"))
    evOev = es.enter_context(nc.semaphore("F_evOev"))
    svF = es.enter_context(nc.semaphore("F_svF"))
    fsem = es.enter_context(nc.semaphore("F_tail"))
    ftc = [0]
    svF_base = [0]
    vps = [0, 0]
    vps2 = [0, 0]
    f_ebase = [0]
    f_obase = [0]

    def phase_F(src, dst, pref):
        with ExitStack() as ph:
            xt0 = ph.enter_context(nc.sbuf_tensor(pref + "_xt0", [128, 512], F32))
            xt1 = ph.enter_context(nc.sbuf_tensor(pref + "_xt1", [128, 512], F32))
            x2t = ph.enter_context(nc.sbuf_tensor(pref + "_x2", [128, 512], F32))
            rowst = ph.enter_context(nc.sbuf_tensor(pref + "_rows", [1, 4 * 512], F32))
            xn = ph.enter_context(nc.sbuf_tensor(pref + "_xn", [128, 16 * TCF], BF16))
            wg1 = ph.enter_context(nc.sbuf_tensor(pref + "_wg1", [128, 2 * 16 * 512], BF16))
            wu1 = ph.enter_context(nc.sbuf_tensor(pref + "_wu1", [128, 2 * 16 * 512], BF16))
            wd1 = wg1
            sg = ph.enter_context(nc.sbuf_tensor(pref + "_sg", [128, 4 * 512], F32))
            hh = ph.enter_context(nc.sbuf_tensor(pref + "_hh", [128, 4 * 512], F32))
            gx = ph.enter_context(nc.sbuf_tensor(pref + "_gx", [128, 10 * 512], BF16))
            ab = gx[:, 6 * 512:10 * 512]
            hs = ph.enter_context(nc.sbuf_tensor(pref + "_hs", [128, 64 * TCF], BF16))
            osub = sg[:, 0:1024]
            ostg = (ph.enter_context(nc.sbuf_tensor(pref + "_ostg", [128, 2 * 512], F32))
                    if dst is None else None)

            pA = ph.enter_context(nc.psum_tensor(pref + "_pA", [128, 512], F32))
            pB = ph.enter_context(nc.psum_tensor(pref + "_pB", [128, 512], F32))
            pC = ph.enter_context(nc.psum_tensor(pref + "_pC", [128, 512], F32))
            pD = ph.enter_context(nc.psum_tensor(pref + "_pD", [128, 512], F32))
            pE = ph.enter_context(nc.psum_tensor(pref + "_pE", [128, 512], F32))
            pF_ = ph.enter_context(nc.psum_tensor(pref + "_pF", [128, 512], F32))
            pG = ph.enter_context(nc.psum_tensor(pref + "_pG", [128, 512], F32))
            pH = ph.enter_context(nc.psum_tensor(pref + "_pH", [128, 512], F32))
            pGU = [pA, pB, pC, pD, pE, pF_, pG, pH]

            def RW(r):
                return rowst[0:1, r * 512:(r + 1) * 512]


            v_fwr = [None]
            for c in range(S // TCF):
                with nc.Block() as blk:
                    st = Stepper(blk, sem, ctr, dsem, dctr)
                    vx = {}
                    vw = {}
                    vd = {}
                    vo = {}
                    vst = {}

                    def ldx0(t, e, c=c):
                        if c == 0:
                            drain_all(e)
                        vx[0] = sput(e, 12, xt0[:], src.ap()[0, :, c * TCF:(c + 1) * TCF])
                        sput(e, 14, w1buf2(wg1, 0), CWG.ap()[0:128, :])
                        vw[0] = sput(e, 14, w1buf2(wu1, 0), CWU.ap()[0:128, :])

                    st.step(sync=ldx0)

                    for fc in range(16):
                        xt = (xt0, xt1)[fc % 2]

                        def mms(t, e, fc=fc, xt=xt):
                            swt(e, 12 + (fc % 2), vx[fc])
                            t.matmul(pA[0:1, :], ones[:, 0:1], xt[:],
                                     start=(fc == 0), stop=(fc == 15))
                            if fc > 0:
                                t.matmul(pC[0:1, :], ones[:, 0:1], (x2t, osub)[(fc - 1) % 2][:, 0:512],
                                         start=(fc == 1), stop=False)

                        def sq(t, e, fc=fc, xt=xt):
                            swt(e, 12 + (fc % 2), vx[fc])
                            if fc < 2:
                                swt(e, 20, vps[0])
                                swt(e, 21, vps[1])
                            t.activation((x2t, osub)[fc % 2][:, 0:512], xt[:], AF.Square)

                        def pfx(t, e, fc=fc, c=c):
                            if fc + 1 < 16:
                                vx[fc + 1] = sput(e, 12 + ((fc + 1) % 2), (xt0, xt1)[(fc + 1) % 2][:],
                                                  src.ap()[fc + 1, :, c * TCF:(c + 1) * TCF])

                        st.step(tensor=mms, scalar=sq, sync=pfx)

                    def mmq_last(t, e):
                        t.matmul(pC[0:1, :], ones[:, 0:1], osub[:, 0:512],
                                 start=False, stop=True)

                    st.step(tensor=mmq_last)

                    def evs(t, e):
                        t.tensor_scalar_mul(RW(0), pA[0:1, :], 1.0 / D)
                        t.tensor_scalar_mul(RW(1), pC[0:1, :], 1.0 / D)

                    st.step(vector=evs)

                    def var1(t, e):
                        t.tensor_mul(RW(2), RW(0), RW(0))

                    st.step(vector=var1)

                    def var2(t, e):
                        t.tensor_sub(RW(1), RW(1), RW(2))

                    st.step(vector=var2)

                    def var3(t, e):
                        t.tensor_scalar_add(RW(1), RW(1), LN_EPS)

                    st.step(vector=var3)

                    def var4(t, e):
                        t.reciprocal(RW(2), RW(1))

                    st.step(vector=var4)

                    def var5(t, e):
                        t.activation(RW(2), RW(2), AF.Sqrt)

                    st.step(scalar=var5)

                    def var6(t, e):
                        t.tensor_mul(RW(3), RW(0), RW(2))

                    st.step(vector=var6)

                    def bcs(t, e):
                        t.matmul(pA[:], ones[0:1, :], RW(2), start=True, stop=True)
                        mmi = e.matmul(pC[:], ones[0:1, :], RW(3), start=True, stop=True)
                        mmi.then_inc(sem, 1)
                        ctr[0] += 1

                    st.step(tensor=bcs)

                    def ldn0(t, e, c=c):
                        vx[100] = sput(e, 12, xt0[:], src.ap()[0, :, c * TCF:(c + 1) * TCF])

                    st.step(sync=ldn0)

                    for fc in range(16):
                        xt = (xt0, xt1)[fc % 2]

                        def nm1(t, e, fc=fc, xt=xt):
                            swt(e, 12 + (fc % 2), vx[100 + fc])
                            t.tensor_mul((x2t, osub)[fc % 2][:, 0:512], xt[:], pA[:])

                        def pfn(t, e, fc=fc, c=c):
                            if fc + 1 < 16:
                                vx[101 + fc] = sput(e, 12 + ((fc + 1) % 2), (xt0, xt1)[(fc + 1) % 2][:],
                                                    src.ap()[fc + 1, :, c * TCF:(c + 1) * TCF])

                        st.step(vector=nm1, sync=pfn)

                        def nm2(t, e, fc=fc):
                            t.tensor_sub(xn[:, fc * TCF:(fc + 1) * TCF],
                                         (x2t, osub)[fc % 2][:, 0:512], pC[:])

                        st.step(vector=nm2)

                vout = {}
                if not ASYNC_F:
                  with nc.Block() as blk:
                    st = Stepper(blk, sem, ctr, dsem, dctr)
                    hv = {}
                    devv = {}
                    otrv = {}
                    oevv = {}

                    def tinc(inst):
                        inst.then_inc(fsem, 1)
                        ftc[0] += 1
                        return ftc[0]

                    def em_silu(e, s1):
                        p = s1 % 2
                        e.wait_ge(fsem, ftc[0])
                        for j in range(2):
                            jj = 2 * p + j
                            inst = e.activation(sg[:, jj * 512:(jj + 1) * 512],
                                                pGU[4 * p + j][:], AF.Silu)
                        return tinc(inst)

                    def em_hmul(e, s1):
                        p = s1 % 2
                        e.wait_ge(fsem, ftc[0])
                        for j in range(2):
                            jj = 2 * (s1 % 2) + j
                            inst = e.tensor_mul(hh[:, jj * 512:(jj + 1) * 512],
                                                sg[:, jj * 512:(jj + 1) * 512],
                                                pGU[4 * p + 2 + j][:])
                        hv[s1] = tinc(inst)
                        return hv[s1]

                    def em_habs(e):
                        e.wait_ge(fsem, ftc[0])
                        for jj in range(4):
                            inst = e.activation(gx[:, (6 + jj) * 512:(7 + jj) * 512],
                                                hh[:, jj * 512:(jj + 1) * 512], AF.Square)
                        return tinc(inst)

                    def em_tail24(e, gcx):
                        e.wait_ge(fsem, ftc[0])
                        pairs = [(0, 1), (0, 2), (0, 3), (1, 2), (1, 3), (2, 3)]
                        for i2, (a, b) in enumerate(pairs):
                            inst = e.tensor_tensor(gx[:, i2 * 512:(i2 + 1) * 512],
                                                   gx[:, (6 + a) * 512:(7 + a) * 512],
                                                   gx[:, (6 + b) * 512:(7 + b) * 512], ALU.is_ge)
                        tinc(inst)
                        e.wait_ge(fsem, ftc[0])
                        inst = e.tensor_add(gx[:, 6 * 512:7 * 512], gx[:, 0:512], gx[:, 512:1024])
                        inst = e.tensor_add(gx[:, 7 * 512:8 * 512], gx[:, 3 * 512:4 * 512], gx[:, 4 * 512:5 * 512])
                        inst = e.tensor_add(gx[:, 8 * 512:9 * 512], gx[:, 512:1024], gx[:, 3 * 512:4 * 512])
                        inst = e.tensor_add(gx[:, 9 * 512:10 * 512], gx[:, 2 * 512:3 * 512], gx[:, 4 * 512:5 * 512])
                        tinc(inst)
                        e.wait_ge(fsem, ftc[0])
                        inst = e.tensor_add(gx[:, 6 * 512:7 * 512], gx[:, 6 * 512:7 * 512], gx[:, 2 * 512:3 * 512])
                        inst = e.tensor_sub(gx[:, 7 * 512:8 * 512], gx[:, 0:512], gx[:, 7 * 512:8 * 512])
                        inst = e.tensor_sub(gx[:, 8 * 512:9 * 512], gx[:, 8 * 512:9 * 512], gx[:, 5 * 512:6 * 512])
                        inst = e.tensor_add(gx[:, 9 * 512:10 * 512], gx[:, 9 * 512:10 * 512], gx[:, 5 * 512:6 * 512])
                        tinc(inst)
                        e.wait_ge(fsem, ftc[0])
                        inst = e.tensor_single_scalar(gx[:, 6 * 512:7 * 512], gx[:, 6 * 512:7 * 512], 1.5, ALU.is_ge)
                        inst = e.tensor_single_scalar(gx[:, 7 * 512:8 * 512], gx[:, 7 * 512:8 * 512], -0.5, ALU.is_le)
                        inst = e.tensor_single_scalar(gx[:, 8 * 512:9 * 512], gx[:, 8 * 512:9 * 512], 0.5, ALU.is_le)
                        inst = e.tensor_single_scalar(gx[:, 9 * 512:10 * 512], gx[:, 9 * 512:10 * 512], 1.5, ALU.is_le)
                        tinc(inst)
                        e.wait_ge(fsem, ftc[0])
                        for jj in range(4):
                            inst = e.tensor_mul(hs[:, (gcx * 4 + jj) * TCF:(gcx * 4 + jj + 1) * TCF],
                                                hh[:, jj * 512:(jj + 1) * 512],
                                                gx[:, (6 + jj) * 512:(7 + jj) * 512])
                        return tinc(inst)

                    for s in range(32):
                        gc, h = s // 2, s % 2

                        def gumm(t, e, s=s, gc=gc, h=h):
                            if h == 0:
                                swt(e, 14 + (gc % 2), vw[gc])
                            if s >= 2:
                                e.wait_ge(fsem, hv[s - 2])
                            wgb = w1buf2(wg1, gc)
                            wub = w1buf2(wu1, gc)
                            p = s % 2
                            for j in range(2):
                                jj = 2 * h + j
                                for kc in range(16):
                                    e.matmul(pGU[4 * p + j][:],
                                             wgb[:, kc * 512 + jj * 128: kc * 512 + (jj + 1) * 128],
                                             xn[:, kc * TCF:(kc + 1) * TCF],
                                             start=(kc == 0), stop=(kc == 15))
                            for j in range(2):
                                jj = 2 * h + j
                                for kc in range(16):
                                    mmi = e.matmul(pGU[4 * p + 2 + j][:],
                                                   wub[:, kc * 512 + jj * 128: kc * 512 + (jj + 1) * 128],
                                                   xn[:, kc * TCF:(kc + 1) * TCF],
                                                   start=(kc == 0), stop=(kc == 15))
                            mmi.then_inc(sem, 1)
                            ctr[0] += 1

                        def pfw(t, e, gc=gc, h=h):
                            if h == 0 and gc + 1 < 16:
                                sput(e, 14 + ((gc + 1) % 2), w1buf2(wg1, gc + 1),
                                     CWG.ap()[(gc + 1) * 128:(gc + 2) * 128, :])
                                vw[gc + 1] = sput(e, 14 + ((gc + 1) % 2), w1buf2(wu1, gc + 1),
                                                  CWU.ap()[(gc + 1) * 128:(gc + 2) * 128, :])

                        def tact(t, e, s=s):
                            if s >= 1:
                                em_silu(e, s - 1)
                            if s >= 3 and s % 2 == 1:
                                em_habs(e)

                        def tdve(t, e, s=s):
                            if s >= 3 and s % 2 == 1:
                                em_tail24(e, (s - 3) // 2)
                            if s >= 1:
                                em_hmul(e, s - 1)

                        st.step(scalar=tact, vector=tdve, tensor=gumm, sync=pfw)

                    def tfa1(t, e):
                        em_silu(e, 31)

                    def tfv1(t, e):
                        em_hmul(e, 31)

                    def ldwd0(t, e, c=c):
                        vd[0] = sput(e, 16, wd1[:, 0:8192], CWD.ap()[0:128, :])
                        if dst is not None:
                            swt(e, 20, vps[0])
                            swt(e, 21, vps[1])
                        vo[0] = sput(e, 18, osub[:, 0:512], src.ap()[0, :, c * TCF:(c + 1) * TCF])

                    st.step(scalar=tfa1, vector=tfv1, sync=ldwd0)

                    def tfa2(t, e):
                        em_habs(e)

                    def tfv2(t, e):
                        em_tail24(e, 15)

                    st.step(scalar=tfa2, vector=tfv2)
                    tail_done = ftc[0]

                    for oc in range(16):
                        def dve_fn(t, e, oc=oc):
                            if oc >= 1:
                                swt(e, 18 + ((oc - 1) % 2), vo[oc - 1])
                                e.wait_ge(fsem, ftc[0])
                                ob = osub[:, ((oc - 1) % 2) * 512:((oc - 1) % 2 + 1) * 512]
                                inst = e.tensor_add(ob, ob, (pE, pF_)[(oc - 1) % 2][:])
                                devv[oc - 1] = tinc(inst)
                            if dst is None and oc >= 2:
                                swt(e, 20 + ((oc - 2) % 2), vout.get(oc - 4, vps2[(oc - 2) % 2]))
                                e.wait_ge(fsem, otrv[oc - 2])
                                o = ((oc - 2) % 2) * 512
                                for tt in range(4):
                                    pp = (pG, pH)[tt // 2]
                                    inst = e.tensor_copy(ostg[:, o + tt * 128: o + (tt + 1) * 128],
                                                         pp[:, (tt % 2) * 128:(tt % 2 + 1) * 128])
                                oevv[oc - 2] = tinc(inst)

                        def dmm(t, e, oc=oc):
                            swt(e, 16 + (oc % 2), vd[oc])
                            if oc < 2:
                                e.wait_ge(fsem, tail_done)
                            else:
                                e.wait_ge(fsem, devv[oc - 2])
                            wdb = wd1[:, (oc % 2) * 8192:(oc % 2 + 1) * 8192]
                            pp = (pE, pF_)[oc % 2]
                            for kc in range(64):
                                mmi = e.matmul(pp[:], wdb[:, kc * 128:(kc + 1) * 128],
                                               hs[:, kc * TCF:(kc + 1) * TCF],
                                               start=(kc == 0), stop=(kc == 63))
                            mmi.then_inc(sem, 1)
                            ctr[0] += 1
                            if dst is None and oc >= 1:
                                e.wait_ge(fsem, devv[oc - 1])
                                if oc >= 2:
                                    e.wait_ge(fsem, oevv[oc - 2])
                                for tt in range(4):
                                    pp2 = (pG, pH)[tt // 2]
                                    inst = e.transpose(pp2[:, (tt % 2) * 128:(tt % 2 + 1) * 128],
                                                       osub[:, ((oc - 1) % 2) * 512 + tt * 128:
                                                            ((oc - 1) % 2) * 512 + (tt + 1) * 128],
                                                       ident[:])
                                otrv[oc - 1] = tinc(inst)

                        def sync_fn(t, e, oc=oc, c=c):
                            if oc + 1 < 16:
                                vd[oc + 1] = sput(e, 16 + ((oc + 1) % 2),
                                                  wd1[:, ((oc + 1) % 2) * 8192:((oc + 1) % 2 + 1) * 8192],
                                                  CWD.ap()[(oc + 1) * 128:(oc + 2) * 128, :])
                            if dst is not None:
                                if oc >= 1:
                                    e.wait_ge(fsem, devv[oc - 1])
                                    vst[oc - 1] = sput(e, 20 + ((oc - 1) % 2),
                                                       dst.ap()[oc - 1, :, c * TCF:(c + 1) * TCF],
                                                       osub[:, ((oc - 1) % 2) * 512:((oc - 1) % 2 + 1) * 512])
                                if oc + 1 < 16:
                                    swt(e, 20 + ((oc + 1) % 2), vst.get(oc - 1, vps[(oc + 1) % 2]))
                                    vo[oc + 1] = sput(e, 18 + ((oc + 1) % 2),
                                                      osub[:, ((oc + 1) % 2) * 512:((oc + 1) % 2 + 1) * 512],
                                                      src.ap()[oc + 1, :, c * TCF:(c + 1) * TCF])
                            else:
                                if oc + 1 < 16:
                                    if oc >= 1:
                                        e.wait_ge(fsem, otrv[oc - 1])
                                    vo[oc + 1] = sput(e, 18 + ((oc + 1) % 2),
                                                      osub[:, ((oc + 1) % 2) * 512:((oc + 1) % 2 + 1) * 512],
                                                      src.ap()[oc + 1, :, c * TCF:(c + 1) * TCF])
                                if oc >= 2:
                                    e.wait_ge(fsem, oevv[oc - 2])
                                    o = ((oc - 2) % 2) * 512
                                    for tt in range(4):
                                        vout[oc - 2] = sput(e, 20 + ((oc - 2) % 2),
                                                            out.ap()[c * TCF + tt * 128: c * TCF + (tt + 1) * 128,
                                                                     (oc - 2) * 128:(oc - 1) * 128],
                                                            ostg[:, o + tt * 128: o + (tt + 1) * 128])

                        st.step(vector=dve_fn, tensor=dmm, sync=sync_fn)

                    if dst is not None:
                        def dve_t(t, e):
                            swt(e, 18 + (15 % 2), vo[15])
                            e.wait_ge(fsem, ftc[0])
                            ob = osub[:, 512:1024]
                            inst = e.tensor_add(ob, ob, pF_[:])
                            devv[15] = tinc(inst)

                        def sync_t(t, e, c=c):
                            e.wait_ge(fsem, devv[15])
                            vst[15] = sput(e, 20 + 1,
                                           dst.ap()[15, :, c * TCF:(c + 1) * TCF],
                                           osub[:, 512:1024])

                        st.step(vector=dve_t, sync=sync_t)
                        vps[0] = vst[14]
                        vps[1] = vst[15]
                    else:
                        def dve_t1(t, e):
                            swt(e, 18 + 1, vo[15])
                            e.wait_ge(fsem, ftc[0])
                            ob = osub[:, 512:1024]
                            inst = e.tensor_add(ob, ob, pF_[:])
                            devv[15] = tinc(inst)
                            swt(e, 20 + 0, vout.get(12, vps2[0]))
                            e.wait_ge(fsem, otrv[14])
                            for tt in range(4):
                                inst = e.tensor_copy(ostg[:, tt * 128:(tt + 1) * 128],
                                                     (pG, pH)[tt // 2][:, (tt % 2) * 128:(tt % 2 + 1) * 128])
                            oevv[14] = tinc(inst)

                        def pe_t1(t, e):
                            e.wait_ge(fsem, devv[15])
                            e.wait_ge(fsem, oevv[14])
                            for tt in range(4):
                                inst = e.transpose((pG, pH)[tt // 2][:, (tt % 2) * 128:(tt % 2 + 1) * 128],
                                                   osub[:, 512 + tt * 128:512 + (tt + 1) * 128],
                                                   ident[:])
                            otrv[15] = tinc(inst)

                        def sync_t1(t, e, c=c):
                            e.wait_ge(fsem, oevv[14])
                            for tt in range(4):
                                vout[14] = sput(e, 20 + 0,
                                                out.ap()[c * TCF + tt * 128: c * TCF + (tt + 1) * 128,
                                                         14 * 128:15 * 128],
                                                ostg[:, tt * 128:(tt + 1) * 128])

                        st.step(vector=dve_t1, tensor=pe_t1, sync=sync_t1)

                        def dve_t2(t, e):
                            swt(e, 20 + 1, vout.get(13, vps2[1]))
                            e.wait_ge(fsem, otrv[15])
                            for tt in range(4):
                                inst = e.tensor_copy(ostg[:, 512 + tt * 128:512 + (tt + 1) * 128],
                                                     (pG, pH)[tt // 2][:, (tt % 2) * 128:(tt % 2 + 1) * 128])
                            oevv[15] = tinc(inst)

                        def sync_t2(t, e, c=c):
                            e.wait_ge(fsem, oevv[15])
                            for tt in range(4):
                                vout[15] = sput(e, 20 + 1,
                                                out.ap()[c * TCF + tt * 128: c * TCF + (tt + 1) * 128,
                                                         15 * 128:16 * 128],
                                                ostg[:, 512 + tt * 128:512 + (tt + 1) * 128])

                        st.step(vector=dve_t2, sync=sync_t2)
                        vps2[0] = vout[14]
                        vps2[1] = vout[15]

                if ASYNC_F:
                  with nc.Block() as blk:
                    base_mm = 16 * (f_ebase[0] + c)
                    base_hm = base_mm
                    base_hs = base_mm
                    base_mm2 = base_mm
                    base_dev = base_mm
                    base_otr = f_obase[0]
                    if dst is None:
                        f_obase[0] += 16

                    @blk.sync
                    def _(eng, c=c):
                        for gc in range(1, 16):
                            eng.wait_ge(evMM, base_mm + gc)
                            sput(eng, 14 + (gc % 2), w1buf2(wg1, gc),
                                 CWG.ap()[gc * 128:(gc + 1) * 128, :])
                            vw[gc] = sput(eng, 14 + (gc % 2), w1buf2(wu1, gc),
                                          CWU.ap()[gc * 128:(gc + 1) * 128, :])
                        eng.wait_ge(evMM, base_mm + 16)
                        eng.wait_ge(evHm, base_hm + 16)
                        vd[0] = sput(eng, 16, wd1[:, 0:8192], CWD.ap()[0:128, :])
                        if dst is not None:
                            swt(eng, 20, vps[0])
                            swt(eng, 21, vps[1])
                        else:
                            eng.wait_ge(evOtr, base_otr)
                        vo[0] = sput(eng, 18, osub[:, 0:512],
                                     src.ap()[0, :, c * TCF:(c + 1) * TCF])
                        for oc in range(16):
                            if oc + 1 < 16:
                                vd[oc + 1] = sput(eng, 16 + ((oc + 1) % 2),
                                                  wd1[:, ((oc + 1) % 2) * 8192:((oc + 1) % 2 + 1) * 8192],
                                                  CWD.ap()[(oc + 1) * 128:(oc + 2) * 128, :])
                                if dst is not None:
                                    swt(eng, 20 + ((oc + 1) % 2), vst.get(oc - 1, vps[(oc + 1) % 2]))
                                else:
                                    eng.wait_ge(evOtr, base_otr + oc)
                                vo[oc + 1] = sput(eng, 18 + ((oc + 1) % 2),
                                                  osub[:, ((oc + 1) % 2) * 512:((oc + 1) % 2 + 1) * 512],
                                                  src.ap()[oc + 1, :, c * TCF:(c + 1) * TCF])
                            if dst is not None:
                                eng.wait_ge(evDev, base_dev + oc + 1)
                                vst[oc] = sput(eng, 20 + (oc % 2),
                                               dst.ap()[oc, :, c * TCF:(c + 1) * TCF],
                                               osub[:, (oc % 2) * 512:(oc % 2 + 1) * 512])
                            else:
                                if oc > 0 and not DBG["no_out"]:
                                    if not DBG["no_oev_wait"]:
                                        eng.wait_ge(evOev, base_otr + oc)
                                    o = ((oc - 1) % 2) * 512
                                    for tt in range(4):
                                        vout[oc - 1] = sput(
                                            eng, 20 + ((oc - 1) % 2),
                                            out.ap()[c * TCF + tt * 128: c * TCF + (tt + 1) * 128,
                                                     (oc - 1) * 128: oc * 128],
                                            ostg[:, o + tt * 128: o + (tt + 1) * 128])
                        if dst is None:
                            if not DBG["no_out"]:
                                if not DBG["no_oev_wait"]:
                                    eng.wait_ge(evOev, base_otr + 16)
                                o = (15 % 2) * 512
                                for tt in range(4):
                                    vout[15] = sput(
                                        eng, 21,
                                        out.ap()[c * TCF + tt * 128: c * TCF + (tt + 1) * 128,
                                                 15 * 128: 16 * 128],
                                        ostg[:, o + tt * 128: o + (tt + 1) * 128])
                                vps2[0] = vout[14]
                                vps2[1] = vout[15]
                        else:
                            vps[0] = vst[14]
                            vps[1] = vst[15]

                    @blk.tensor
                    def _(eng, c=c):
                        for gc in range(16):
                            swt(eng, 14 + (gc % 2), vw[gc])
                            if gc > 0:
                                eng.wait_ge(evHm, base_hm + gc)
                            wgb = w1buf2(wg1, gc)
                            wub = w1buf2(wu1, gc)
                            for jj in range(4):
                                for kc in range(16):
                                    eng.matmul(pGU[jj][:],
                                               wgb[:, kc * 512 + jj * 128: kc * 512 + (jj + 1) * 128],
                                               xn[:, kc * TCF:(kc + 1) * TCF],
                                               start=(kc == 0), stop=(kc == 15))
                            for jj in range(4):
                                for kc in range(16):
                                    mmi = eng.matmul(pGU[4 + jj][:],
                                                     wub[:, kc * 512 + jj * 128: kc * 512 + (jj + 1) * 128],
                                                     xn[:, kc * TCF:(kc + 1) * TCF],
                                                     start=(kc == 0), stop=(kc == 15))
                            mmi.then_inc(evMM, 1)
                        eng.wait_ge(evHS, base_hs + 16)
                        for oc in range(16):
                            swt(eng, 16 + (oc % 2), vd[oc])
                            if oc >= 2:
                                eng.wait_ge(evDev, base_dev + oc - 1)
                            wdb = wd1[:, (oc % 2) * 8192:(oc % 2 + 1) * 8192]
                            pp = (pE, pF_)[oc % 2]
                            for kc in range(64):
                                mmi = eng.matmul(pp[:], wdb[:, kc * 128:(kc + 1) * 128],
                                                 hs[:, kc * TCF:(kc + 1) * TCF],
                                                 start=(kc == 0), stop=(kc == 63))
                            mmi.then_inc(evMM2, 1)
                            if dst is None and oc > 0:
                                eng.wait_ge(evDev, base_dev + oc)
                                eng.wait_ge(evOev, base_otr + oc - 1)
                                for tt in range(4):
                                    ppx = (pG, pH)[tt // 2]
                                    mmi = eng.transpose(
                                        ppx[:, (tt % 2) * 128:(tt % 2 + 1) * 128],
                                        osub[:, ((oc - 1) % 2) * 512 + tt * 128:
                                             ((oc - 1) % 2) * 512 + (tt + 1) * 128], ident[:])
                                mmi.then_inc(evOtr, 1)
                        if dst is None:
                            eng.wait_ge(evDev, base_dev + 16)
                            eng.wait_ge(evOev, base_otr + 15)
                            for tt in range(4):
                                ppx = (pG, pH)[tt // 2]
                                mmi = eng.transpose(
                                    ppx[:, (tt % 2) * 128:(tt % 2 + 1) * 128],
                                    osub[:, (15 % 2) * 512 + tt * 128:
                                         (15 % 2) * 512 + (tt + 1) * 128], ident[:])
                            mmi.then_inc(evOtr, 1)

                    @blk.scalar
                    def _(eng, c=c):
                        for gc in range(16):
                            eng.wait_ge(evMM, base_mm + gc + 1)
                            if gc > 0:
                                eng.wait_ge(evHm, base_hm + gc)
                            for jj in range(4):
                                ins = eng.activation(sg[:, jj * 512:(jj + 1) * 512], pGU[jj][:], AF.Silu)
                            ins.then_inc(evSilu, 1)
                            eng.wait_ge(evHm, base_hm + gc + 1)
                            if gc > 0:
                                eng.wait_ge(evHS, base_hs + gc)
                            for jj in range(4):
                                ins = eng.activation(gx[:, (6 + jj) * 512:(7 + jj) * 512],
                                                     hh[:, jj * 512:(jj + 1) * 512], AF.Square)
                            ins.then_inc(evAbs, 1)

                    @blk.vector
                    def _(eng, c=c):
                        svn = [0]

                        def edge(ins):
                            ins.then_inc(svF, 1)
                            svn[0] += 1
                            eng.wait_ge(svF, svn[0] + svF_base[0])

                        for gc in range(16):
                            eng.wait_ge(evSilu, base_mm + gc + 1)
                            if gc > 0:
                                eng.wait_ge(evHS, base_hs + gc)
                            for jj in range(4):
                                ins = eng.tensor_mul(hh[:, jj * 512:(jj + 1) * 512],
                                                     sg[:, jj * 512:(jj + 1) * 512], pGU[4 + jj][:])
                            ins.then_inc(evHm, 1)
                            eng.wait_ge(evAbs, base_mm + gc + 1)
                            pairs = [(0, 1), (0, 2), (0, 3), (1, 2), (1, 3), (2, 3)]
                            for i, (a, b) in enumerate(pairs):
                                ins = eng.tensor_tensor(gx[:, i * 512:(i + 1) * 512],
                                                        gx[:, (6 + a) * 512:(7 + a) * 512],
                                                        gx[:, (6 + b) * 512:(7 + b) * 512], ALU.is_ge)
                            edge(ins)
                            eng.tensor_add(gx[:, 6 * 512:7 * 512], gx[:, 0:512], gx[:, 512:1024])
                            eng.tensor_add(gx[:, 7 * 512:8 * 512], gx[:, 3 * 512:4 * 512], gx[:, 4 * 512:5 * 512])
                            eng.tensor_add(gx[:, 8 * 512:9 * 512], gx[:, 512:1024], gx[:, 3 * 512:4 * 512])
                            ins = eng.tensor_add(gx[:, 9 * 512:10 * 512], gx[:, 2 * 512:3 * 512], gx[:, 4 * 512:5 * 512])
                            edge(ins)
                            eng.tensor_add(gx[:, 6 * 512:7 * 512], gx[:, 6 * 512:7 * 512], gx[:, 2 * 512:3 * 512])
                            eng.tensor_sub(gx[:, 7 * 512:8 * 512], gx[:, 0:512], gx[:, 7 * 512:8 * 512])
                            eng.tensor_sub(gx[:, 8 * 512:9 * 512], gx[:, 8 * 512:9 * 512], gx[:, 5 * 512:6 * 512])
                            ins = eng.tensor_add(gx[:, 9 * 512:10 * 512], gx[:, 9 * 512:10 * 512], gx[:, 5 * 512:6 * 512])
                            edge(ins)
                            eng.tensor_single_scalar(gx[:, 6 * 512:7 * 512], gx[:, 6 * 512:7 * 512], 1.5, ALU.is_ge)
                            eng.tensor_single_scalar(gx[:, 7 * 512:8 * 512], gx[:, 7 * 512:8 * 512], -0.5, ALU.is_le)
                            eng.tensor_single_scalar(gx[:, 8 * 512:9 * 512], gx[:, 8 * 512:9 * 512], 0.5, ALU.is_le)
                            ins = eng.tensor_single_scalar(gx[:, 9 * 512:10 * 512], gx[:, 9 * 512:10 * 512], 1.5, ALU.is_le)
                            edge(ins)
                            for jj in range(4):
                                ins = eng.tensor_mul(hs[:, (gc * 4 + jj) * TCF:(gc * 4 + jj + 1) * TCF],
                                                     hh[:, jj * 512:(jj + 1) * 512],
                                                     gx[:, (6 + jj) * 512:(7 + jj) * 512])
                            ins.then_inc(evHS, 1)
                        svF_base[0] += svn[0]
                        for oc in range(16):
                            eng.wait_ge(evMM2, base_mm2 + oc + 1)
                            swt(eng, 18 + (oc % 2), vo[oc])
                            ob = osub[:, (oc % 2) * 512:(oc % 2 + 1) * 512]
                            ins = eng.tensor_add(ob, ob, (pE, pF_)[oc % 2][:])
                            ins.then_inc(evDev, 1)
                            if dst is None:
                                if oc > 0:
                                    eng.wait_ge(evOtr, base_otr + oc)
                                    if not DBG["no_out"]:
                                        swt(eng, 20 + ((oc - 1) % 2), vout.get(oc - 3, vps2[(oc - 1) % 2]))
                                    o = ((oc - 1) % 2) * 512
                                    for tt in range(4):
                                        ppx = (pG, pH)[tt // 2]
                                        ins = eng.tensor_copy(ostg[:, o + tt * 128: o + (tt + 1) * 128],
                                                              ppx[:, (tt % 2) * 128:(tt % 2 + 1) * 128])
                                    ins.then_inc(evOev, 1)
                        if dst is None:
                            eng.wait_ge(evOtr, base_otr + 16)
                            if not DBG["no_out"]:
                                swt(eng, 21, vout.get(13, vps2[1]))
                            o = (15 % 2) * 512
                            for tt in range(4):
                                ppx = (pG, pH)[tt // 2]
                                ins = eng.tensor_copy(ostg[:, o + tt * 128: o + (tt + 1) * 128],
                                                      ppx[:, (tt % 2) * 128:(tt % 2 + 1) * 128])
                            ins.then_inc(evOev, 1)

    if "F1" in phases:
        phase_F(XR1, XR2, "f1")
        f_ebase[0] += 8

    if "FX" in phases:
        phase_F(XR2, XR3, "fx")
        f_ebase[0] += 8

    if "FZ" in phases:
        phase_F(XR1, None, "fz")
        f_ebase[0] += 8

    TCB = 512
    asb = es.enter_context(nc.sbuf_tensor("b_a", [128, 18], F32))
    if "B1" in phases:
        with ExitStack() as ph:
            xt0 = ph.enter_context(nc.sbuf_tensor("b1_xt0", [128, 512], F32))
            xt1 = ph.enter_context(nc.sbuf_tensor("b1_xt1", [128, 512], F32))
            x2t = ph.enter_context(nc.sbuf_tensor("b1_x2", [128, 512], F32))
            rowst = ph.enter_context(nc.sbuf_tensor("b1_rows", [1, 4 * 512], F32))
            xn = ph.enter_context(nc.sbuf_tensor("b1_xn", [128, 16 * TCB], BF16))
            wsl = ph.enter_context(nc.sbuf_tensor("b1_wsl", [128, 4 * 2048], BF16))
            qro = ph.enter_context(nc.sbuf_tensor("b1_qro", [128, 8 * TCB], F32))
            kro = ph.enter_context(nc.sbuf_tensor("b1_kro", [128, 8 * TCB], F32))
            kps = ph.enter_context(nc.sbuf_tensor("b1_kps", [128, 8 * TCB], F32))
            vb = ph.enter_context(nc.sbuf_tensor("b1_v", [128, 16 * TCB], BF16))
            cst = ph.enter_context(nc.sbuf_tensor("b1_cs", [128, 8 * TCB], F32))
            qstg = ph.enter_context(nc.sbuf_tensor("b1_qstg", [128, 2 * TCB], BF16))
            ksq = ph.enter_context(nc.sbuf_tensor("b1_ksq", [128, 4 * 512], F32))
            fkv = ph.enter_context(nc.sbuf_tensor("b1_fkv", [128, 512], F32))

            pA = ph.enter_context(nc.psum_tensor("b1_pA", [128, 512], F32))
            pB = ph.enter_context(nc.psum_tensor("b1_pB", [128, 512], F32))
            pC = ph.enter_context(nc.psum_tensor("b1_pC", [128, 512], F32))
            pE = ph.enter_context(nc.psum_tensor("b1_pE", [128, 512], F32))
            pF_ = ph.enter_context(nc.psum_tensor("b1_pF", [128, 512], F32))

            def RW(r):
                return rowst[0:1, r * 512:(r + 1) * 512]

            with nc.Block() as blk:
                st = Stepper(blk, sem, ctr, dsem, dctr)

                def az(t, e):
                    t.memset(asb[:], 0.0)

                def drn(t, e):
                    drain_all(e)
                    t.nop()

                st.step(vector=az, sync=drn)

            for c in range(S // TCB):
                with nc.Block() as blk:
                    st = Stepper(blk, sem, ctr, dsem, dctr)

                    for fc in range(16):
                        xt = (xt0, xt1)[fc % 2]

                        def ldx(t, e, fc=fc, xt=xt, c=c):
                            t.dma_start(out=xt[:], in_=XR2.ap()[fc, :, c * TCB:(c + 1) * TCB], inc=16)

                        st.step(sync=ldx)

                        def mms(t, e, fc=fc, xt=xt):
                            t.matmul(pA[0:1, :], ones[:, 0:1], xt[:],
                                     start=(fc == 0), stop=(fc == 15))

                        def sq(t, e, xt=xt):
                            t.activation(x2t[:], xt[:], AF.Square)

                        st.step(tensor=mms, scalar=sq)

                        def mmq(t, e, fc=fc):
                            t.matmul(pC[0:1, :], ones[:, 0:1], x2t[:],
                                     start=(fc == 0), stop=(fc == 15))

                        st.step(tensor=mmq)

                    def evs(t, e):
                        t.tensor_scalar_mul(RW(0), pA[0:1, :], 1.0 / D)
                        t.tensor_scalar_mul(RW(1), pC[0:1, :], 1.0 / D)

                    st.step(vector=evs)

                    def var1(t, e):
                        t.tensor_mul(RW(2), RW(0), RW(0))

                    st.step(vector=var1)

                    def var2(t, e):
                        t.tensor_sub(RW(1), RW(1), RW(2))

                    st.step(vector=var2)

                    def var3(t, e):
                        t.tensor_scalar_add(RW(1), RW(1), LN_EPS)

                    st.step(vector=var3)

                    def var4(t, e):
                        t.reciprocal(RW(2), RW(1))

                    st.step(vector=var4)

                    def var5(t, e):
                        t.activation(RW(2), RW(2), AF.Sqrt)

                    st.step(scalar=var5)

                    def var6(t, e):
                        t.tensor_mul(RW(3), RW(0), RW(2))

                    st.step(vector=var6)

                    def bcs(t, e):
                        t.matmul(pA[:], ones[0:1, :], RW(2), start=True, stop=True)
                        mmi = e.matmul(pC[:], ones[0:1, :], RW(3), start=True, stop=True)
                        mmi.then_inc(sem, 1)
                        ctr[0] += 1

                    st.step(tensor=bcs)

                    for fc in range(16):
                        xt = (xt0, xt1)[fc % 2]

                        def ldx2(t, e, fc=fc, xt=xt, c=c):
                            t.dma_start(out=xt[:], in_=XR2.ap()[fc, :, c * TCB:(c + 1) * TCB], inc=16)

                        st.step(sync=ldx2)

                        def nm1(t, e, xt=xt):
                            t.tensor_mul(x2t[:], xt[:], pA[:])

                        st.step(vector=nm1)

                        def nm2(t, e, fc=fc):
                            t.tensor_sub(xn[:, fc * TCB:(fc + 1) * TCB], x2t[:], pC[:])

                        st.step(vector=nm2)

                    def spxn(t, e, c=c):
                        t.dma_start(out=XN2B.ap()[:, :, c * TCB:(c + 1) * TCB]
                                    .rearrange("fc p t2 -> p fc t2"),
                                    in_=xn.ap().rearrange("p (fc t2) -> p fc t2", fc=16), inc=16)

                    st.step(sync=spxn)

                    def ldcs(t, e, c=c):
                        t.dma_start(out=cst.ap()[:, 0:4 * TCB].rearrange("p (pc t2) -> p pc t2", pc=4),
                                    in_=g["costab"].ap()[:, c * TCB:(c + 1) * TCB]
                                    .rearrange("(pc p) t2 -> p pc t2", p=128), inc=16)
                        t.dma_start(out=cst.ap()[:, 4 * TCB:8 * TCB].rearrange("p (pc t2) -> p pc t2", pc=4),
                                    in_=g["sintab"].ap()[:, c * TCB:(c + 1) * TCB]
                                    .rearrange("(pc p) t2 -> p pc t2", p=128), inc=16)

                    st.step(sync=ldcs)

                    def b1dst(oc):
                        if oc < 8:
                            return qro[:, oc * TCB:(oc + 1) * TCB], False
                        if oc < 16:
                            return qstg[:, (oc % 2) * TCB:(oc % 2 + 1) * TCB], True
                        if oc < 24:
                            return kro[:, (oc - 16) * TCB:(oc - 15) * TCB], False
                        if oc < 32:
                            return kps[:, (oc - 24) * TCB:(oc - 23) * TCB], False
                        return vb[:, (oc - 32) * TCB:(oc - 31) * TCB], False

                    def ldw0(t, e):
                        for j in range(2):
                            t.dma_start(out=wsl.ap()[:, j * 2048:(j + 1) * 2048],
                                        in_=CWIN.ap()[j * 128:(j + 1) * 128, :], inc=16)

                    st.step(sync=ldw0)

                    for op in range(24):
                        oc0, oc1 = 2 * op, 2 * op + 1

                        def pfw(t, e, op=op):
                            for j in range(2):
                                oc = 2 * op + 2 + j
                                if oc < 48:
                                    t.dma_start(out=wsl.ap()[:, (oc % 4) * 2048:(oc % 4 + 1) * 2048],
                                                in_=CWIN.ap()[oc * 128:(oc + 1) * 128, :], inc=16)

                        def qkvmm(t, e, oc0=oc0, oc1=oc1):
                            for oc, pp in ((oc0, pE), (oc1, pF_)):
                                sl = wsl[:, (oc % 4) * 2048:(oc % 4 + 1) * 2048]
                                for kc in range(16):
                                    mmi = e.matmul(pp[:], sl[:, kc * 128:(kc + 1) * 128],
                                                   xn[:, kc * TCB:(kc + 1) * TCB],
                                                   start=(kc == 0), stop=(kc == 15))
                            mmi.then_inc(sem, 1)
                            ctr[0] += 1

                        st.step(tensor=qkvmm, sync=pfw)

                        def ev(t, e, oc0=oc0, oc1=oc1):
                            d0, _ = b1dst(oc0)
                            d1, _ = b1dst(oc1)
                            t.activation(d0, pE[:], AF.Identity, bias=binc[:, oc0:oc0 + 1])
                            t.activation(d1, pF_[:], AF.Identity, bias=binc[:, oc1:oc1 + 1])

                        st.step(scalar=ev)

                        if 8 <= oc0 < 16:
                            def wr(t, e, oc0=oc0, oc1=oc1, c=c):
                                t.dma_start(out=QR.ap()[oc0, :, c * TCB:(c + 1) * TCB],
                                            in_=qstg[:, (oc0 % 2) * TCB:(oc0 % 2 + 1) * TCB], inc=16)
                                t.dma_start(out=QR.ap()[oc1, :, c * TCB:(c + 1) * TCB],
                                            in_=qstg[:, (oc1 % 2) * TCB:(oc1 % 2 + 1) * TCB], inc=16)

                            st.step(sync=wr)

                    for rot in (qro, kro):
                        for e_ in range(4):
                            sl = slice(e_ * TCB, (e_ + 1) * TCB)
                            slo = slice((e_ + 4) * TCB, (e_ + 5) * TCB)
                            cs = slice(e_ * TCB, (e_ + 1) * TCB)
                            sn = slice((4 + e_) * TCB, (5 + e_) * TCB)

                            def r1(t, e, rot=rot, sl=sl, slo=slo, cs=cs, sn=sn):
                                t.tensor_mul(x2t[:], rot[:, sl], cst[:, cs])
                                t.tensor_mul(ksq[:, 0:512], rot[:, slo], cst[:, sn])
                                t.tensor_mul(fkv[:], rot[:, slo], cst[:, cs])
                                t.tensor_mul(xt0[:], rot[:, sl], cst[:, sn])

                            st.step(vector=r1)

                            def r2(t, e, rot=rot, sl=sl, slo=slo):
                                t.tensor_sub(rot[:, sl], x2t[:], ksq[:, 0:512])
                                t.tensor_add(rot[:, slo], fkv[:], xt0[:])

                            st.step(vector=r2)

                    for op in range(4):
                        def wq(t, e, op=op):
                            t.tensor_copy(qstg[:, 0:TCB], qro[:, (2 * op) * TCB:(2 * op + 1) * TCB])
                            t.tensor_copy(qstg[:, TCB:2 * TCB], qro[:, (2 * op + 1) * TCB:(2 * op + 2) * TCB])

                        st.step(vector=wq)

                        def wq2(t, e, op=op, c=c):
                            t.dma_start(out=QR.ap()[2 * op, :, c * TCB:(c + 1) * TCB],
                                        in_=qstg[:, 0:TCB], inc=16)
                            t.dma_start(out=QR.ap()[2 * op + 1, :, c * TCB:(c + 1) * TCB],
                                        in_=qstg[:, TCB:2 * TCB], inc=16)

                        st.step(sync=wq2)

                    for q4 in range(4):
                        def sq1(t, e, q4=q4):
                            for j in range(4):
                                i = q4 * 4 + j
                                srck = kro if i < 8 else kps
                                off = i if i < 8 else i - 8
                                t.activation(ksq[:, j * 512:(j + 1) * 512],
                                             srck[:, off * TCB:(off + 1) * TCB], AF.Square)

                        st.step(scalar=sq1)

                        def mmn(t, e, q4=q4):
                            for j in range(4):
                                i = q4 * 4 + j
                                mmi = e.matmul(pA[0:1, :], ones[:, 0:1], ksq[:, j * 512:(j + 1) * 512],
                                               start=(i == 0), stop=(i == 15))
                            mmi.then_inc(sem, 1)
                            ctr[0] += 1

                        st.step(tensor=mmn)

                    def nrm1(t, e):
                        t.tensor_copy(RW(0), pA[0:1, :])

                    st.step(vector=nrm1)

                    def nrm2(t, e):
                        t.activation(RW(0), RW(0), AF.Sqrt)

                    st.step(scalar=nrm2)

                    def nrm3(t, e):
                        t.tensor_scalar_max(RW(0), RW(0), 1e-12)

                    st.step(vector=nrm3)

                    def nrm4(t, e):
                        t.reciprocal(RW(0), RW(0))

                    st.step(vector=nrm4)

                    def bci(t, e):
                        t.matmul(pA[:], ones[0:1, :], RW(0), start=True, stop=True)

                    st.step(tensor=bci)

                    for p2 in range(8):
                        i0, i1 = 2 * p2, 2 * p2 + 1

                        def fk1(t, e, i0=i0, i1=i1):
                            for j, i in enumerate((i0, i1)):
                                srck = kro if i < 8 else kps
                                off = i if i < 8 else i - 8
                                t.tensor_mul(ksq[:, (2 + j) * 512:(3 + j) * 512],
                                             srck[:, off * TCB:(off + 1) * TCB], pA[:])

                        st.step(vector=fk1)

                        def fk2(t, e, i0=i0, i1=i1):
                            for j, i in enumerate((i0, i1)):
                                t.tensor_mul(ksq[:, j * 512:(j + 1) * 512],
                                             ksq[:, (2 + j) * 512:(3 + j) * 512],
                                             vb[:, i * TCB:(i + 1) * TCB])

                        st.step(vector=fk2)

                        def fk3(t, e):
                            t.reduce_sum(asb[:, 16:17], ksq[:, 0:512], axis=AX.X)
                            t.reduce_sum(asb[:, 17:18], ksq[:, 512:1024], axis=AX.X)

                        st.step(vector=fk3)

                        def fk4(t, e, i0=i0, i1=i1):
                            t.tensor_add(asb[:, i0:i0 + 1], asb[:, i0:i0 + 1], asb[:, 16:17])
                            t.tensor_add(asb[:, i1:i1 + 1], asb[:, i1:i1 + 1], asb[:, 17:18])

                        st.step(vector=fk4)

    if "B2" in phases:
        with ExitStack() as ph:
            qrc = ph.enter_context(nc.sbuf_tensor("b2_qr", [128, 16 * TCB], BF16))
            yb = ph.enter_context(nc.sbuf_tensor("b2_y", [128, 16 * TCB], BF16))
            wsl = ph.enter_context(nc.sbuf_tensor("b2_wsl", [128, 4 * 2048], BF16))
            xnt = ph.enter_context(nc.sbuf_tensor("b2_xnt", [128, 2 * TCB], BF16))
            xrt = ph.enter_context(nc.sbuf_tensor("b2_xrt", [128, 2 * TCB], F32))
            ot = ph.enter_context(nc.sbuf_tensor("b2_ot", [128, 2 * TCB], F32))
            pE = ph.enter_context(nc.psum_tensor("b2_pE", [128, 512], F32))
            pF_ = ph.enter_context(nc.psum_tensor("b2_pF", [128, 512], F32))

            for c in range(S // TCB):
                with nc.Block() as blk:
                    st = Stepper(blk, sem, ctr, dsem, dctr)

                    def ldq(t, e, c=c):
                        t.dma_start(out=qrc.ap().rearrange("p (fc t2) -> p fc t2", fc=16),
                                    in_=QR.ap()[:, :, c * TCB:(c + 1) * TCB]
                                    .rearrange("fc p t2 -> p fc t2"), inc=16)

                    def ldw0(t, e):
                        for j in range(2):
                            t.dma_start(out=wsl.ap()[:, j * 2048:(j + 1) * 2048],
                                        in_=CWOUT.ap()[j * 128:(j + 1) * 128, :], inc=16)

                    st.step(sync=ldq, scalar=ldw0)

                    def ymul(t, e):
                        for fc in range(16):
                            t.tensor_single_scalar(yb[:, fc * TCB:(fc + 1) * TCB],
                                                   qrc[:, fc * TCB:(fc + 1) * TCB],
                                                   asb[:, fc:fc + 1], ALU.mult)

                    st.step(vector=ymul)

                    for op in range(8):
                        oc0, oc1 = 2 * op, 2 * op + 1

                        def pf(t, e, op=op):
                            for j in range(2):
                                oc = 2 * op + 2 + j
                                if oc < 16:
                                    t.dma_start(
                                        out=wsl.ap()[:, (oc % 4) * 2048:(oc % 4 + 1) * 2048],
                                        in_=CWOUT.ap()[oc * 128:(oc + 1) * 128, :], inc=16)

                        def ldxr(t, e, oc0=oc0, oc1=oc1, c=c):
                            for j, oc in enumerate((oc0, oc1)):
                                t.dma_start(out=xrt[:, j * TCB:(j + 1) * TCB],
                                            in_=XR2.ap()[oc, :, c * TCB:(c + 1) * TCB], inc=16)
                                t.dma_start(out=xnt[:, j * TCB:(j + 1) * TCB],
                                            in_=XN2B.ap()[oc, :, c * TCB:(c + 1) * TCB], inc=16)

                        def omm(t, e, oc0=oc0, oc1=oc1):
                            for oc, pp in ((oc0, pE), (oc1, pF_)):
                                sl = wsl[:, (oc % 4) * 2048:(oc % 4 + 1) * 2048]
                                for kc in range(16):
                                    mmi = e.matmul(pp[:], sl[:, kc * 128:(kc + 1) * 128],
                                                   yb[:, kc * TCB:(kc + 1) * TCB],
                                                   start=(kc == 0), stop=(kc == 15))
                            mmi.then_inc(sem, 1)
                            ctr[0] += 1

                        st.step(tensor=omm, scalar=pf, sync=ldxr)

                        def oev(t, e, oc0=oc0, oc1=oc1):
                            t.activation(ot[:, 0:TCB], pE[:], AF.Identity,
                                         bias=bias[:, BOUT + oc0:BOUT + oc0 + 1])
                            t.activation(ot[:, TCB:2 * TCB], pF_[:], AF.Identity,
                                         bias=bias[:, BOUT + oc1:BOUT + oc1 + 1])

                        st.step(scalar=oev)

                        def oadd1(t, e):
                            t.tensor_add(ot[:], ot[:], xrt[:])

                        st.step(vector=oadd1)

                        def oadd2(t, e):
                            t.tensor_add(ot[:], ot[:], xnt[:])

                        st.step(vector=oadd2)

                        def owr(t, e, oc0=oc0, oc1=oc1, c=c):
                            t.dma_start(out=XR3.ap()[oc0, :, c * TCB:(c + 1) * TCB],
                                        in_=ot[:, 0:TCB], inc=16)
                            t.dma_start(out=XR3.ap()[oc1, :, c * TCB:(c + 1) * TCB],
                                        in_=ot[:, TCB:2 * TCB], inc=16)

                        st.step(sync=owr)

    if "F2" in phases:
        phase_F(XR3, None, "f2")
        f_ebase[0] += 8

    with nc.Block() as blk:
        st = Stepper(blk, sem, ctr, dsem, dctr)
        for name in taps:
            src = scrmap[name]

            def cp(t, e, src=src, name=name):
                drain_all(e)
                tt_ = nc.dram_tensor("tap_" + name, list(src.shape), src.dtype,
                                     kind="ExternalOutput")
                t.dma_start(out=tt_.ap(), in_=src.ap(), inc=16)

            st.step(sync=cp)

        def fin(t, e):
            drain_all(e)
            t.nop()

        st.step(sync=fin)

    es.close()
    global LAST_CTRS
    LAST_CTRS = (ctr[0], dctr[0])
    return nc


INPUT_NAMES = ["x", "wkT", "wvT", "bq_c", "bk_c", "bv_c", "bg_c", "bmb_c",
               "mem", "normT", "winT", "bin_c", "bout_c", "wq_sl", "wgt_sl",
               "wmb_sl", "wout_sl", "wg_p", "wu_p", "wd_p", "costab", "sintab",
               "ident", "identb", "ones", "onehot4"]


def kernel(**inputs):
    W = host_prep(inputs)
    x_all = np.ascontiguousarray(np.asarray(inputs["L0_attn_a_hidden_states"], np.float32))
    nc = build_nc()
    in_maps = []
    for b in range(NCORE):
        m = {k: W[k] for k in INPUT_NAMES if k != "x"}
        m["x"] = np.ascontiguousarray(x_all[b])
        in_maps.append(m)
    res = run_bass_kernel_spmd(nc, in_maps, core_ids=list(range(NCORE)))
    return np.stack([r["out"] for r in res.results], axis=0)


# version 30
# speedup vs baseline: 1.2990x; 1.0864x over previous
import numpy as np
import ml_dtypes
import concourse.bass as bass
import concourse.mybir as mybir
from concourse.bass_utils import run_bass_kernel_spmd
from contextlib import ExitStack

F32 = mybir.dt.float32
BF16 = mybir.dt.bfloat16
AF = mybir.ActivationFunctionType
ALU = mybir.AluOpType
AX = mybir.AxisListType

S, D = 4096, 2048
NCORE = 8
EPS_A = 1e-8
LN_EPS = 1e-5

ALL_PHASES = ("P0", "A1", "MID", "A2", "F1", "B1", "B2", "F2")
DBG = {"no_out": False, "no_otr": False, "no_oev_wait": False}
ASYNC_F = False
LAST_CTRS = (0, 0)
GSEM = [None, [0]]


class Tag:
    def __init__(self, eng, sem, ctr, dsem, dctr):
        self._eng = eng
        self._sem = sem
        self._ctr = ctr
        self._dsem = dsem
        self._dctr = dctr

    def ldma(self, out, in_, **k):
        self._eng.dma_start(out=out, in_=in_, **k).then_inc(self._dsem, 16)
        self._dctr[0] += 16
        return self._dctr[0]

    def gdma(self, out, in_, **k):
        self._eng.dma_start(out=out, in_=in_, **k).then_inc(GSEM[0], 16)
        GSEM[1][0] += 16
        return GSEM[1][0]

    def gwait(self, val):
        if val is not None:
            self._eng.wait_ge(GSEM[0], val)

    def dwait(self, val):
        if val is not None:
            self._eng.wait_ge(self._dsem, val)

    def __getattr__(self, name):
        f = getattr(self._eng, name)

        def g(*a, inc=1, **k):
            inst = f(*a, **k)
            inst.then_inc(self._sem, inc)
            self._ctr[0] += inc
            return inst

        return g


class Stepper:

    def __init__(self, block, sem, ctr, dsem=None, dctr=None):
        self.block = block
        self.sem = sem
        self.ctr = ctr
        self.dsem = dsem
        self.dctr = dctr

    def step(self, **eng_fns):
        base = self.ctr[0]
        for ename, fn in eng_fns.items():
            def body(eng, fn=fn, base=base):
                eng.wait_ge(self.sem, base)
                fn(Tag(eng, self.sem, self.ctr, self.dsem, self.dctr), eng)

            getattr(self.block, ename)(body)

    def drain_lazy(self, ename="sync"):
        base = self.ctr[0]
        tgt = self.dctr[0]

        def body(eng, base=base, tgt=tgt):
            eng.wait_ge(self.sem, base)
            eng.wait_ge(self.dsem, tgt)
            eng.nop().then_inc(self.sem, 1)

        getattr(self.block, ename)(body)
        self.ctr[0] += 1


def _col_layout(b):
    n = b.shape[0] // 128
    return np.ascontiguousarray(b.reshape(n, 128).T.astype(np.float32))


def host_prep(inputs):
    f = np.float32
    W = {}
    W["wqT"] = np.ascontiguousarray(np.asarray(inputs["L0_attn_a_Wq"], f).T)
    W["wkT"] = np.ascontiguousarray(np.asarray(inputs["L0_attn_a_Wk"], f).T)
    W["wvT"] = np.ascontiguousarray(np.asarray(inputs["L0_attn_a_Wv"], f).T)
    W["wgT"] = np.ascontiguousarray(np.asarray(inputs["L0_attn_a_Wg"], f).T)
    W["wmbT"] = np.ascontiguousarray(np.asarray(inputs["L0_attn_a_Wmb"], f).T)
    W["bq_c"] = _col_layout(np.asarray(inputs["L0_attn_a_bq"], f))
    W["bk_c"] = _col_layout(np.asarray(inputs["L0_attn_a_bk"], f))
    W["bv_c"] = _col_layout(np.asarray(inputs["L0_attn_a_bv"], f))
    W["bg_c"] = _col_layout(np.asarray(inputs["L0_attn_a_bg"], f))
    W["bmb_c"] = _col_layout(np.asarray(inputs["L0_attn_a_bmb"], f))
    W["mem"] = np.ascontiguousarray(np.asarray(inputs["L0_attn_a_memory"], f))
    W["normT"] = np.ascontiguousarray(np.asarray(inputs["L0_attn_a_norm"], f).T)

    rp = np.concatenate([np.arange(0, 1024, 2), np.arange(1, 1024, 2), np.arange(1024, 2048)])
    win = np.asarray(inputs["L0_attn_b_W_in"], f)
    perm6k = np.concatenate([rp, 2048 + rp, 4096 + rp])
    W["winT"] = np.ascontiguousarray(win[perm6k, :].T)
    W["bin_c"] = _col_layout(np.asarray(inputs["L0_attn_b_b_in"], f)[perm6k])
    W["woutT"] = np.ascontiguousarray(np.asarray(inputs["L0_attn_b_W_out"], f).T[rp, :])
    W["bout_c"] = _col_layout(np.asarray(inputs["L0_attn_b_b_out"], f))

    gc = np.arange(16)[:, None, None]
    j = np.arange(4)[None, :, None]
    p = np.arange(128)[None, None, :]
    permF = (4 * (gc * 128 + p) + j).reshape(-1)

    def slab_major(a, nk, nm):
        K2, M2 = a.shape
        ms = M2 // nm
        return np.ascontiguousarray(
            a.reshape(nk, 128, nm, ms).transpose(2, 1, 0, 3).reshape(nm * 128, nk * ms))

    W["wg_p"] = slab_major(np.asarray(inputs["L0_ffn_w_gate"], f)[:, permF], 16, 16)
    W["wu_p"] = slab_major(np.asarray(inputs["L0_ffn_w_up"], f)[:, permF], 16, 16)
    W["wd_p"] = slab_major(np.asarray(inputs["L0_ffn_w_down"], f)[permF, :], 64, 16)
    W["winT"] = slab_major(W["winT"], 16, 48)
    W["wq_sl"] = slab_major(W["wqT"], 16, 16)
    W["wgt_sl"] = slab_major(W["wgT"], 16, 16)
    W["wmb_sl"] = slab_major(W["wmbT"], 16, 16)
    W["wout_sl"] = slab_major(W["woutT"], 16, 16)

    rot = D // 2
    inv_freq = 1.0 / (10000.0 ** (np.arange(0, rot, 2, dtype=np.float64) / rot))
    freqs = np.arange(S, dtype=np.float64)[:, None] * inv_freq[None, :]
    W["costab"] = np.ascontiguousarray(np.cos(freqs).T.astype(f))
    W["sintab"] = np.ascontiguousarray(np.sin(freqs).T.astype(f))

    W["ident"] = np.eye(128, dtype=f)
    W["identb"] = np.eye(128, dtype=ml_dtypes.bfloat16)
    W["ones"] = np.ones((128, 128), dtype=f)
    oh = np.zeros((4, 512), dtype=f)
    for r in range(4):
        oh[r, r * 128:(r + 1) * 128] = 1.0
    W["onehot4"] = oh
    return W


def build_nc(taps=(), phases=ALL_PHASES):
    nc = bass.Bass("TRN2", target_bir_lowering=False)
    g = {}

    def din(name, shape, dtype=F32):
        g[name] = nc.dram_tensor(name, shape, dtype, kind="ExternalInput")

    din("x", [S, D])
    din("wkT", [D, 512]); din("wvT", [D, 512])
    din("bq_c", [128, 16]); din("bk_c", [128, 4]); din("bv_c", [128, 4])
    din("bg_c", [128, 16]); din("bmb_c", [128, 16])
    din("mem", [16, 768, 128]); din("normT", [768, 16])
    din("winT", [48 * 128, D]); din("bin_c", [128, 48])
    din("bout_c", [128, 16])
    din("wq_sl", [D, D]); din("wgt_sl", [D, D]); din("wmb_sl", [D, D])
    din("wout_sl", [D, D])
    din("wg_p", [D, 8192]); din("wu_p", [D, 8192]); din("wd_p", [D, 8192])
    din("costab", [512, S]); din("sintab", [512, S])
    din("ident", [128, 128]); din("identb", [128, 128], BF16)
    din("ones", [128, 128]); din("onehot4", [4, 512])

    out = nc.dram_tensor("out", [S, D], F32, kind="ExternalOutput")

    XN1 = nc.dram_tensor("XN1", [16, 128, S], BF16)
    XN2B = nc.dram_tensor("XN2B", [16, 128, S], BF16)
    QR = nc.dram_tensor("QR", [16, 128, S], BF16)
    XR1 = nc.dram_tensor("XR1", [16, 128, S], F32)
    XR2 = nc.dram_tensor("XR2", [16, 128, S], F32)
    XR3 = nc.dram_tensor("XR3", [16, 128, S], F32)
    ACC = nc.dram_tensor("ACC", [24, 128, 516], F32)
    ACC8 = nc.dram_tensor("ACC8", [8, 24, 128, 516], F32)
    ACC_R = nc.dram_tensor("ACC_R", [24, 128, 516], F32)
    CWG = nc.dram_tensor("CWG", [D, 8192], BF16)
    CWU = nc.dram_tensor("CWU", [D, 8192], BF16)
    CWD = nc.dram_tensor("CWD", [D, 8192], BF16)
    CWIN = nc.dram_tensor("CWIN", [48 * 128, D], BF16)
    CWQ = nc.dram_tensor("CWQ", [D, D], BF16)
    CWGT = nc.dram_tensor("CWGT", [D, D], BF16)
    CWMB = nc.dram_tensor("CWMB", [D, D], BF16)
    CWOUT = nc.dram_tensor("CWOUT", [D, D], BF16)
    scrmap = {"XN1": XN1, "XN2B": XN2B, "QR": QR, "XR1": XR1, "XR2": XR2,
              "XR3": XR3, "ACC": ACC, "ACC_R": ACC_R, "CWQ": CWQ}

    ctr = [0]
    dctr = [0]
    es = ExitStack()
    sem = es.enter_context(nc.semaphore("seq"))
    dsem = es.enter_context(nc.semaphore("dseq"))
    gsem = es.enter_context(nc.semaphore("gseq"))
    fsem = es.enter_context(nc.semaphore("fchain"))
    ftc = [0]

    def cinc(inst, n=1):
        inst.then_inc(fsem, n)
        ftc[0] += n
        return ftc[0]
    GSEM[0] = gsem
    GSEM[1] = [0]
    slots = [(es.enter_context(nc.semaphore("slot%d" % i)), [0]) for i in range(22)]

    def sput(eng, i, out, in_):
        h, cc = slots[i]
        eng.dma_start(out=out, in_=in_).then_inc(h, 16)
        cc[0] += 16
        return cc[0]

    def swt(eng, i, val):
        if val:
            eng.wait_ge(slots[i][0], val)

    def drain_all(eng):
        for h, cc in slots:
            if cc[0]:
                eng.wait_ge(h, cc[0])
        eng.wait_ge(dsem, dctr[0])
        if GSEM[1][0]:
            eng.wait_ge(gsem, GSEM[1][0])

    ident = es.enter_context(nc.sbuf_tensor("c_ident", [128, 128], F32))
    identb = es.enter_context(nc.sbuf_tensor("c_identb", [128, 128], BF16))
    ones = es.enter_context(nc.sbuf_tensor("c_ones", [128, 128], F32))
    oh4 = es.enter_context(nc.sbuf_tensor("c_oh4", [4, 512], F32))
    bias = es.enter_context(nc.sbuf_tensor("c_bias", [128, 96], F32))
    binc = es.enter_context(nc.sbuf_tensor("c_binc", [128, 48], F32))
    BQ, BK, BV, BG, BMB, BOUT, NBK, NBQ = 0, 16, 20, 24, 40, 56, 72, 76

    with nc.Block() as blk:
        st = Stepper(blk, sem, ctr, dsem, dctr)

        def ld_const(t, e):
            t.dma_start(out=ident[:], in_=g["ident"].ap(), inc=16)
            t.dma_start(out=identb[:], in_=g["identb"].ap(), inc=16)
            t.dma_start(out=ones[:], in_=g["ones"].ap(), inc=16)
            t.dma_start(out=oh4[:], in_=g["onehot4"].ap(), inc=16)
            t.dma_start(out=bias[:, BQ:BQ + 16], in_=g["bq_c"].ap(), inc=16)
            t.dma_start(out=bias[:, BK:BK + 4], in_=g["bk_c"].ap(), inc=16)
            t.dma_start(out=bias[:, BV:BV + 4], in_=g["bv_c"].ap(), inc=16)
            t.dma_start(out=bias[:, BG:BG + 16], in_=g["bg_c"].ap(), inc=16)
            t.dma_start(out=bias[:, BMB:BMB + 16], in_=g["bmb_c"].ap(), inc=16)
            t.dma_start(out=bias[:, BOUT:BOUT + 16], in_=g["bout_c"].ap(), inc=16)
            t.dma_start(out=binc[:], in_=g["bin_c"].ap(), inc=16)

        st.step(sync=ld_const)

        def negb(t, e):
            t.tensor_scalar_mul(bias[:, NBK:NBK + 4], bias[:, BK:BK + 4], -1.0)
            t.tensor_scalar_mul(bias[:, NBQ:NBQ + 16], bias[:, BQ:BQ + 16], -1.0)

        st.step(vector=negb)

    if "P0" in phases:
        with nc.Block() as blk, nc.sbuf_tensor("cbuf", [128, 2 * 8192], BF16) as cbuf:
            st = Stepper(blk, sem, ctr, dsem, dctr)
            jobs = [(g["wg_p"], CWG, D, 8192, 2), (g["wu_p"], CWU, D, 8192, 2),
                    (g["wd_p"], CWD, D, 8192, 2), (g["winT"], CWIN, 48 * 128, D, 8),
                    (g["wq_sl"], CWQ, D, D, 8), (g["wgt_sl"], CWGT, D, D, 8),
                    (g["wmb_sl"], CWMB, D, D, 8), (g["wout_sl"], CWOUT, D, D, 8)]
            for src, dst, rows_, cols, rblk in jobs:
                for r0 in range(0, rows_, 128 * rblk):
                    nb = rblk * cols

                    gv = [None]

                    def ld(t, e, src=src, r0=r0, cols=cols, rblk=rblk, nb=nb, gv=gv):
                        gv[0] = t.gdma(cbuf.ap()[:, 0:nb].rearrange("p (r c) -> p r c", r=rblk),
                                       src.ap()[r0:r0 + 128 * rblk, :]
                                       .rearrange("(r p) c -> p r c", p=128))

                    def sv(t, e, dst=dst, r0=r0, cols=cols, rblk=rblk, nb=nb, gv=gv):
                        t.gwait(gv[0])
                        t.dma_start(out=dst.ap()[r0:r0 + 128 * rblk, :]
                                    .rearrange("(r p) c -> p r c", p=128),
                                    in_=cbuf.ap()[:, 0:nb].rearrange("p (r c) -> p r c", r=rblk),
                                    inc=16)

                    st.step(gpsimd=ld)
                    st.step(sync=sv)

    ases = ExitStack()
    memsb = ases.enter_context(nc.sbuf_tensor("memsb", [128, 16 * 6 * 128], BF16))
    normsb = ases.enter_context(nc.sbuf_tensor("normsb", [128, 6 * 16], BF16))

    def mem_tile(h, fc):
        o = (h * 6 + fc) * 128
        return memsb[:, o:o + 128]

    with nc.Block() as blk:
        st = Stepper(blk, sem, ctr, dsem, dctr)

        gvm = [None]

        def ld_mem(t, e):
            t.gdma(memsb.ap().rearrange("p (h fc d) -> p h fc d", h=16, fc=6),
                   g["mem"].ap().rearrange("h (fc p) d -> p h fc d", p=128))
            gvm[0] = t.gdma(normsb.ap().rearrange("p (fc h) -> p fc h", fc=6),
                            g["normT"].ap().rearrange("(fc p) h -> p fc h", p=128))

        st.step(gpsimd=ld_mem)

        def ld_mem_drain(t, e):
            t.gwait(gvm[0])
            t.nop()

        st.step(gpsimd=ld_mem_drain)

    TC = 512
    if "A1" in phases:
        with ExitStack() as ph:
            x_tm = ph.enter_context(nc.sbuf_tensor("a1_xtm", [128, 4 * D], F32))
            scr2k = ph.enter_context(nc.sbuf_tensor("a1_scr2k", [128, 2 * D], F32))
            stats = ph.enter_context(nc.sbuf_tensor("a1_stats", [128, 16], F32))
            xn_tm = ph.enter_context(nc.sbuf_tensor("a1_xntm", [128, 4 * D], BF16))
            xn_fm = ph.enter_context(nc.sbuf_tensor("a1_xnfm", [128, 16 * TC], BF16))
            wk_sb = ph.enter_context(nc.sbuf_tensor("a1_wk", [128, 16 * 512], BF16))
            wv_sb = ph.enter_context(nc.sbuf_tensor("a1_wv", [128, 16 * 512], BF16))
            wmb_sl = ph.enter_context(nc.sbuf_tensor("a1_wmb", [128, 2 * 2048], BF16))
            k_fm = ph.enter_context(nc.sbuf_tensor("a1_kfm", [128, 4 * TC], F32))
            v_fm = ph.enter_context(nc.sbuf_tensor("a1_vfm", [128, 4 * TC], F32))
            mb_fm = xn_tm
            k_tm = ph.enter_context(nc.sbuf_tensor("a1_ktm", [128, 4 * 128], F32))
            xx = ph.enter_context(nc.sbuf_tensor("a1_xx", [128, 4 * 260], F32))
            u_tm = ph.enter_context(nc.sbuf_tensor("a1_utm", [128, 4 * 768], BF16))
            u_fm = ph.enter_context(nc.sbuf_tensor("a1_ufm", [128, 6 * TC], BF16))
            ss6 = ph.enter_context(nc.sbuf_tensor("a1_ss6", [128, 4 * 8], F32))
            num_sb = ph.enter_context(nc.sbuf_tensor("a1_num", [128, TC], F32))
            vninv = ph.enter_context(nc.sbuf_tensor("a1_vninv", [128, TC], F32))
            p1 = ph.enter_context(nc.sbuf_tensor("a1_p1", [128, TC], F32))
            wmv_fm = ph.enter_context(nc.sbuf_tensor("a1_wmvfm", [128, TC], BF16))
            rtm = ph.enter_context(nc.sbuf_tensor("a1_rtm", [128, 4 * 520], BF16))
            astg = ph.enter_context(nc.sbuf_tensor("a1_astg", [128, 2 * 516], F32))

            pT0 = ph.enter_context(nc.psum_tensor("a1_pT0", [128, 1024], BF16))
            pT1 = ph.enter_context(nc.psum_tensor("a1_pT1", [128, 1024], BF16))
            pP0 = ph.enter_context(nc.psum_tensor("a1_pP0", [128, 512], F32))
            pP1 = ph.enter_context(nc.psum_tensor("a1_pP1", [128, 512], F32))
            pN0 = ph.enter_context(nc.psum_tensor("a1_pN0", [128, 512], F32))
            pN1 = ph.enter_context(nc.psum_tensor("a1_pN1", [128, 512], F32))
            pSm = ph.enter_context(nc.psum_tensor("a1_pSm", [128, 512], F32))
            pBC = ph.enter_context(nc.psum_tensor("a1_pBC", [128, 512], F32))

            def R(r):
                return scr2k[0:1, r * 512:(r + 1) * 512]

            def R4(r):
                return scr2k[0:4, r * 512:(r + 1) * 512]

            vload = {}
            vspill = {}
            for c in range(S // TC):
                with nc.Block() as blk:
                    st = Stepper(blk, sem, ctr, dsem, dctr)

                    def ld_x(t, e, c=c):
                        vload["x"] = sput(e, 0, x_tm.ap().rearrange("p (tt d) -> p tt d", tt=4),
                                          g["x"].ap()[c * TC:(c + 1) * TC, :]
                                          .rearrange("(tt p) d -> p tt d", p=128))
                        vload["wmb0"] = sput(e, 1, wmb_sl.ap()[:, 0:2048], CWMB.ap()[0:128, :])

                    def ld_kv(t, e, c=c):
                        if c == 0:
                            t.gdma(wk_sb.ap().rearrange("p (kc m) -> p kc m", kc=16),
                                   g["wkT"].ap().rearrange("(kc p) m -> p kc m", p=128))
                            vload["kv"] = t.gdma(wv_sb.ap().rearrange("p (kc m) -> p kc m", kc=16),
                                                 g["wvT"].ap().rearrange("(kc p) m -> p kc m", p=128))

                    st.step(sync=ld_x, gpsimd=ld_kv)

                    def ln_sums(t, e):
                        swt(e, 0, vload["x"])
                        for tt in range(4):
                            t.reduce_sum(stats[:, tt:tt + 1], x_tm[:, tt * D:(tt + 1) * D], axis=AX.X)

                    def ln_sq2(t, e):
                        swt(e, 0, vload["x"])
                        for tt in range(4):
                            t.activation(xn_tm[:, tt * D:(tt + 1) * D],
                                         x_tm[:, tt * D:(tt + 1) * D], AF.Square,
                                         accum_out=stats[:, 4 + tt:5 + tt])

                    st.step(vector=ln_sums, scalar=ln_sq2)

                    def ln_mv(t, e):
                        t.tensor_scalar_mul(stats[:, 8:12], stats[:, 0:4], 1.0 / D)
                        t.tensor_scalar_mul(stats[:, 4:8], stats[:, 4:8], 1.0 / D)

                    st.step(vector=ln_mv)

                    def ln_m2(t, e):
                        t.tensor_mul(stats[:, 12:16], stats[:, 8:12], stats[:, 8:12])

                    st.step(vector=ln_m2)

                    def ln_var(t, e):
                        t.tensor_sub(stats[:, 4:8], stats[:, 4:8], stats[:, 12:16])

                    st.step(vector=ln_var)

                    def ln_eps(t, e):
                        t.tensor_scalar_add(stats[:, 4:8], stats[:, 4:8], LN_EPS)

                    st.step(vector=ln_eps)

                    def ln_rc(t, e):
                        t.reciprocal(stats[:, 12:16], stats[:, 4:8])

                    st.step(vector=ln_rc)

                    def ln_sq(t, e):
                        t.activation(stats[:, 12:16], stats[:, 12:16], AF.Sqrt)

                    st.step(scalar=ln_sq)

                    def ln_norm(t, e):
                        for tt in range(4):
                            t.tensor_scalar(xn_tm[:, tt * D:(tt + 1) * D], x_tm[:, tt * D:(tt + 1) * D],
                                            stats[:, 8 + tt:9 + tt], stats[:, 12 + tt:13 + tt],
                                            ALU.subtract, ALU.mult)

                    st.step(vector=ln_norm)

                    for grp in range(4):
                        def tr(t, e, grp=grp):
                            for k in range(4):
                                fc = grp * 4 + k
                                pt = (pT0, pT1)[k // 2]
                                for tt in range(4):
                                    mmi = e.transpose(
                                        pt[:, (k % 2) * 512 + tt * 128: (k % 2) * 512 + (tt + 1) * 128],
                                        xn_tm[:, tt * D + fc * 128: tt * D + (fc + 1) * 128],
                                        identb[:])
                            mmi.then_inc(sem, 1)
                            ctr[0] += 1

                        def ev(t, e, grp=grp):
                            swt(e, 3, vspill.get("xn"))
                            t.tensor_copy(xn_fm[:, (grp * 4) * TC:(grp * 4 + 2) * TC], pT0[:])
                            t.tensor_copy(xn_fm[:, (grp * 4 + 2) * TC:(grp * 4 + 4) * TC], pT1[:])

                        st.step(tensor=tr)
                        st.step(vector=ev)

                    def sp_xn(t, e, c=c):
                        vspill["xn"] = sput(e, 3, XN1.ap()[:, :, c * TC:(c + 1) * TC]
                                            .rearrange("fc p t -> p fc t"),
                                            xn_fm.ap().rearrange("p (fc t) -> p fc t", fc=16))

                    st.step(sync=sp_xn)

                    for name, wsb, nco, dst, bcol in (("k", wk_sb, 4, k_fm, BK),
                                                      ("v", wv_sb, 4, v_fm, BV)):
                        for oc in range(nco):
                            pp = (pP0, pP1)[oc % 2]

                            def mm(t, e, wsb=wsb, oc=oc, pp=pp):
                                t.gwait(vload.get("kv"))
                                for kc in range(16):
                                    mmi = e.matmul(
                                        pp[:], wsb[:, kc * 512 + oc * 128: kc * 512 + (oc + 1) * 128],
                                        xn_fm[:, kc * TC:(kc + 1) * TC],
                                        start=(kc == 0), stop=(kc == 15))
                                mmi.then_inc(sem, 1)
                                ctr[0] += 1

                            def ev(t, e, oc=oc, pp=pp, dst=dst, bcol=bcol):
                                t.activation(dst[:, oc * TC:(oc + 1) * TC], pp[:], AF.Identity,
                                             bias=bias[:, bcol + oc:bcol + oc + 1])

                            st.step(tensor=mm)
                            st.step(scalar=ev)

                    for oc in range(16):
                        sl = wmb_sl[:, (oc % 2) * 2048:(oc % 2 + 1) * 2048]
                        pp = (pP0, pP1)[oc % 2]

                        def mm(t, e, sl=sl, pp=pp, oc=oc):
                            swt(e, 1 + (oc % 2), vload["wmb" + str(oc)] if oc else vload["wmb0"])
                            for kc in range(16):
                                mmi = e.matmul(pp[:], sl[:, kc * 128:(kc + 1) * 128],
                                                      xn_fm[:, kc * TC:(kc + 1) * TC],
                                                      start=(kc == 0), stop=(kc == 15))
                            mmi.then_inc(sem, 1)
                            ctr[0] += 1

                        def pf(t, e, oc=oc):
                            if oc + 1 < 16:
                                vload["wmb" + str(oc + 1)] = sput(
                                    e, 1 + ((oc + 1) % 2),
                                    wmb_sl.ap()[:, ((oc + 1) % 2) * 2048:((oc + 1) % 2 + 1) * 2048],
                                    CWMB.ap()[(oc + 1) * 128:(oc + 2) * 128, :])

                        def ev(t, e, oc=oc, pp=pp):
                            t.activation(mb_fm[:, oc * TC:(oc + 1) * TC], pp[:], AF.Sigmoid,
                                         bias=bias[:, BMB + oc:BMB + oc + 1])

                        st.step(tensor=mm, sync=pf)
                        st.step(scalar=ev)

                    for gi in range(4):
                        def ktr(t, e, gi=gi):
                            for tt in range(4):
                                mmi = e.transpose(
                                    pSm[:, tt * 128:(tt + 1) * 128],
                                    k_fm[:, gi * TC + tt * 128: gi * TC + (tt + 1) * 128],
                                    ident[:])
                            mmi.then_inc(sem, 1)
                            ctr[0] += 1

                        def kev(t, e):
                            t.tensor_copy(k_tm[:], pSm[:])

                        st.step(tensor=ktr)
                        st.step(vector=kev)

                        def relu12(t, e):
                            t.activation(xx.ap().rearrange("p (tt cc) -> p tt cc", tt=4)[:, :, 3:131],
                                         k_tm.ap().rearrange("p (tt cc) -> p tt cc", tt=4), AF.Relu)
                            t.activation(xx.ap().rearrange("p (tt cc) -> p tt cc", tt=4)[:, :, 131:259],
                                         k_tm.ap().rearrange("p (tt cc) -> p tt cc", tt=4), AF.Relu,
                                         scale=-1.0)

                        st.step(scalar=relu12)

                        def wrap(t, e):
                            t.tensor_copy(xx.ap().rearrange("p (tt cc) -> p tt cc", tt=4)[:, :, 0:3],
                                          xx.ap().rearrange("p (tt cc) -> p tt cc", tt=4)[:, :, 256:259])

                        st.step(vector=wrap)

                        def dp(t, e):
                            for tt in range(4):
                                xo = tt * 260
                                for jj, j in enumerate((1, 2, 3)):
                                    for cc in range(2):
                                        fo = tt * 768 + (jj * 2 + cc) * 128
                                        t.tensor_mul(
                                            u_tm[:, fo:fo + 128],
                                            xx[:, xo + 3 + cc * 128: xo + 3 + (cc + 1) * 128],
                                            xx[:, xo + 3 + cc * 128 - j: xo + 3 + (cc + 1) * 128 - j])

                        st.step(vector=dp)

                        def ssq(t, e):
                            for tt in range(4):
                                t.activation(u_fm[:, tt * 768:(tt + 1) * 768],
                                             u_tm[:, tt * 768:(tt + 1) * 768],
                                             AF.Square, accum_out=ss6[:, tt * 8 + 6:tt * 8 + 7])

                        st.step(scalar=ssq)

                        def sstr(t, e):
                            for tt in range(4):
                                mmi = e.transpose(pSm[0:1, tt * 128:(tt + 1) * 128],
                                                         ss6[:, tt * 8 + 6:tt * 8 + 7], ident[:])
                            mmi.then_inc(sem, 1)
                            ctr[0] += 1

                        def ssev(t, e):
                            t.tensor_copy(R(0), pSm[0:1, 0:512])

                        st.step(tensor=sstr)
                        st.step(vector=ssev)

                        def nrow(t, e):
                            t.activation(R(1), R(0), AF.Sqrt)

                        st.step(scalar=nrow)

                        def ninv1(t, e):
                            t.tensor_scalar_max(R(1), R(1), 1e-12)

                        st.step(vector=ninv1)

                        def ninv2(t, e):
                            t.reciprocal(R(2), R(1))

                        st.step(vector=ninv2)

                        for grp in range(2):
                            nfc = 4 if grp == 0 else 2

                            def utr(t, e, grp=grp, nfc=nfc):
                                for k in range(nfc):
                                    fc = grp * 4 + k
                                    pt = (pT0, pT1)[k // 2]
                                    for tt in range(4):
                                        mmi = e.transpose(
                                            pt[:, (k % 2) * 512 + tt * 128: (k % 2) * 512 + (tt + 1) * 128],
                                            u_tm[:, tt * 768 + fc * 128: tt * 768 + (fc + 1) * 128],
                                            identb[:])
                                mmi.then_inc(sem, 1)
                                ctr[0] += 1

                            def uev(t, e, grp=grp, nfc=nfc):
                                t.tensor_copy(u_fm[:, (grp * 4) * TC:(grp * 4 + 2) * TC], pT0[:])
                                if nfc == 4:
                                    t.tensor_copy(u_fm[:, (grp * 4 + 2) * TC:(grp * 4 + 4) * TC], pT1[:])

                            st.step(tensor=utr)
                            st.step(vector=uev)

                        def dnm(t, e, gi=gi):
                            for fc in range(6):
                                mmi = e.matmul(
                                    pSm[0:4, :], normsb[:, fc * 16 + gi * 4: fc * 16 + gi * 4 + 4],
                                    u_fm[:, fc * TC:(fc + 1) * TC],
                                    start=(fc == 0), stop=(fc == 5))
                            mmi.then_inc(sem, 1)
                            ctr[0] += 1

                        def dnmev(t, e):
                            t.tensor_copy(R4(4), pSm[0:4, :])

                        st.step(tensor=dnm)
                        st.step(vector=dnmev)

                        def bcn(t, e):
                            t.matmul(pBC[0:4, :], ones[0:1, 0:4], R(2), start=True, stop=True)

                        st.step(tensor=bcn)

                        def den1(t, e):
                            t.tensor_mul(R4(5), R4(4), pBC[0:4, :])

                        st.step(vector=den1)

                        def den2(t, e):
                            t.tensor_scalar_add(R4(5), R4(5), EPS_A)

                        st.step(vector=den2)

                        def den3(t, e):
                            t.reciprocal(R4(6), R4(5))

                        st.step(vector=den3)

                        def aa1(t, e):
                            t.tensor_mul(R4(6), R4(6), pBC[0:4, :])

                        st.step(vector=aa1)

                        def aa2(t, e):
                            t.tensor_mul(R4(6), R4(6), pBC[0:4, :])

                        st.step(vector=aa2)

                        def cf2(t, e):
                            t.tensor_scalar(R4(7), R4(5), 1.0, -1.0, ALU.subtract, ALU.mult)

                        st.step(vector=cf2)

                        def cf3(t, e):
                            t.tensor_scalar(R4(7), R4(7), 0.0, 1.0, ALU.max, ALU.min)

                        st.step(vector=cf3)

                        def bcn2(t, e):
                            t.matmul(pBC[0:4, :], ones[0:1, 0:4], R(2), start=True, stop=True)

                        st.step(tensor=bcn2)

                        def cp1(t, e):
                            t.tensor_mul(R4(7), R4(7), pBC[0:4, :])

                        st.step(vector=cp1)

                        def cptr(t, e):
                            for tt in range(4):
                                mmi = e.transpose(pSm[:, 448 + tt * 4:448 + (tt + 1) * 4],
                                                  R4(7)[:, tt * 128:(tt + 1) * 128], ident[0:4, 0:4])
                            mmi.then_inc(sem, 1)
                            ctr[0] += 1

                        def cpev(t, e):
                            for tt in range(4):
                                t.tensor_copy(rtm[:, tt * 520 + 512: tt * 520 + 516],
                                              pSm[:, 448 + tt * 4:448 + (tt + 1) * 4])

                        st.step(tensor=cptr)
                        st.step(vector=cpev)

                        def bcv(t, e):
                            t.matmul(pBC[:], ones[0:1, :], R(2), start=True, stop=True)

                        st.step(tensor=bcv)

                        def vn(t, e, gi=gi):
                            t.tensor_mul(vninv[:], v_fm[:, gi * TC:(gi + 1) * TC], pBC[:])

                        st.step(vector=vn)

                        for r in range(4):
                            h = gi * 4 + r
                            pn = (pN0, pN1)[r % 2]

                            def nmm(t, e, h=h, pn=pn):
                                for fc in range(6):
                                    mmi = e.matmul(pn[:], mem_tile(h, fc),
                                                          u_fm[:, fc * TC:(fc + 1) * TC],
                                                          start=(fc == 0), stop=(fc == 5))
                                mmi.then_inc(sem, 1)
                                ctr[0] += 1

                            def nev(t, e, r=r, pn=pn):
                                t.copy(num_sb[:], pn[:])

                            st.step(tensor=nmm)
                            st.step(scalar=nev)

                            def bca(t, e, r=r):
                                t.matmul(pBC[:], oh4[:, r * 128:(r + 1) * 128], R4(6),
                                         start=True, stop=True)

                            st.step(tensor=bca)

                            def pp1(t, e, r=r):
                                t.tensor_mul(p1[:], num_sb[:], pBC[:])

                            st.step(vector=pp1)

                            def pp3(t, e, r=r):
                                t.tensor_sub(p1[:], vninv[:], p1[:])

                            st.step(vector=pp3)

                            def wm(t, e, h=h, r=r):
                                t.tensor_mul(wmv_fm[:], p1[:], mb_fm[:, h * TC:(h + 1) * TC])

                            st.step(vector=wm)

                            pt = (pT0, pT1)[r % 2]

                            def wtr(t, e, r=r, pt=pt):
                                for tt in range(4):
                                    mmi = e.transpose(
                                        pt[:, tt * 128:(tt + 1) * 128],
                                        wmv_fm[:, tt * 128:(tt + 1) * 128],
                                        identb[:])
                                mmi.then_inc(sem, 1)
                                ctr[0] += 1

                            def wev(t, e, r=r, pt=pt):
                                for tt in range(4):
                                    t.tensor_copy(rtm[:, tt * 520 + r * 128: tt * 520 + (r + 1) * 128],
                                                  pt[:, tt * 128:(tt + 1) * 128])

                            st.step(tensor=wtr)
                            st.step(vector=wev)

                        for fc in range(6):
                            def dmm(t, e, fc=fc):
                                for tt in range(4):
                                    lt = u_tm[:, tt * 768 + fc * 128: tt * 768 + (fc + 1) * 128]
                                    mmi = e.matmul(pP0[:], lt, rtm[:, tt * 520: tt * 520 + 512],
                                                          start=(tt == 0), stop=(tt == 3))
                                for tt in range(4):
                                    lt = u_tm[:, tt * 768 + fc * 128: tt * 768 + (fc + 1) * 128]
                                    mmi = e.matmul(pP1[:, 0:4], lt,
                                                          rtm[:, tt * 520 + 512: tt * 520 + 516],
                                                          start=(tt == 0), stop=(tt == 3))
                                mmi.then_inc(sem, 1)
                                ctr[0] += 1

                            i6 = gi * 6 + fc

                            def dev(t, e, i6=i6):
                                o = (i6 % 2) * 516
                                swt(e, 4 + (i6 % 2), vspill.get("dlast" + str(i6 % 2)))
                                t.tensor_copy(astg[:, o:o + 512], pP0[:])
                                t.tensor_copy(astg[:, o + 512:o + 516], pP1[:, 0:4])

                            def dwr(t, e, i6=i6, c=c):
                                o = (i6 % 2) * 516
                                vspill["dlast" + str(i6 % 2)] = sput(e, 4 + (i6 % 2),
                                                                     ACC8.ap()[c, i6, :, :],
                                                                     astg[:, o:o + 516])

                            st.step(tensor=dmm)
                            st.step(vector=dev)
                            st.step(sync=dwr)

    if "MID" in phases:
        with ExitStack() as ph:
            s0 = ph.enter_context(nc.sbuf_tensor("mid_s0", [128, 24 * 516], F32))
            s1 = ph.enter_context(nc.sbuf_tensor("mid_s1", [128, 24 * 516], F32))
            with nc.Block() as blk:
                st = Stepper(blk, sem, ctr, dsem, dctr)

                def l0(t, e):
                    drain_all(e)
                    t.dma_start(out=s0.ap().rearrange("p (i cc2) -> p i cc2", i=24),
                                in_=ACC8.ap()[0].rearrange("i p cc2 -> p i cc2"), inc=16)

                st.step(sync=l0)
                for c in range(1, 8):
                    def lc(t, e, c=c):
                        t.dma_start(out=s1.ap().rearrange("p (i cc2) -> p i cc2", i=24),
                                    in_=ACC8.ap()[c].rearrange("i p cc2 -> p i cc2"), inc=16)

                    st.step(scalar=lc)

                    def ac(t, e):
                        t.tensor_add(s0[:], s0[:], s1[:])

                    st.step(vector=ac)

                def sv(t, e):
                    t.dma_start(out=ACC.ap().rearrange("i p cc2 -> p i cc2"),
                                in_=s0.ap().rearrange("p (i cc2) -> p i cc2", i=24), inc=16)

                st.step(sync=sv)

        with nc.Block() as blk:
            st = Stepper(blk, sem, ctr, dsem, dctr)

            def cc(t, e):
                t.collective_compute(
                    "AllReduce", ALU.add,
                    replica_groups=[list(range(NCORE))],
                    ins=[ACC.ap().opt()], outs=[ACC_R.ap().opt()])

            st.step(gpsimd=cc)

        with ExitStack() as ph:
            accsb = ph.enter_context(nc.sbuf_tensor("mid_acc", [128, 24 * 516], F32))
            memf = ph.enter_context(nc.sbuf_tensor("mid_memf", [128, 16 * 6 * 128], F32))
            normf = ph.enter_context(nc.sbuf_tensor("mid_normf", [128, 96], F32))
            with nc.Block() as blk:
                st = Stepper(blk, sem, ctr, dsem, dctr)

                def ld(t, e):
                    t.dma_start(out=accsb.ap().rearrange("p (i cc) -> p i cc", i=24),
                                in_=ACC_R.ap().rearrange("i p cc -> p i cc"), inc=16)
                    t.dma_start(out=memf.ap().rearrange("p (h fc d) -> p h fc d", h=16, fc=6),
                                in_=g["mem"].ap().rearrange("h (fc p) d -> p h fc d", p=128), inc=16)
                    t.dma_start(out=normf.ap().rearrange("p (fc h) -> p fc h", fc=6),
                                in_=g["normT"].ap().rearrange("(fc p) h -> p fc h", p=128), inc=16)

                st.step(sync=ld)

                def addm(t, e):
                    for gi in range(4):
                        for fc in range(6):
                            i = gi * 6 + fc
                            memv = memf.ap().rearrange("p (h f2 d) -> p h f2 d", h=16, f2=6)[:, 4 * gi:4 * gi + 4, fc, :]
                            outv = memsb.ap().rearrange("p (h f2 d) -> p h f2 d", h=16, f2=6)[:, 4 * gi:4 * gi + 4, fc, :]
                            accv = accsb.ap()[:, i * 516:i * 516 + 512].rearrange("p (r d) -> p r d", r=4)
                            t.tensor_add(outv, memv, accv)
                    for gi in range(4):
                        accn = accsb.ap().rearrange("p (i cc) -> p i cc", i=24)[:, 6 * gi:6 * gi + 6, 512:516]
                        normv = normf.ap().rearrange("p (fc h) -> p fc h", fc=6)[:, :, 4 * gi:4 * gi + 4]
                        outn = normsb.ap().rearrange("p (fc h) -> p fc h", fc=6)[:, :, 4 * gi:4 * gi + 4]
                        t.tensor_add(outn, normv, accn)

                st.step(vector=addm)

    if "A2" in phases:
        with ExitStack() as ph:
            x_tm = ph.enter_context(nc.sbuf_tensor("a2_xtm", [128, 4 * D], F32))
            x_fm = ph.enter_context(nc.sbuf_tensor("a2_xfm", [128, 16 * TC], F32))
            xn1 = ph.enter_context(nc.sbuf_tensor("a2_xn1", [128, 16 * TC], BF16))
            scr2k = ph.enter_context(nc.sbuf_tensor("a2_scr", [128, 2 * D], F32))
            wq_sl = ph.enter_context(nc.sbuf_tensor("a2_wq", [128, 2 * 2048], BF16))
            wg_sl = ph.enter_context(nc.sbuf_tensor("a2_wg", [128, 2 * 2048], BF16))
            q_fm = ph.enter_context(nc.sbuf_tensor("a2_qfm", [128, TC], F32))
            xx = ph.enter_context(nc.sbuf_tensor("a2_xx", [128, 4 * 260], F32))
            u_tm = ph.enter_context(nc.sbuf_tensor("a2_utm", [128, 2 * 4 * 768], BF16))
            u_fm = ph.enter_context(nc.sbuf_tensor("a2_ufm", [128, 6 * TC], BF16))
            ss6 = ph.enter_context(nc.sbuf_tensor("a2_ss6", [128, 4 * 8], F32))
            num_sb = ph.enter_context(nc.sbuf_tensor("a2_num", [128, 2 * TC], F32))
            gate_all = ph.enter_context(nc.sbuf_tensor("a2_gate", [128, 16 * TC], BF16))
            o_sb = ph.enter_context(nc.sbuf_tensor("a2_o", [128, TC], F32))

            pF0 = ph.enter_context(nc.psum_tensor("a2_pF0", [128, 512], F32))
            pF1 = ph.enter_context(nc.psum_tensor("a2_pF1", [128, 512], F32))
            pB0 = ph.enter_context(nc.psum_tensor("a2_pB0", [128, 1024], BF16))
            pB1 = ph.enter_context(nc.psum_tensor("a2_pB1", [128, 1024], BF16))
            pN0 = ph.enter_context(nc.psum_tensor("a2_pN0", [128, 512], F32))
            pB2 = ph.enter_context(nc.psum_tensor("a2_pB2", [128, 1024], BF16))
            pSm = ph.enter_context(nc.psum_tensor("a2_pSm", [128, 512], F32))
            pBC = ph.enter_context(nc.psum_tensor("a2_pBC", [128, 512], F32))

            def R(r):
                return scr2k[0:1, r * 512:(r + 1) * 512]

            va2 = {}
            for c in range(S // TC):
                with nc.Block() as blk:
                    st = Stepper(blk, sem, ctr, dsem, dctr)

                    def ld_x(t, e, c=c):
                        if c == 0:
                            drain_all(e)
                        va2["x"] = sput(e, 0, x_tm.ap().rearrange("p (tt d) -> p tt d", tt=4),
                                        g["x"].ap()[c * TC:(c + 1) * TC, :]
                                        .rearrange("(tt p) d -> p tt d", p=128))
                        va2["xn1"] = sput(e, 6, xn1.ap().rearrange("p (fc t2) -> p fc t2", fc=16),
                                          XN1.ap()[:, :, c * TC:(c + 1) * TC]
                                          .rearrange("fc p t2 -> p fc t2"))
                        va2["wg0"] = sput(e, 7, wg_sl.ap()[:, 0:2048], CWGT.ap()[0:128, :])

                    st.step(sync=ld_x)

                    XTR = {}
                    XEV = {}
                    for f in range(17):
                        def evf(t, e, f=f):
                            if f >= 1:
                                pt = (pF0, pSm)[(f - 1) % 2]
                                inst = e.tensor_copy(x_fm[:, (f - 1) * TC:f * TC], pt[:])
                                XEV[f - 1] = cinc(inst)

                        def trf(t, e, f=f):
                            if f < 16:
                                if f == 0:
                                    swt(e, 0, va2["x"])
                                if f >= 2:
                                    e.wait_ge(fsem, XEV[f - 2])
                                pt = (pF0, pSm)[f % 2]
                                for tt in range(4):
                                    mmi = e.transpose(pt[:, tt * 128:(tt + 1) * 128],
                                                      x_tm[:, tt * D + f * 128: tt * D + (f + 1) * 128],
                                                      ident[:])
                                mmi.then_inc(sem, 1)
                                ctr[0] += 1

                        st.step(vector=evf, tensor=trf)

                    GEV = {}
                    for h2 in range(16):
                        def gmm(t, e, h2=h2):
                            swt(e, 7 + (h2 % 2), va2["wg" + str(h2)])
                            if h2 >= 2:
                                e.wait_ge(fsem, GEV[h2 - 2])
                            elif h2 == 1:
                                e.wait_ge(fsem, XEV[15])
                            if h2 == 0:
                                swt(e, 6, va2["xn1"])
                            gsl = wg_sl[:, (h2 % 2) * 2048:(h2 % 2 + 1) * 2048]
                            pg = (pN0, pSm)[h2 % 2]
                            for kc in range(16):
                                mmi = e.matmul(pg[:], gsl[:, kc * 128:(kc + 1) * 128],
                                               xn1[:, kc * TC:(kc + 1) * TC],
                                               start=(kc == 0), stop=(kc == 15))
                            mmi.then_inc(sem, 1)
                            ctr[0] += 1

                        def gpf(t, e, h2=h2):
                            if h2 + 1 < 16:
                                va2["wg" + str(h2 + 1)] = sput(
                                    e, 7 + ((h2 + 1) % 2),
                                    wg_sl.ap()[:, ((h2 + 1) % 2) * 2048:((h2 + 1) % 2 + 1) * 2048],
                                    CWGT.ap()[(h2 + 1) * 128:(h2 + 2) * 128, :])
                            if h2 == 15:
                                va2["wq0"] = sput(e, 9, wq_sl.ap()[:, 0:2048], CWQ.ap()[0:128, :])

                        def gevf(t, e, h2=h2):
                            if h2 >= 1:
                                pg = (pN0, pSm)[(h2 - 1) % 2]
                                inst = e.activation(gate_all[:, (h2 - 1) * TC:h2 * TC], pg[:],
                                                    AF.Sigmoid, bias=bias[:, BG + h2 - 1:BG + h2])
                                GEV[h2 - 1] = cinc(inst)

                        st.step(scalar=gevf, tensor=gmm, sync=gpf)

                    QEV = {}
                    RELU = {}
                    DPV = {}
                    SSQ = {}
                    SSEV = {}
                    UEV = {}
                    NEV = {}
                    DEN4 = {}
                    MH1 = {}
                    MH2 = {}
                    MH4 = {}

                    def utmr(h):
                        ub = (h % 2) * 3072
                        return u_tm.ap()[:, ub:ub + 3072].rearrange("p (tt f) -> p tt f", tt=4)

                    for k in range(39):
                        if k % 2 == 0:
                            def scl(t, e, k=k):
                                if k == 0:
                                    pg = pSm
                                    inst = e.activation(gate_all[:, 15 * TC:16 * TC], pg[:],
                                                        AF.Sigmoid, bias=bias[:, BG + 15:BG + 16])
                                    GEV[15] = cinc(inst)
                                h = (k - 2) // 2
                                if 0 <= h < 16:
                                    if h >= 1:
                                        e.wait_ge(fsem, DPV[h - 1])
                                    pr = pSm.ap().rearrange("p (tt cc) -> p tt cc", tt=4)
                                    xr = xx.ap().rearrange("p (tt cc) -> p tt cc", tt=4)
                                    e.activation(xr[:, :, 3:131], pr, AF.Relu)
                                    inst = e.activation(xr[:, :, 131:259], pr, AF.Relu, scale=-1.0)
                                    RELU[h] = cinc(inst)
                                h = (k - 4) // 2
                                if 0 <= h < 16:
                                    e.wait_ge(fsem, DPV[h])
                                    if h >= 1:
                                        e.wait_ge(fsem, UEV[h - 1])
                                    ub = (h % 2) * 3072
                                    for tt in range(4):
                                        inst = e.activation(u_fm[:, tt * 768:(tt + 1) * 768],
                                                            u_tm[:, ub + tt * 768:ub + (tt + 1) * 768],
                                                            AF.Square,
                                                            accum_out=ss6[:, tt * 8 + 6:tt * 8 + 7])
                                    SSQ[h] = cinc(inst)
                                h = (k - 6) // 2
                                if 0 <= h < 16:
                                    e.wait_ge(fsem, SSEV[h])
                                    inst = e.activation(R(1), R(0), AF.Sqrt)
                                    cinc(inst)
                                    if h >= 2:
                                        e.wait_ge(fsem, MH2[h - 2])
                                    inst = e.copy(num_sb[:, (h % 2) * TC:(h % 2 + 1) * TC], pF0[:])
                                    NEV[h] = cinc(inst)

                            def vec(t, e, k=k):
                                h = (k - 2) // 2
                                if 0 <= h < 16:
                                    e.wait_ge(fsem, RELU[h])
                                    xr = xx.ap().rearrange("p (tt cc) -> p tt cc", tt=4)
                                    inst = e.tensor_copy(xr[:, :, 0:3], xr[:, :, 256:259])
                                    w0 = cinc(inst)
                                    e.wait_ge(fsem, w0)
                                    ut = utmr(h)
                                    for jj, jv in enumerate((1, 2, 3)):
                                        for cc in range(2):
                                            inst = e.tensor_tensor(
                                                ut[:, :, (jj * 2 + cc) * 128:(jj * 2 + cc + 1) * 128],
                                                xx.ap().rearrange("p (tt cc) -> p tt cc", tt=4)
                                                [:, :, 3 + cc * 128: 3 + (cc + 1) * 128],
                                                xx.ap().rearrange("p (tt cc) -> p tt cc", tt=4)
                                                [:, :, 3 + cc * 128 - jv: 3 + (cc + 1) * 128 - jv],
                                                ALU.mult)
                                    DPV[h] = cinc(inst)
                                h = (k - 6) // 2
                                if 0 <= h < 16:
                                    e.wait_ge(fsem, ftc[0])
                                    inst = e.tensor_scalar_max(R(1), R(1), 1e-12)
                                    inst = e.reciprocal(R(2), R(1))
                                    cinc(inst)
                                    e.wait_ge(fsem, ftc[0])
                                    inst = e.tensor_mul(R(3), pF1[0:1, :], R(2))
                                    inst = e.tensor_scalar_add(R(3), R(3), EPS_A)
                                    cinc(inst)
                                    e.wait_ge(fsem, ftc[0])
                                    inst = e.reciprocal(R(3), R(3))
                                    cinc(inst)
                                    e.wait_ge(fsem, ftc[0])
                                    inst = e.tensor_mul(R(3), R(3), R(2))
                                    DEN4[h] = cinc(inst)
                                h = (k - 8) // 2
                                if 0 <= h < 16:
                                    e.wait_ge(fsem, NEV[h])
                                    ns = num_sb[:, (h % 2) * TC:(h % 2 + 1) * TC]
                                    inst = e.tensor_mul(ns, ns, pBC[:])
                                    MH1[h] = cinc(inst)
                                    swt(e, 11, va2.get("o" + str((h - 1) % 16)))
                                    e.wait_ge(fsem, MH1[h])
                                    inst = e.tensor_mul(o_sb[:], gate_all[:, h * TC:(h + 1) * TC], ns)
                                    MH2[h] = cinc(inst)
                                    e.wait_ge(fsem, MH2[h])
                                    inst = e.tensor_add(o_sb[:], o_sb[:], x_fm[:, h * TC:(h + 1) * TC])
                                    cinc(inst)
                                    e.wait_ge(fsem, ftc[0])
                                    inst = e.tensor_add(o_sb[:], o_sb[:], xn1[:, h * TC:(h + 1) * TC])
                                    MH4[h] = cinc(inst)

                            def ten(t, e, k=k):
                                h = k // 2
                                if h < 16:
                                    swt(e, 9 + (h % 2), va2["wq" + str(h)])
                                    if h == 0:
                                        e.wait_ge(fsem, GEV[14])
                                    else:
                                        e.wait_ge(fsem, QEV[h - 1])
                                    qsl = wq_sl[:, (h % 2) * 2048:(h % 2 + 1) * 2048]
                                    for kc in range(16):
                                        mmi = e.matmul(pN0[:], qsl[:, kc * 128:(kc + 1) * 128],
                                                       xn1[:, kc * TC:(kc + 1) * TC],
                                                       start=(kc == 0), stop=(kc == 15))
                                    mmi.then_inc(sem, 1)
                                    ctr[0] += 1
                                h = (k - 4) // 2
                                if 0 <= h < 16:
                                    e.wait_ge(fsem, SSQ[h])
                                    if h >= 1:
                                        e.wait_ge(fsem, DEN4[h - 1])
                                    for tt in range(4):
                                        mmi = e.transpose(pF1[0:1, tt * 128:(tt + 1) * 128],
                                                          ss6[:, tt * 8 + 6:tt * 8 + 7], ident[:])
                                    mmi.then_inc(sem, 1)
                                    ctr[0] += 1
                                    if h >= 1:
                                        e.wait_ge(fsem, UEV[h - 1])
                                    ub = (h % 2) * 3072
                                    for fc in range(6):
                                        pt = (pB0, pB0, pB1, pB1, pB2, pB2)[fc]
                                        for tt in range(4):
                                            mmi = e.transpose(
                                                pt[:, (fc % 2) * 512 + tt * 128:(fc % 2) * 512 + (tt + 1) * 128],
                                                u_tm[:, ub + tt * 768 + fc * 128: ub + tt * 768 + (fc + 1) * 128],
                                                identb[:])
                                    mmi.then_inc(sem, 1)
                                    ctr[0] += 1

                            def syn(t, e, k=k):
                                h = k // 2
                                if h + 1 < 16:
                                    va2["wq" + str(h + 1)] = sput(
                                        e, 9 + ((h + 1) % 2),
                                        wq_sl.ap()[:, ((h + 1) % 2) * 2048:((h + 1) % 2 + 1) * 2048],
                                        CWQ.ap()[(h + 1) * 128:(h + 2) * 128, :])
                                h = (k - 8) // 2
                                if 0 <= h < 16:
                                    e.wait_ge(fsem, MH4[h])
                                    va2["o" + str(h)] = sput(e, 11, XR1.ap()[h, :, c * TC:(c + 1) * TC],
                                                             o_sb[:])

                            st.step(scalar=scl, vector=vec, tensor=ten, sync=syn)
                        else:
                            def scl(t, e, k=k):
                                h = (k - 1) // 2
                                if h < 16:
                                    inst = e.activation(q_fm[:], pN0[:], AF.Identity,
                                                        bias=bias[:, BQ + h:BQ + h + 1])
                                    QEV[h] = cinc(inst)

                            def vec(t, e, k=k):
                                h = (k - 5) // 2
                                if 0 <= h < 16:
                                    e.wait_ge(fsem, ftc[0])
                                    inst = e.tensor_copy(R(0), pF1[0:1, 0:512])
                                    SSEV[h] = cinc(inst)
                                    inst = e.tensor_copy(u_fm[:, 0:2 * TC], pB0[:])
                                    inst = e.tensor_copy(u_fm[:, 2 * TC:4 * TC], pB1[:])
                                    inst = e.tensor_copy(u_fm[:, 4 * TC:6 * TC], pB2[:])
                                    UEV[h] = cinc(inst)

                            def ten(t, e, k=k):
                                h = (k - 1) // 2
                                if h < 16:
                                    e.wait_ge(fsem, QEV[h])
                                    if h >= 1:
                                        e.wait_ge(fsem, RELU[h - 1])
                                    else:
                                        e.wait_ge(fsem, GEV[15])
                                    for tt in range(4):
                                        mmi = e.transpose(pSm[:, tt * 128:(tt + 1) * 128],
                                                          q_fm[:, tt * 128:(tt + 1) * 128], ident[:])
                                    mmi.then_inc(sem, 1)
                                    ctr[0] += 1
                                h = (k - 5) // 2
                                if 0 <= h < 16:
                                    e.wait_ge(fsem, UEV[h])
                                    if h >= 1:
                                        e.wait_ge(fsem, NEV[h - 1])
                                    elif h == 0:
                                        e.wait_ge(fsem, XEV[14])
                                    e.wait_ge(fsem, SSEV[h])
                                    for fc in range(6):
                                        mmi = e.matmul(pF0[:], mem_tile(h, fc),
                                                       u_fm[:, fc * TC:(fc + 1) * TC],
                                                       start=(fc == 0), stop=(fc == 5))
                                    for fc in range(6):
                                        mmi = e.matmul(pF1[0:1, :], normsb[:, fc * 16 + h: fc * 16 + h + 1],
                                                       u_fm[:, fc * TC:(fc + 1) * TC],
                                                       start=(fc == 0), stop=(fc == 5))
                                    mmi.then_inc(sem, 1)
                                    ctr[0] += 1
                                h = (k - 7) // 2
                                if 0 <= h < 16:
                                    e.wait_ge(fsem, DEN4[h])
                                    if h >= 1:
                                        e.wait_ge(fsem, MH1[h - 1])
                                    mmi = e.matmul(pBC[:], ones[0:1, :], R(3), start=True, stop=True)
                                    mmi.then_inc(sem, 1)
                                    ctr[0] += 1

                            st.step(scalar=scl, vector=vec, tensor=ten)

    ases.close()

    TCF = 512

    def w1buf2(w, i):
        return w[:, (i % 2) * 8192:(i % 2 + 1) * 8192]

    evMM = es.enter_context(nc.semaphore("F_evMM"))
    evMM2 = es.enter_context(nc.semaphore("F_evMM2"))
    evSilu = es.enter_context(nc.semaphore("F_evSilu"))
    evAbs = es.enter_context(nc.semaphore("F_evAbs"))
    evHm = es.enter_context(nc.semaphore("F_evHm"))
    evHS = es.enter_context(nc.semaphore("F_evHS"))
    evDev = es.enter_context(nc.semaphore("F_evDev"))
    evOtr = es.enter_context(nc.semaphore("F_evOtr"))
    evOev = es.enter_context(nc.semaphore("F_evOev"))
    svF = es.enter_context(nc.semaphore("F_svF"))
    svF_base = [0]
    vps = [0, 0]
    vps2 = [0, 0]
    f_ebase = [0]
    f_obase = [0]

    def phase_F(src, dst, pref):
        with ExitStack() as ph:
            xt0 = ph.enter_context(nc.sbuf_tensor(pref + "_xt0", [128, 512], F32))
            xt1 = ph.enter_context(nc.sbuf_tensor(pref + "_xt1", [128, 512], F32))
            x2t = ph.enter_context(nc.sbuf_tensor(pref + "_x2", [128, 512], F32))
            rowst = ph.enter_context(nc.sbuf_tensor(pref + "_rows", [1, 4 * 512], F32))
            xn = ph.enter_context(nc.sbuf_tensor(pref + "_xn", [128, 16 * TCF], BF16))
            wg1 = ph.enter_context(nc.sbuf_tensor(pref + "_wg1", [128, 2 * 16 * 512], BF16))
            wu1 = ph.enter_context(nc.sbuf_tensor(pref + "_wu1", [128, 2 * 16 * 512], BF16))
            wd1 = wg1
            sg = ph.enter_context(nc.sbuf_tensor(pref + "_sg", [128, 4 * 512], F32))
            hh = ph.enter_context(nc.sbuf_tensor(pref + "_hh", [128, 4 * 512], F32))
            gx = ph.enter_context(nc.sbuf_tensor(pref + "_gx", [128, 10 * 512], BF16))
            ab = gx[:, 6 * 512:10 * 512]
            hs = ph.enter_context(nc.sbuf_tensor(pref + "_hs", [128, 64 * TCF], BF16))
            osub = sg[:, 0:1024]
            ostg = (ph.enter_context(nc.sbuf_tensor(pref + "_ostg", [128, 2 * 512], F32))
                    if dst is None else None)

            pA = ph.enter_context(nc.psum_tensor(pref + "_pA", [128, 512], F32))
            pB = ph.enter_context(nc.psum_tensor(pref + "_pB", [128, 512], F32))
            pC = ph.enter_context(nc.psum_tensor(pref + "_pC", [128, 512], F32))
            pD = ph.enter_context(nc.psum_tensor(pref + "_pD", [128, 512], F32))
            pE = ph.enter_context(nc.psum_tensor(pref + "_pE", [128, 512], F32))
            pF_ = ph.enter_context(nc.psum_tensor(pref + "_pF", [128, 512], F32))
            pG = ph.enter_context(nc.psum_tensor(pref + "_pG", [128, 512], F32))
            pH = ph.enter_context(nc.psum_tensor(pref + "_pH", [128, 512], F32))
            pGU = [pA, pB, pC, pD, pE, pF_, pG, pH]

            def RW(r):
                return rowst[0:1, r * 512:(r + 1) * 512]


            v_fwr = [None]
            for c in range(S // TCF):
                with nc.Block() as blk:
                    st = Stepper(blk, sem, ctr, dsem, dctr)
                    vx = {}
                    vw = {}
                    vd = {}
                    vo = {}
                    vst = {}

                    def ldx0(t, e, c=c):
                        if c == 0:
                            drain_all(e)
                        vx[0] = sput(e, 12, xt0[:], src.ap()[0, :, c * TCF:(c + 1) * TCF])
                        sput(e, 14, w1buf2(wg1, 0), CWG.ap()[0:128, :])
                        vw[0] = sput(e, 14, w1buf2(wu1, 0), CWU.ap()[0:128, :])

                    st.step(sync=ldx0)

                    for fc in range(16):
                        xt = (xt0, xt1)[fc % 2]

                        def mms(t, e, fc=fc, xt=xt):
                            swt(e, 12 + (fc % 2), vx[fc])
                            t.matmul(pA[0:1, :], ones[:, 0:1], xt[:],
                                     start=(fc == 0), stop=(fc == 15))
                            if fc > 0:
                                t.matmul(pC[0:1, :], ones[:, 0:1], (x2t, osub)[(fc - 1) % 2][:, 0:512],
                                         start=(fc == 1), stop=False)

                        def sq(t, e, fc=fc, xt=xt):
                            swt(e, 12 + (fc % 2), vx[fc])
                            if fc < 2:
                                swt(e, 20, vps[0])
                                swt(e, 21, vps[1])
                            t.activation((x2t, osub)[fc % 2][:, 0:512], xt[:], AF.Square)

                        def pfx(t, e, fc=fc, c=c):
                            if fc + 1 < 16:
                                vx[fc + 1] = sput(e, 12 + ((fc + 1) % 2), (xt0, xt1)[(fc + 1) % 2][:],
                                                  src.ap()[fc + 1, :, c * TCF:(c + 1) * TCF])

                        st.step(tensor=mms, scalar=sq, sync=pfx)

                    def mmq_last(t, e):
                        t.matmul(pC[0:1, :], ones[:, 0:1], osub[:, 0:512],
                                 start=False, stop=True)

                    st.step(tensor=mmq_last)

                    def evs(t, e):
                        t.tensor_scalar_mul(RW(0), pA[0:1, :], 1.0 / D)
                        t.tensor_scalar_mul(RW(1), pC[0:1, :], 1.0 / D)

                    st.step(vector=evs)

                    def var1(t, e):
                        t.tensor_mul(RW(2), RW(0), RW(0))

                    st.step(vector=var1)

                    def var2(t, e):
                        t.tensor_sub(RW(1), RW(1), RW(2))

                    st.step(vector=var2)

                    def var3(t, e):
                        t.tensor_scalar_add(RW(1), RW(1), LN_EPS)

                    st.step(vector=var3)

                    def var4(t, e):
                        t.reciprocal(RW(2), RW(1))

                    st.step(vector=var4)

                    def var5(t, e):
                        t.activation(RW(2), RW(2), AF.Sqrt)

                    st.step(scalar=var5)

                    def var6(t, e):
                        t.tensor_mul(RW(3), RW(0), RW(2))

                    st.step(vector=var6)

                    def bcs(t, e):
                        t.matmul(pA[:], ones[0:1, :], RW(2), start=True, stop=True)
                        mmi = e.matmul(pC[:], ones[0:1, :], RW(3), start=True, stop=True)
                        mmi.then_inc(sem, 1)
                        ctr[0] += 1

                    st.step(tensor=bcs)

                    def ldn0(t, e, c=c):
                        vx[100] = sput(e, 12, xt0[:], src.ap()[0, :, c * TCF:(c + 1) * TCF])

                    st.step(sync=ldn0)

                    for fc in range(16):
                        xt = (xt0, xt1)[fc % 2]

                        def nm1(t, e, fc=fc, xt=xt):
                            swt(e, 12 + (fc % 2), vx[100 + fc])
                            t.tensor_mul((x2t, osub)[fc % 2][:, 0:512], xt[:], pA[:])

                        def pfn(t, e, fc=fc, c=c):
                            if fc + 1 < 16:
                                vx[101 + fc] = sput(e, 12 + ((fc + 1) % 2), (xt0, xt1)[(fc + 1) % 2][:],
                                                    src.ap()[fc + 1, :, c * TCF:(c + 1) * TCF])

                        st.step(vector=nm1, sync=pfn)

                        def nm2(t, e, fc=fc):
                            t.tensor_sub(xn[:, fc * TCF:(fc + 1) * TCF],
                                         (x2t, osub)[fc % 2][:, 0:512], pC[:])

                        st.step(vector=nm2)

                vout = {}
                if not ASYNC_F:
                  with nc.Block() as blk:
                    st = Stepper(blk, sem, ctr, dsem, dctr)
                    hv = {}
                    devv = {}
                    otrv = {}
                    oevv = {}

                    def tinc(inst):
                        inst.then_inc(fsem, 1)
                        ftc[0] += 1
                        return ftc[0]

                    def em_silu(e, s1):
                        p = s1 % 2
                        e.wait_ge(fsem, ftc[0])
                        for j in range(2):
                            jj = 2 * p + j
                            inst = e.activation(sg[:, jj * 512:(jj + 1) * 512],
                                                pGU[4 * p + j][:], AF.Silu)
                        return tinc(inst)

                    def em_hmul(e, s1):
                        p = s1 % 2
                        e.wait_ge(fsem, ftc[0])
                        for j in range(2):
                            jj = 2 * (s1 % 2) + j
                            inst = e.tensor_mul(hh[:, jj * 512:(jj + 1) * 512],
                                                sg[:, jj * 512:(jj + 1) * 512],
                                                pGU[4 * p + 2 + j][:])
                        hv[s1] = tinc(inst)
                        return hv[s1]

                    def em_habs(e):
                        e.wait_ge(fsem, ftc[0])
                        for jj in range(4):
                            inst = e.activation(gx[:, (6 + jj) * 512:(7 + jj) * 512],
                                                hh[:, jj * 512:(jj + 1) * 512], AF.Square)
                        return tinc(inst)

                    def em_tail24(e, gcx):
                        e.wait_ge(fsem, ftc[0])
                        pairs = [(0, 1), (0, 2), (0, 3), (1, 2), (1, 3), (2, 3)]
                        for i2, (a, b) in enumerate(pairs):
                            inst = e.tensor_tensor(gx[:, i2 * 512:(i2 + 1) * 512],
                                                   gx[:, (6 + a) * 512:(7 + a) * 512],
                                                   gx[:, (6 + b) * 512:(7 + b) * 512], ALU.is_ge)
                        tinc(inst)
                        e.wait_ge(fsem, ftc[0])
                        inst = e.tensor_add(gx[:, 6 * 512:7 * 512], gx[:, 0:512], gx[:, 512:1024])
                        inst = e.tensor_add(gx[:, 7 * 512:8 * 512], gx[:, 3 * 512:4 * 512], gx[:, 4 * 512:5 * 512])
                        inst = e.tensor_add(gx[:, 8 * 512:9 * 512], gx[:, 512:1024], gx[:, 3 * 512:4 * 512])
                        inst = e.tensor_add(gx[:, 9 * 512:10 * 512], gx[:, 2 * 512:3 * 512], gx[:, 4 * 512:5 * 512])
                        tinc(inst)
                        e.wait_ge(fsem, ftc[0])
                        inst = e.tensor_add(gx[:, 6 * 512:7 * 512], gx[:, 6 * 512:7 * 512], gx[:, 2 * 512:3 * 512])
                        inst = e.tensor_sub(gx[:, 7 * 512:8 * 512], gx[:, 0:512], gx[:, 7 * 512:8 * 512])
                        inst = e.tensor_sub(gx[:, 8 * 512:9 * 512], gx[:, 8 * 512:9 * 512], gx[:, 5 * 512:6 * 512])
                        inst = e.tensor_add(gx[:, 9 * 512:10 * 512], gx[:, 9 * 512:10 * 512], gx[:, 5 * 512:6 * 512])
                        tinc(inst)
                        e.wait_ge(fsem, ftc[0])
                        inst = e.tensor_single_scalar(gx[:, 6 * 512:7 * 512], gx[:, 6 * 512:7 * 512], 1.5, ALU.is_ge)
                        inst = e.tensor_single_scalar(gx[:, 7 * 512:8 * 512], gx[:, 7 * 512:8 * 512], -0.5, ALU.is_le)
                        inst = e.tensor_single_scalar(gx[:, 8 * 512:9 * 512], gx[:, 8 * 512:9 * 512], 0.5, ALU.is_le)
                        inst = e.tensor_single_scalar(gx[:, 9 * 512:10 * 512], gx[:, 9 * 512:10 * 512], 1.5, ALU.is_le)
                        tinc(inst)
                        e.wait_ge(fsem, ftc[0])
                        for jj in range(4):
                            inst = e.tensor_mul(hs[:, (gcx * 4 + jj) * TCF:(gcx * 4 + jj + 1) * TCF],
                                                hh[:, jj * 512:(jj + 1) * 512],
                                                gx[:, (6 + jj) * 512:(7 + jj) * 512])
                        return tinc(inst)

                    for s in range(32):
                        gc, h = s // 2, s % 2

                        def gumm(t, e, s=s, gc=gc, h=h):
                            if h == 0:
                                swt(e, 14 + (gc % 2), vw[gc])
                            if s >= 2:
                                e.wait_ge(fsem, hv[s - 2])
                            wgb = w1buf2(wg1, gc)
                            wub = w1buf2(wu1, gc)
                            p = s % 2
                            for j in range(2):
                                jj = 2 * h + j
                                for kc in range(16):
                                    e.matmul(pGU[4 * p + j][:],
                                             wgb[:, kc * 512 + jj * 128: kc * 512 + (jj + 1) * 128],
                                             xn[:, kc * TCF:(kc + 1) * TCF],
                                             start=(kc == 0), stop=(kc == 15))
                            for j in range(2):
                                jj = 2 * h + j
                                for kc in range(16):
                                    mmi = e.matmul(pGU[4 * p + 2 + j][:],
                                                   wub[:, kc * 512 + jj * 128: kc * 512 + (jj + 1) * 128],
                                                   xn[:, kc * TCF:(kc + 1) * TCF],
                                                   start=(kc == 0), stop=(kc == 15))
                            mmi.then_inc(sem, 1)
                            ctr[0] += 1

                        def pfw(t, e, gc=gc, h=h):
                            if h == 0 and gc + 1 < 16:
                                sput(e, 14 + ((gc + 1) % 2), w1buf2(wg1, gc + 1),
                                     CWG.ap()[(gc + 1) * 128:(gc + 2) * 128, :])
                                vw[gc + 1] = sput(e, 14 + ((gc + 1) % 2), w1buf2(wu1, gc + 1),
                                                  CWU.ap()[(gc + 1) * 128:(gc + 2) * 128, :])

                        def tact(t, e, s=s):
                            if s >= 1:
                                em_silu(e, s - 1)
                            if s >= 3 and s % 2 == 1:
                                em_habs(e)

                        def tdve(t, e, s=s):
                            if s >= 3 and s % 2 == 1:
                                em_tail24(e, (s - 3) // 2)
                            if s >= 1:
                                em_hmul(e, s - 1)

                        st.step(scalar=tact, vector=tdve, tensor=gumm, sync=pfw)

                    def tfa1(t, e):
                        em_silu(e, 31)

                    def tfv1(t, e):
                        em_hmul(e, 31)

                    def ldwd0(t, e, c=c):
                        vd[0] = sput(e, 16, wd1[:, 0:8192], CWD.ap()[0:128, :])
                        if dst is not None:
                            swt(e, 20, vps[0])
                            swt(e, 21, vps[1])
                        vo[0] = sput(e, 18, osub[:, 0:512], src.ap()[0, :, c * TCF:(c + 1) * TCF])

                    st.step(scalar=tfa1, vector=tfv1, sync=ldwd0)

                    def tfa2(t, e):
                        em_habs(e)

                    def tfv2(t, e):
                        em_tail24(e, 15)

                    st.step(scalar=tfa2, vector=tfv2)
                    tail_done = ftc[0]

                    for oc in range(16):
                        def dve_fn(t, e, oc=oc):
                            if oc >= 1:
                                swt(e, 18 + ((oc - 1) % 2), vo[oc - 1])
                                e.wait_ge(fsem, ftc[0])
                                ob = osub[:, ((oc - 1) % 2) * 512:((oc - 1) % 2 + 1) * 512]
                                inst = e.tensor_add(ob, ob, (pE, pF_)[(oc - 1) % 2][:])
                                devv[oc - 1] = tinc(inst)
                            if dst is None and oc >= 2:
                                swt(e, 20 + ((oc - 2) % 2), vout.get(oc - 4, vps2[(oc - 2) % 2]))
                                e.wait_ge(fsem, otrv[oc - 2])
                                o = ((oc - 2) % 2) * 512
                                for tt in range(4):
                                    pp = (pG, pH)[tt // 2]
                                    inst = e.tensor_copy(ostg[:, o + tt * 128: o + (tt + 1) * 128],
                                                         pp[:, (tt % 2) * 128:(tt % 2 + 1) * 128])
                                oevv[oc - 2] = tinc(inst)

                        def dmm(t, e, oc=oc):
                            swt(e, 16 + (oc % 2), vd[oc])
                            if oc < 2:
                                e.wait_ge(fsem, tail_done)
                            else:
                                e.wait_ge(fsem, devv[oc - 2])
                            wdb = wd1[:, (oc % 2) * 8192:(oc % 2 + 1) * 8192]
                            pp = (pE, pF_)[oc % 2]
                            for kc in range(64):
                                mmi = e.matmul(pp[:], wdb[:, kc * 128:(kc + 1) * 128],
                                               hs[:, kc * TCF:(kc + 1) * TCF],
                                               start=(kc == 0), stop=(kc == 63))
                            mmi.then_inc(sem, 1)
                            ctr[0] += 1
                            if dst is None and oc >= 1:
                                e.wait_ge(fsem, devv[oc - 1])
                                if oc >= 2:
                                    e.wait_ge(fsem, oevv[oc - 2])
                                for tt in range(4):
                                    pp2 = (pG, pH)[tt // 2]
                                    inst = e.transpose(pp2[:, (tt % 2) * 128:(tt % 2 + 1) * 128],
                                                       osub[:, ((oc - 1) % 2) * 512 + tt * 128:
                                                            ((oc - 1) % 2) * 512 + (tt + 1) * 128],
                                                       ident[:])
                                otrv[oc - 1] = tinc(inst)

                        def sync_fn(t, e, oc=oc, c=c):
                            if oc + 1 < 16:
                                vd[oc + 1] = sput(e, 16 + ((oc + 1) % 2),
                                                  wd1[:, ((oc + 1) % 2) * 8192:((oc + 1) % 2 + 1) * 8192],
                                                  CWD.ap()[(oc + 1) * 128:(oc + 2) * 128, :])
                            if dst is not None:
                                if oc >= 1:
                                    e.wait_ge(fsem, devv[oc - 1])
                                    vst[oc - 1] = sput(e, 20 + ((oc - 1) % 2),
                                                       dst.ap()[oc - 1, :, c * TCF:(c + 1) * TCF],
                                                       osub[:, ((oc - 1) % 2) * 512:((oc - 1) % 2 + 1) * 512])
                                if oc + 1 < 16:
                                    swt(e, 20 + ((oc + 1) % 2), vst.get(oc - 1, vps[(oc + 1) % 2]))
                                    vo[oc + 1] = sput(e, 18 + ((oc + 1) % 2),
                                                      osub[:, ((oc + 1) % 2) * 512:((oc + 1) % 2 + 1) * 512],
                                                      src.ap()[oc + 1, :, c * TCF:(c + 1) * TCF])
                            else:
                                if oc + 1 < 16:
                                    if oc >= 1:
                                        e.wait_ge(fsem, otrv[oc - 1])
                                    vo[oc + 1] = sput(e, 18 + ((oc + 1) % 2),
                                                      osub[:, ((oc + 1) % 2) * 512:((oc + 1) % 2 + 1) * 512],
                                                      src.ap()[oc + 1, :, c * TCF:(c + 1) * TCF])
                                if oc >= 2:
                                    e.wait_ge(fsem, oevv[oc - 2])
                                    o = ((oc - 2) % 2) * 512
                                    for tt in range(4):
                                        vout[oc - 2] = sput(e, 20 + ((oc - 2) % 2),
                                                            out.ap()[c * TCF + tt * 128: c * TCF + (tt + 1) * 128,
                                                                     (oc - 2) * 128:(oc - 1) * 128],
                                                            ostg[:, o + tt * 128: o + (tt + 1) * 128])

                        st.step(vector=dve_fn, tensor=dmm, sync=sync_fn)

                    if dst is not None:
                        def dve_t(t, e):
                            swt(e, 18 + (15 % 2), vo[15])
                            e.wait_ge(fsem, ftc[0])
                            ob = osub[:, 512:1024]
                            inst = e.tensor_add(ob, ob, pF_[:])
                            devv[15] = tinc(inst)

                        def sync_t(t, e, c=c):
                            e.wait_ge(fsem, devv[15])
                            vst[15] = sput(e, 20 + 1,
                                           dst.ap()[15, :, c * TCF:(c + 1) * TCF],
                                           osub[:, 512:1024])

                        st.step(vector=dve_t, sync=sync_t)
                        vps[0] = vst[14]
                        vps[1] = vst[15]
                    else:
                        def dve_t1(t, e):
                            swt(e, 18 + 1, vo[15])
                            e.wait_ge(fsem, ftc[0])
                            ob = osub[:, 512:1024]
                            inst = e.tensor_add(ob, ob, pF_[:])
                            devv[15] = tinc(inst)
                            swt(e, 20 + 0, vout.get(12, vps2[0]))
                            e.wait_ge(fsem, otrv[14])
                            for tt in range(4):
                                inst = e.tensor_copy(ostg[:, tt * 128:(tt + 1) * 128],
                                                     (pG, pH)[tt // 2][:, (tt % 2) * 128:(tt % 2 + 1) * 128])
                            oevv[14] = tinc(inst)

                        def pe_t1(t, e):
                            e.wait_ge(fsem, devv[15])
                            e.wait_ge(fsem, oevv[14])
                            for tt in range(4):
                                inst = e.transpose((pG, pH)[tt // 2][:, (tt % 2) * 128:(tt % 2 + 1) * 128],
                                                   osub[:, 512 + tt * 128:512 + (tt + 1) * 128],
                                                   ident[:])
                            otrv[15] = tinc(inst)

                        def sync_t1(t, e, c=c):
                            e.wait_ge(fsem, oevv[14])
                            for tt in range(4):
                                vout[14] = sput(e, 20 + 0,
                                                out.ap()[c * TCF + tt * 128: c * TCF + (tt + 1) * 128,
                                                         14 * 128:15 * 128],
                                                ostg[:, tt * 128:(tt + 1) * 128])

                        st.step(vector=dve_t1, tensor=pe_t1, sync=sync_t1)

                        def dve_t2(t, e):
                            swt(e, 20 + 1, vout.get(13, vps2[1]))
                            e.wait_ge(fsem, otrv[15])
                            for tt in range(4):
                                inst = e.tensor_copy(ostg[:, 512 + tt * 128:512 + (tt + 1) * 128],
                                                     (pG, pH)[tt // 2][:, (tt % 2) * 128:(tt % 2 + 1) * 128])
                            oevv[15] = tinc(inst)

                        def sync_t2(t, e, c=c):
                            e.wait_ge(fsem, oevv[15])
                            for tt in range(4):
                                vout[15] = sput(e, 20 + 1,
                                                out.ap()[c * TCF + tt * 128: c * TCF + (tt + 1) * 128,
                                                         15 * 128:16 * 128],
                                                ostg[:, 512 + tt * 128:512 + (tt + 1) * 128])

                        st.step(vector=dve_t2, sync=sync_t2)
                        vps2[0] = vout[14]
                        vps2[1] = vout[15]

                if ASYNC_F:
                  with nc.Block() as blk:
                    base_mm = 16 * (f_ebase[0] + c)
                    base_hm = base_mm
                    base_hs = base_mm
                    base_mm2 = base_mm
                    base_dev = base_mm
                    base_otr = f_obase[0]
                    if dst is None:
                        f_obase[0] += 16

                    @blk.sync
                    def _(eng, c=c):
                        for gc in range(1, 16):
                            eng.wait_ge(evMM, base_mm + gc)
                            sput(eng, 14 + (gc % 2), w1buf2(wg1, gc),
                                 CWG.ap()[gc * 128:(gc + 1) * 128, :])
                            vw[gc] = sput(eng, 14 + (gc % 2), w1buf2(wu1, gc),
                                          CWU.ap()[gc * 128:(gc + 1) * 128, :])
                        eng.wait_ge(evMM, base_mm + 16)
                        eng.wait_ge(evHm, base_hm + 16)
                        vd[0] = sput(eng, 16, wd1[:, 0:8192], CWD.ap()[0:128, :])
                        if dst is not None:
                            swt(eng, 20, vps[0])
                            swt(eng, 21, vps[1])
                        else:
                            eng.wait_ge(evOtr, base_otr)
                        vo[0] = sput(eng, 18, osub[:, 0:512],
                                     src.ap()[0, :, c * TCF:(c + 1) * TCF])
                        for oc in range(16):
                            if oc + 1 < 16:
                                vd[oc + 1] = sput(eng, 16 + ((oc + 1) % 2),
                                                  wd1[:, ((oc + 1) % 2) * 8192:((oc + 1) % 2 + 1) * 8192],
                                                  CWD.ap()[(oc + 1) * 128:(oc + 2) * 128, :])
                                if dst is not None:
                                    swt(eng, 20 + ((oc + 1) % 2), vst.get(oc - 1, vps[(oc + 1) % 2]))
                                else:
                                    eng.wait_ge(evOtr, base_otr + oc)
                                vo[oc + 1] = sput(eng, 18 + ((oc + 1) % 2),
                                                  osub[:, ((oc + 1) % 2) * 512:((oc + 1) % 2 + 1) * 512],
                                                  src.ap()[oc + 1, :, c * TCF:(c + 1) * TCF])
                            if dst is not None:
                                eng.wait_ge(evDev, base_dev + oc + 1)
                                vst[oc] = sput(eng, 20 + (oc % 2),
                                               dst.ap()[oc, :, c * TCF:(c + 1) * TCF],
                                               osub[:, (oc % 2) * 512:(oc % 2 + 1) * 512])
                            else:
                                if oc > 0 and not DBG["no_out"]:
                                    if not DBG["no_oev_wait"]:
                                        eng.wait_ge(evOev, base_otr + oc)
                                    o = ((oc - 1) % 2) * 512
                                    for tt in range(4):
                                        vout[oc - 1] = sput(
                                            eng, 20 + ((oc - 1) % 2),
                                            out.ap()[c * TCF + tt * 128: c * TCF + (tt + 1) * 128,
                                                     (oc - 1) * 128: oc * 128],
                                            ostg[:, o + tt * 128: o + (tt + 1) * 128])
                        if dst is None:
                            if not DBG["no_out"]:
                                if not DBG["no_oev_wait"]:
                                    eng.wait_ge(evOev, base_otr + 16)
                                o = (15 % 2) * 512
                                for tt in range(4):
                                    vout[15] = sput(
                                        eng, 21,
                                        out.ap()[c * TCF + tt * 128: c * TCF + (tt + 1) * 128,
                                                 15 * 128: 16 * 128],
                                        ostg[:, o + tt * 128: o + (tt + 1) * 128])
                                vps2[0] = vout[14]
                                vps2[1] = vout[15]
                        else:
                            vps[0] = vst[14]
                            vps[1] = vst[15]

                    @blk.tensor
                    def _(eng, c=c):
                        for gc in range(16):
                            swt(eng, 14 + (gc % 2), vw[gc])
                            if gc > 0:
                                eng.wait_ge(evHm, base_hm + gc)
                            wgb = w1buf2(wg1, gc)
                            wub = w1buf2(wu1, gc)
                            for jj in range(4):
                                for kc in range(16):
                                    eng.matmul(pGU[jj][:],
                                               wgb[:, kc * 512 + jj * 128: kc * 512 + (jj + 1) * 128],
                                               xn[:, kc * TCF:(kc + 1) * TCF],
                                               start=(kc == 0), stop=(kc == 15))
                            for jj in range(4):
                                for kc in range(16):
                                    mmi = eng.matmul(pGU[4 + jj][:],
                                                     wub[:, kc * 512 + jj * 128: kc * 512 + (jj + 1) * 128],
                                                     xn[:, kc * TCF:(kc + 1) * TCF],
                                                     start=(kc == 0), stop=(kc == 15))
                            mmi.then_inc(evMM, 1)
                        eng.wait_ge(evHS, base_hs + 16)
                        for oc in range(16):
                            swt(eng, 16 + (oc % 2), vd[oc])
                            if oc >= 2:
                                eng.wait_ge(evDev, base_dev + oc - 1)
                            wdb = wd1[:, (oc % 2) * 8192:(oc % 2 + 1) * 8192]
                            pp = (pE, pF_)[oc % 2]
                            for kc in range(64):
                                mmi = eng.matmul(pp[:], wdb[:, kc * 128:(kc + 1) * 128],
                                                 hs[:, kc * TCF:(kc + 1) * TCF],
                                                 start=(kc == 0), stop=(kc == 63))
                            mmi.then_inc(evMM2, 1)
                            if dst is None and oc > 0:
                                eng.wait_ge(evDev, base_dev + oc)
                                eng.wait_ge(evOev, base_otr + oc - 1)
                                for tt in range(4):
                                    ppx = (pG, pH)[tt // 2]
                                    mmi = eng.transpose(
                                        ppx[:, (tt % 2) * 128:(tt % 2 + 1) * 128],
                                        osub[:, ((oc - 1) % 2) * 512 + tt * 128:
                                             ((oc - 1) % 2) * 512 + (tt + 1) * 128], ident[:])
                                mmi.then_inc(evOtr, 1)
                        if dst is None:
                            eng.wait_ge(evDev, base_dev + 16)
                            eng.wait_ge(evOev, base_otr + 15)
                            for tt in range(4):
                                ppx = (pG, pH)[tt // 2]
                                mmi = eng.transpose(
                                    ppx[:, (tt % 2) * 128:(tt % 2 + 1) * 128],
                                    osub[:, (15 % 2) * 512 + tt * 128:
                                         (15 % 2) * 512 + (tt + 1) * 128], ident[:])
                            mmi.then_inc(evOtr, 1)

                    @blk.scalar
                    def _(eng, c=c):
                        for gc in range(16):
                            eng.wait_ge(evMM, base_mm + gc + 1)
                            if gc > 0:
                                eng.wait_ge(evHm, base_hm + gc)
                            for jj in range(4):
                                ins = eng.activation(sg[:, jj * 512:(jj + 1) * 512], pGU[jj][:], AF.Silu)
                            ins.then_inc(evSilu, 1)
                            eng.wait_ge(evHm, base_hm + gc + 1)
                            if gc > 0:
                                eng.wait_ge(evHS, base_hs + gc)
                            for jj in range(4):
                                ins = eng.activation(gx[:, (6 + jj) * 512:(7 + jj) * 512],
                                                     hh[:, jj * 512:(jj + 1) * 512], AF.Square)
                            ins.then_inc(evAbs, 1)

                    @blk.vector
                    def _(eng, c=c):
                        svn = [0]

                        def edge(ins):
                            ins.then_inc(svF, 1)
                            svn[0] += 1
                            eng.wait_ge(svF, svn[0] + svF_base[0])

                        for gc in range(16):
                            eng.wait_ge(evSilu, base_mm + gc + 1)
                            if gc > 0:
                                eng.wait_ge(evHS, base_hs + gc)
                            for jj in range(4):
                                ins = eng.tensor_mul(hh[:, jj * 512:(jj + 1) * 512],
                                                     sg[:, jj * 512:(jj + 1) * 512], pGU[4 + jj][:])
                            ins.then_inc(evHm, 1)
                            eng.wait_ge(evAbs, base_mm + gc + 1)
                            pairs = [(0, 1), (0, 2), (0, 3), (1, 2), (1, 3), (2, 3)]
                            for i, (a, b) in enumerate(pairs):
                                ins = eng.tensor_tensor(gx[:, i * 512:(i + 1) * 512],
                                                        gx[:, (6 + a) * 512:(7 + a) * 512],
                                                        gx[:, (6 + b) * 512:(7 + b) * 512], ALU.is_ge)
                            edge(ins)
                            eng.tensor_add(gx[:, 6 * 512:7 * 512], gx[:, 0:512], gx[:, 512:1024])
                            eng.tensor_add(gx[:, 7 * 512:8 * 512], gx[:, 3 * 512:4 * 512], gx[:, 4 * 512:5 * 512])
                            eng.tensor_add(gx[:, 8 * 512:9 * 512], gx[:, 512:1024], gx[:, 3 * 512:4 * 512])
                            ins = eng.tensor_add(gx[:, 9 * 512:10 * 512], gx[:, 2 * 512:3 * 512], gx[:, 4 * 512:5 * 512])
                            edge(ins)
                            eng.tensor_add(gx[:, 6 * 512:7 * 512], gx[:, 6 * 512:7 * 512], gx[:, 2 * 512:3 * 512])
                            eng.tensor_sub(gx[:, 7 * 512:8 * 512], gx[:, 0:512], gx[:, 7 * 512:8 * 512])
                            eng.tensor_sub(gx[:, 8 * 512:9 * 512], gx[:, 8 * 512:9 * 512], gx[:, 5 * 512:6 * 512])
                            ins = eng.tensor_add(gx[:, 9 * 512:10 * 512], gx[:, 9 * 512:10 * 512], gx[:, 5 * 512:6 * 512])
                            edge(ins)
                            eng.tensor_single_scalar(gx[:, 6 * 512:7 * 512], gx[:, 6 * 512:7 * 512], 1.5, ALU.is_ge)
                            eng.tensor_single_scalar(gx[:, 7 * 512:8 * 512], gx[:, 7 * 512:8 * 512], -0.5, ALU.is_le)
                            eng.tensor_single_scalar(gx[:, 8 * 512:9 * 512], gx[:, 8 * 512:9 * 512], 0.5, ALU.is_le)
                            ins = eng.tensor_single_scalar(gx[:, 9 * 512:10 * 512], gx[:, 9 * 512:10 * 512], 1.5, ALU.is_le)
                            edge(ins)
                            for jj in range(4):
                                ins = eng.tensor_mul(hs[:, (gc * 4 + jj) * TCF:(gc * 4 + jj + 1) * TCF],
                                                     hh[:, jj * 512:(jj + 1) * 512],
                                                     gx[:, (6 + jj) * 512:(7 + jj) * 512])
                            ins.then_inc(evHS, 1)
                        svF_base[0] += svn[0]
                        for oc in range(16):
                            eng.wait_ge(evMM2, base_mm2 + oc + 1)
                            swt(eng, 18 + (oc % 2), vo[oc])
                            ob = osub[:, (oc % 2) * 512:(oc % 2 + 1) * 512]
                            ins = eng.tensor_add(ob, ob, (pE, pF_)[oc % 2][:])
                            ins.then_inc(evDev, 1)
                            if dst is None:
                                if oc > 0:
                                    eng.wait_ge(evOtr, base_otr + oc)
                                    if not DBG["no_out"]:
                                        swt(eng, 20 + ((oc - 1) % 2), vout.get(oc - 3, vps2[(oc - 1) % 2]))
                                    o = ((oc - 1) % 2) * 512
                                    for tt in range(4):
                                        ppx = (pG, pH)[tt // 2]
                                        ins = eng.tensor_copy(ostg[:, o + tt * 128: o + (tt + 1) * 128],
                                                              ppx[:, (tt % 2) * 128:(tt % 2 + 1) * 128])
                                    ins.then_inc(evOev, 1)
                        if dst is None:
                            eng.wait_ge(evOtr, base_otr + 16)
                            if not DBG["no_out"]:
                                swt(eng, 21, vout.get(13, vps2[1]))
                            o = (15 % 2) * 512
                            for tt in range(4):
                                ppx = (pG, pH)[tt // 2]
                                ins = eng.tensor_copy(ostg[:, o + tt * 128: o + (tt + 1) * 128],
                                                      ppx[:, (tt % 2) * 128:(tt % 2 + 1) * 128])
                            ins.then_inc(evOev, 1)

    if "F1" in phases:
        phase_F(XR1, XR2, "f1")
        f_ebase[0] += 8

    if "FX" in phases:
        phase_F(XR2, XR3, "fx")
        f_ebase[0] += 8

    if "FZ" in phases:
        phase_F(XR1, None, "fz")
        f_ebase[0] += 8

    TCB = 512
    asb = es.enter_context(nc.sbuf_tensor("b_a", [128, 18], F32))
    if "B1" in phases:
        with ExitStack() as ph:
            xt0 = ph.enter_context(nc.sbuf_tensor("b1_xt0", [128, 512], F32))
            xt1 = ph.enter_context(nc.sbuf_tensor("b1_xt1", [128, 512], F32))
            x2t = ph.enter_context(nc.sbuf_tensor("b1_x2", [128, 512], F32))
            rowst = ph.enter_context(nc.sbuf_tensor("b1_rows", [1, 4 * 512], F32))
            xn = ph.enter_context(nc.sbuf_tensor("b1_xn", [128, 16 * TCB], BF16))
            wsl = ph.enter_context(nc.sbuf_tensor("b1_wsl", [128, 4 * 2048], BF16))
            qro = ph.enter_context(nc.sbuf_tensor("b1_qro", [128, 8 * TCB], F32))
            kro = ph.enter_context(nc.sbuf_tensor("b1_kro", [128, 8 * TCB], F32))
            kps = ph.enter_context(nc.sbuf_tensor("b1_kps", [128, 8 * TCB], F32))
            vb = ph.enter_context(nc.sbuf_tensor("b1_v", [128, 16 * TCB], BF16))
            cst = ph.enter_context(nc.sbuf_tensor("b1_cs", [128, 8 * TCB], F32))
            qstg = ph.enter_context(nc.sbuf_tensor("b1_qstg", [128, 2 * TCB], BF16))
            ksq = ph.enter_context(nc.sbuf_tensor("b1_ksq", [128, 4 * 512], F32))
            fkv = ph.enter_context(nc.sbuf_tensor("b1_fkv", [128, 512], F32))

            pA = ph.enter_context(nc.psum_tensor("b1_pA", [128, 512], F32))
            pB = ph.enter_context(nc.psum_tensor("b1_pB", [128, 512], F32))
            pC = ph.enter_context(nc.psum_tensor("b1_pC", [128, 512], F32))
            pE = ph.enter_context(nc.psum_tensor("b1_pE", [128, 512], F32))
            pF_ = ph.enter_context(nc.psum_tensor("b1_pF", [128, 512], F32))

            def RW(r):
                return rowst[0:1, r * 512:(r + 1) * 512]

            with nc.Block() as blk:
                st = Stepper(blk, sem, ctr, dsem, dctr)

                def az(t, e):
                    t.memset(asb[:], 0.0)

                def drn(t, e):
                    drain_all(e)
                    t.nop()

                st.step(vector=az, sync=drn)

            for c in range(S // TCB):
                with nc.Block() as blk:
                    st = Stepper(blk, sem, ctr, dsem, dctr)

                    for fc in range(16):
                        xt = (xt0, xt1)[fc % 2]

                        def ldx(t, e, fc=fc, xt=xt, c=c):
                            t.dma_start(out=xt[:], in_=XR2.ap()[fc, :, c * TCB:(c + 1) * TCB], inc=16)

                        st.step(sync=ldx)

                        def mms(t, e, fc=fc, xt=xt):
                            t.matmul(pA[0:1, :], ones[:, 0:1], xt[:],
                                     start=(fc == 0), stop=(fc == 15))

                        def sq(t, e, xt=xt):
                            t.activation(x2t[:], xt[:], AF.Square)

                        st.step(tensor=mms, scalar=sq)

                        def mmq(t, e, fc=fc):
                            t.matmul(pC[0:1, :], ones[:, 0:1], x2t[:],
                                     start=(fc == 0), stop=(fc == 15))

                        st.step(tensor=mmq)

                    def evs(t, e):
                        t.tensor_scalar_mul(RW(0), pA[0:1, :], 1.0 / D)
                        t.tensor_scalar_mul(RW(1), pC[0:1, :], 1.0 / D)

                    st.step(vector=evs)

                    def var1(t, e):
                        t.tensor_mul(RW(2), RW(0), RW(0))

                    st.step(vector=var1)

                    def var2(t, e):
                        t.tensor_sub(RW(1), RW(1), RW(2))

                    st.step(vector=var2)

                    def var3(t, e):
                        t.tensor_scalar_add(RW(1), RW(1), LN_EPS)

                    st.step(vector=var3)

                    def var4(t, e):
                        t.reciprocal(RW(2), RW(1))

                    st.step(vector=var4)

                    def var5(t, e):
                        t.activation(RW(2), RW(2), AF.Sqrt)

                    st.step(scalar=var5)

                    def var6(t, e):
                        t.tensor_mul(RW(3), RW(0), RW(2))

                    st.step(vector=var6)

                    def bcs(t, e):
                        t.matmul(pA[:], ones[0:1, :], RW(2), start=True, stop=True)
                        mmi = e.matmul(pC[:], ones[0:1, :], RW(3), start=True, stop=True)
                        mmi.then_inc(sem, 1)
                        ctr[0] += 1

                    st.step(tensor=bcs)

                    for fc in range(16):
                        xt = (xt0, xt1)[fc % 2]

                        def ldx2(t, e, fc=fc, xt=xt, c=c):
                            t.dma_start(out=xt[:], in_=XR2.ap()[fc, :, c * TCB:(c + 1) * TCB], inc=16)

                        st.step(sync=ldx2)

                        def nm1(t, e, xt=xt):
                            t.tensor_mul(x2t[:], xt[:], pA[:])

                        st.step(vector=nm1)

                        def nm2(t, e, fc=fc):
                            t.tensor_sub(xn[:, fc * TCB:(fc + 1) * TCB], x2t[:], pC[:])

                        st.step(vector=nm2)

                    def spxn(t, e, c=c):
                        t.dma_start(out=XN2B.ap()[:, :, c * TCB:(c + 1) * TCB]
                                    .rearrange("fc p t2 -> p fc t2"),
                                    in_=xn.ap().rearrange("p (fc t2) -> p fc t2", fc=16), inc=16)

                    st.step(sync=spxn)

                    def ldcs(t, e, c=c):
                        t.dma_start(out=cst.ap()[:, 0:4 * TCB].rearrange("p (pc t2) -> p pc t2", pc=4),
                                    in_=g["costab"].ap()[:, c * TCB:(c + 1) * TCB]
                                    .rearrange("(pc p) t2 -> p pc t2", p=128), inc=16)
                        t.dma_start(out=cst.ap()[:, 4 * TCB:8 * TCB].rearrange("p (pc t2) -> p pc t2", pc=4),
                                    in_=g["sintab"].ap()[:, c * TCB:(c + 1) * TCB]
                                    .rearrange("(pc p) t2 -> p pc t2", p=128), inc=16)

                    st.step(sync=ldcs)

                    def b1dst(oc):
                        if oc < 8:
                            return qro[:, oc * TCB:(oc + 1) * TCB], False
                        if oc < 16:
                            return qstg[:, (oc % 2) * TCB:(oc % 2 + 1) * TCB], True
                        if oc < 24:
                            return kro[:, (oc - 16) * TCB:(oc - 15) * TCB], False
                        if oc < 32:
                            return kps[:, (oc - 24) * TCB:(oc - 23) * TCB], False
                        return vb[:, (oc - 32) * TCB:(oc - 31) * TCB], False

                    def ldw0(t, e):
                        for j in range(2):
                            t.dma_start(out=wsl.ap()[:, j * 2048:(j + 1) * 2048],
                                        in_=CWIN.ap()[j * 128:(j + 1) * 128, :], inc=16)

                    st.step(sync=ldw0)

                    for op in range(24):
                        oc0, oc1 = 2 * op, 2 * op + 1

                        def pfw(t, e, op=op):
                            for j in range(2):
                                oc = 2 * op + 2 + j
                                if oc < 48:
                                    t.dma_start(out=wsl.ap()[:, (oc % 4) * 2048:(oc % 4 + 1) * 2048],
                                                in_=CWIN.ap()[oc * 128:(oc + 1) * 128, :], inc=16)

                        def qkvmm(t, e, oc0=oc0, oc1=oc1):
                            for oc, pp in ((oc0, pE), (oc1, pF_)):
                                sl = wsl[:, (oc % 4) * 2048:(oc % 4 + 1) * 2048]
                                for kc in range(16):
                                    mmi = e.matmul(pp[:], sl[:, kc * 128:(kc + 1) * 128],
                                                   xn[:, kc * TCB:(kc + 1) * TCB],
                                                   start=(kc == 0), stop=(kc == 15))
                            mmi.then_inc(sem, 1)
                            ctr[0] += 1

                        st.step(tensor=qkvmm, sync=pfw)

                        def ev(t, e, oc0=oc0, oc1=oc1):
                            d0, _ = b1dst(oc0)
                            d1, _ = b1dst(oc1)
                            t.activation(d0, pE[:], AF.Identity, bias=binc[:, oc0:oc0 + 1])
                            t.activation(d1, pF_[:], AF.Identity, bias=binc[:, oc1:oc1 + 1])

                        st.step(scalar=ev)

                        if 8 <= oc0 < 16:
                            def wr(t, e, oc0=oc0, oc1=oc1, c=c):
                                t.dma_start(out=QR.ap()[oc0, :, c * TCB:(c + 1) * TCB],
                                            in_=qstg[:, (oc0 % 2) * TCB:(oc0 % 2 + 1) * TCB], inc=16)
                                t.dma_start(out=QR.ap()[oc1, :, c * TCB:(c + 1) * TCB],
                                            in_=qstg[:, (oc1 % 2) * TCB:(oc1 % 2 + 1) * TCB], inc=16)

                            st.step(sync=wr)

                    for rot in (qro, kro):
                        for e_ in range(4):
                            sl = slice(e_ * TCB, (e_ + 1) * TCB)
                            slo = slice((e_ + 4) * TCB, (e_ + 5) * TCB)
                            cs = slice(e_ * TCB, (e_ + 1) * TCB)
                            sn = slice((4 + e_) * TCB, (5 + e_) * TCB)

                            def r1(t, e, rot=rot, sl=sl, slo=slo, cs=cs, sn=sn):
                                t.tensor_mul(x2t[:], rot[:, sl], cst[:, cs])
                                t.tensor_mul(ksq[:, 0:512], rot[:, slo], cst[:, sn])
                                t.tensor_mul(fkv[:], rot[:, slo], cst[:, cs])
                                t.tensor_mul(xt0[:], rot[:, sl], cst[:, sn])

                            st.step(vector=r1)

                            def r2(t, e, rot=rot, sl=sl, slo=slo):
                                t.tensor_sub(rot[:, sl], x2t[:], ksq[:, 0:512])
                                t.tensor_add(rot[:, slo], fkv[:], xt0[:])

                            st.step(vector=r2)

                    for op in range(4):
                        def wq(t, e, op=op):
                            t.tensor_copy(qstg[:, 0:TCB], qro[:, (2 * op) * TCB:(2 * op + 1) * TCB])
                            t.tensor_copy(qstg[:, TCB:2 * TCB], qro[:, (2 * op + 1) * TCB:(2 * op + 2) * TCB])

                        st.step(vector=wq)

                        def wq2(t, e, op=op, c=c):
                            t.dma_start(out=QR.ap()[2 * op, :, c * TCB:(c + 1) * TCB],
                                        in_=qstg[:, 0:TCB], inc=16)
                            t.dma_start(out=QR.ap()[2 * op + 1, :, c * TCB:(c + 1) * TCB],
                                        in_=qstg[:, TCB:2 * TCB], inc=16)

                        st.step(sync=wq2)

                    for q4 in range(4):
                        def sq1(t, e, q4=q4):
                            for j in range(4):
                                i = q4 * 4 + j
                                srck = kro if i < 8 else kps
                                off = i if i < 8 else i - 8
                                t.activation(ksq[:, j * 512:(j + 1) * 512],
                                             srck[:, off * TCB:(off + 1) * TCB], AF.Square)

                        st.step(scalar=sq1)

                        def mmn(t, e, q4=q4):
                            for j in range(4):
                                i = q4 * 4 + j
                                mmi = e.matmul(pA[0:1, :], ones[:, 0:1], ksq[:, j * 512:(j + 1) * 512],
                                               start=(i == 0), stop=(i == 15))
                            mmi.then_inc(sem, 1)
                            ctr[0] += 1

                        st.step(tensor=mmn)

                    def nrm1(t, e):
                        t.tensor_copy(RW(0), pA[0:1, :])

                    st.step(vector=nrm1)

                    def nrm2(t, e):
                        t.activation(RW(0), RW(0), AF.Sqrt)

                    st.step(scalar=nrm2)

                    def nrm3(t, e):
                        t.tensor_scalar_max(RW(0), RW(0), 1e-12)

                    st.step(vector=nrm3)

                    def nrm4(t, e):
                        t.reciprocal(RW(0), RW(0))

                    st.step(vector=nrm4)

                    def bci(t, e):
                        t.matmul(pA[:], ones[0:1, :], RW(0), start=True, stop=True)

                    st.step(tensor=bci)

                    for p2 in range(8):
                        i0, i1 = 2 * p2, 2 * p2 + 1

                        def fk1(t, e, i0=i0, i1=i1):
                            for j, i in enumerate((i0, i1)):
                                srck = kro if i < 8 else kps
                                off = i if i < 8 else i - 8
                                t.tensor_mul(ksq[:, (2 + j) * 512:(3 + j) * 512],
                                             srck[:, off * TCB:(off + 1) * TCB], pA[:])

                        st.step(vector=fk1)

                        def fk2(t, e, i0=i0, i1=i1):
                            for j, i in enumerate((i0, i1)):
                                t.tensor_mul(ksq[:, j * 512:(j + 1) * 512],
                                             ksq[:, (2 + j) * 512:(3 + j) * 512],
                                             vb[:, i * TCB:(i + 1) * TCB])

                        st.step(vector=fk2)

                        def fk3(t, e):
                            t.reduce_sum(asb[:, 16:17], ksq[:, 0:512], axis=AX.X)
                            t.reduce_sum(asb[:, 17:18], ksq[:, 512:1024], axis=AX.X)

                        st.step(vector=fk3)

                        def fk4(t, e, i0=i0, i1=i1):
                            t.tensor_add(asb[:, i0:i0 + 1], asb[:, i0:i0 + 1], asb[:, 16:17])
                            t.tensor_add(asb[:, i1:i1 + 1], asb[:, i1:i1 + 1], asb[:, 17:18])

                        st.step(vector=fk4)

    if "B2" in phases:
        with ExitStack() as ph:
            qrc = ph.enter_context(nc.sbuf_tensor("b2_qr", [128, 16 * TCB], BF16))
            yb = ph.enter_context(nc.sbuf_tensor("b2_y", [128, 16 * TCB], BF16))
            wsl = ph.enter_context(nc.sbuf_tensor("b2_wsl", [128, 4 * 2048], BF16))
            xnt = ph.enter_context(nc.sbuf_tensor("b2_xnt", [128, 2 * TCB], BF16))
            xrt = ph.enter_context(nc.sbuf_tensor("b2_xrt", [128, 2 * TCB], F32))
            ot = ph.enter_context(nc.sbuf_tensor("b2_ot", [128, 2 * TCB], F32))
            pE = ph.enter_context(nc.psum_tensor("b2_pE", [128, 512], F32))
            pF_ = ph.enter_context(nc.psum_tensor("b2_pF", [128, 512], F32))

            for c in range(S // TCB):
                with nc.Block() as blk:
                    st = Stepper(blk, sem, ctr, dsem, dctr)

                    def ldq(t, e, c=c):
                        t.dma_start(out=qrc.ap().rearrange("p (fc t2) -> p fc t2", fc=16),
                                    in_=QR.ap()[:, :, c * TCB:(c + 1) * TCB]
                                    .rearrange("fc p t2 -> p fc t2"), inc=16)

                    def ldw0(t, e):
                        for j in range(2):
                            t.dma_start(out=wsl.ap()[:, j * 2048:(j + 1) * 2048],
                                        in_=CWOUT.ap()[j * 128:(j + 1) * 128, :], inc=16)

                    st.step(sync=ldq, scalar=ldw0)

                    def ymul(t, e):
                        for fc in range(16):
                            t.tensor_single_scalar(yb[:, fc * TCB:(fc + 1) * TCB],
                                                   qrc[:, fc * TCB:(fc + 1) * TCB],
                                                   asb[:, fc:fc + 1], ALU.mult)

                    st.step(vector=ymul)

                    for op in range(8):
                        oc0, oc1 = 2 * op, 2 * op + 1

                        def pf(t, e, op=op):
                            for j in range(2):
                                oc = 2 * op + 2 + j
                                if oc < 16:
                                    t.dma_start(
                                        out=wsl.ap()[:, (oc % 4) * 2048:(oc % 4 + 1) * 2048],
                                        in_=CWOUT.ap()[oc * 128:(oc + 1) * 128, :], inc=16)

                        def ldxr(t, e, oc0=oc0, oc1=oc1, c=c):
                            for j, oc in enumerate((oc0, oc1)):
                                t.dma_start(out=xrt[:, j * TCB:(j + 1) * TCB],
                                            in_=XR2.ap()[oc, :, c * TCB:(c + 1) * TCB], inc=16)
                                t.dma_start(out=xnt[:, j * TCB:(j + 1) * TCB],
                                            in_=XN2B.ap()[oc, :, c * TCB:(c + 1) * TCB], inc=16)

                        def omm(t, e, oc0=oc0, oc1=oc1):
                            for oc, pp in ((oc0, pE), (oc1, pF_)):
                                sl = wsl[:, (oc % 4) * 2048:(oc % 4 + 1) * 2048]
                                for kc in range(16):
                                    mmi = e.matmul(pp[:], sl[:, kc * 128:(kc + 1) * 128],
                                                   yb[:, kc * TCB:(kc + 1) * TCB],
                                                   start=(kc == 0), stop=(kc == 15))
                            mmi.then_inc(sem, 1)
                            ctr[0] += 1

                        st.step(tensor=omm, scalar=pf, sync=ldxr)

                        def oev(t, e, oc0=oc0, oc1=oc1):
                            t.activation(ot[:, 0:TCB], pE[:], AF.Identity,
                                         bias=bias[:, BOUT + oc0:BOUT + oc0 + 1])
                            t.activation(ot[:, TCB:2 * TCB], pF_[:], AF.Identity,
                                         bias=bias[:, BOUT + oc1:BOUT + oc1 + 1])

                        st.step(scalar=oev)

                        def oadd1(t, e):
                            t.tensor_add(ot[:], ot[:], xrt[:])

                        st.step(vector=oadd1)

                        def oadd2(t, e):
                            t.tensor_add(ot[:], ot[:], xnt[:])

                        st.step(vector=oadd2)

                        def owr(t, e, oc0=oc0, oc1=oc1, c=c):
                            t.dma_start(out=XR3.ap()[oc0, :, c * TCB:(c + 1) * TCB],
                                        in_=ot[:, 0:TCB], inc=16)
                            t.dma_start(out=XR3.ap()[oc1, :, c * TCB:(c + 1) * TCB],
                                        in_=ot[:, TCB:2 * TCB], inc=16)

                        st.step(sync=owr)

    if "F2" in phases:
        phase_F(XR3, None, "f2")
        f_ebase[0] += 8

    with nc.Block() as blk:
        st = Stepper(blk, sem, ctr, dsem, dctr)
        for name in taps:
            src = scrmap[name]

            def cp(t, e, src=src, name=name):
                drain_all(e)
                tt_ = nc.dram_tensor("tap_" + name, list(src.shape), src.dtype,
                                     kind="ExternalOutput")
                t.dma_start(out=tt_.ap(), in_=src.ap(), inc=16)

            st.step(sync=cp)

        def fin(t, e):
            drain_all(e)
            t.nop()

        st.step(sync=fin)

    es.close()
    global LAST_CTRS
    LAST_CTRS = (ctr[0], dctr[0])
    return nc


INPUT_NAMES = ["x", "wkT", "wvT", "bq_c", "bk_c", "bv_c", "bg_c", "bmb_c",
               "mem", "normT", "winT", "bin_c", "bout_c", "wq_sl", "wgt_sl",
               "wmb_sl", "wout_sl", "wg_p", "wu_p", "wd_p", "costab", "sintab",
               "ident", "identb", "ones", "onehot4"]


def kernel(**inputs):
    W = host_prep(inputs)
    x_all = np.ascontiguousarray(np.asarray(inputs["L0_attn_a_hidden_states"], np.float32))
    nc = build_nc()
    in_maps = []
    for b in range(NCORE):
        m = {k: W[k] for k in INPUT_NAMES if k != "x"}
        m["x"] = np.ascontiguousarray(x_all[b])
        in_maps.append(m)
    res = run_bass_kernel_spmd(nc, in_maps, core_ids=list(range(NCORE)))
    return np.stack([r["out"] for r in res.results], axis=0)


# version 32 (final)
# speedup vs baseline: 1.3292x; 1.0233x over previous
import numpy as np
import ml_dtypes
import concourse.bass as bass
import concourse.mybir as mybir
from concourse.bass_utils import run_bass_kernel_spmd
from contextlib import ExitStack

F32 = mybir.dt.float32
BF16 = mybir.dt.bfloat16
AF = mybir.ActivationFunctionType
ALU = mybir.AluOpType
AX = mybir.AxisListType

S, D = 4096, 2048
NCORE = 8
EPS_A = 1e-8
LN_EPS = 1e-5

ALL_PHASES = ("P0", "A1", "MID", "A2", "F1", "B1", "B2", "F2")
DBG = {"no_out": False, "no_otr": False, "no_oev_wait": False}
ASYNC_F = False
LAST_CTRS = (0, 0)
GSEM = [None, [0]]


class Tag:
    def __init__(self, eng, sem, ctr, dsem, dctr):
        self._eng = eng
        self._sem = sem
        self._ctr = ctr
        self._dsem = dsem
        self._dctr = dctr

    def ldma(self, out, in_, **k):
        self._eng.dma_start(out=out, in_=in_, **k).then_inc(self._dsem, 16)
        self._dctr[0] += 16
        return self._dctr[0]

    def gdma(self, out, in_, **k):
        self._eng.dma_start(out=out, in_=in_, **k).then_inc(GSEM[0], 16)
        GSEM[1][0] += 16
        return GSEM[1][0]

    def gwait(self, val):
        if val is not None:
            self._eng.wait_ge(GSEM[0], val)

    def dwait(self, val):
        if val is not None:
            self._eng.wait_ge(self._dsem, val)

    def __getattr__(self, name):
        f = getattr(self._eng, name)

        def g(*a, inc=1, **k):
            inst = f(*a, **k)
            inst.then_inc(self._sem, inc)
            self._ctr[0] += inc
            return inst

        return g


class Stepper:

    def __init__(self, block, sem, ctr, dsem=None, dctr=None):
        self.block = block
        self.sem = sem
        self.ctr = ctr
        self.dsem = dsem
        self.dctr = dctr

    def step(self, **eng_fns):
        base = self.ctr[0]
        for ename, fn in eng_fns.items():
            def body(eng, fn=fn, base=base):
                eng.wait_ge(self.sem, base)
                fn(Tag(eng, self.sem, self.ctr, self.dsem, self.dctr), eng)

            getattr(self.block, ename)(body)

    def drain_lazy(self, ename="sync"):
        base = self.ctr[0]
        tgt = self.dctr[0]

        def body(eng, base=base, tgt=tgt):
            eng.wait_ge(self.sem, base)
            eng.wait_ge(self.dsem, tgt)
            eng.nop().then_inc(self.sem, 1)

        getattr(self.block, ename)(body)
        self.ctr[0] += 1


def _col_layout(b):
    n = b.shape[0] // 128
    return np.ascontiguousarray(b.reshape(n, 128).T.astype(np.float32))


def host_prep(inputs):
    f = np.float32
    W = {}
    W["wqT"] = np.ascontiguousarray(np.asarray(inputs["L0_attn_a_Wq"], f).T)
    W["wkT"] = np.ascontiguousarray(np.asarray(inputs["L0_attn_a_Wk"], f).T)
    W["wvT"] = np.ascontiguousarray(np.asarray(inputs["L0_attn_a_Wv"], f).T)
    W["wgT"] = np.ascontiguousarray(np.asarray(inputs["L0_attn_a_Wg"], f).T)
    W["wmbT"] = np.ascontiguousarray(np.asarray(inputs["L0_attn_a_Wmb"], f).T)
    W["bq_c"] = _col_layout(np.asarray(inputs["L0_attn_a_bq"], f))
    W["bk_c"] = _col_layout(np.asarray(inputs["L0_attn_a_bk"], f))
    W["bv_c"] = _col_layout(np.asarray(inputs["L0_attn_a_bv"], f))
    W["bg_c"] = _col_layout(np.asarray(inputs["L0_attn_a_bg"], f))
    W["bmb_c"] = _col_layout(np.asarray(inputs["L0_attn_a_bmb"], f))
    W["mem"] = np.ascontiguousarray(np.asarray(inputs["L0_attn_a_memory"], f))
    W["normT"] = np.ascontiguousarray(np.asarray(inputs["L0_attn_a_norm"], f).T)

    rp = np.concatenate([np.arange(0, 1024, 2), np.arange(1, 1024, 2), np.arange(1024, 2048)])
    win = np.asarray(inputs["L0_attn_b_W_in"], f)
    perm6k = np.concatenate([rp, 2048 + rp, 4096 + rp])
    W["winT"] = np.ascontiguousarray(win[perm6k, :].T)
    W["bin_c"] = _col_layout(np.asarray(inputs["L0_attn_b_b_in"], f)[perm6k])
    W["woutT"] = np.ascontiguousarray(np.asarray(inputs["L0_attn_b_W_out"], f).T[rp, :])
    W["bout_c"] = _col_layout(np.asarray(inputs["L0_attn_b_b_out"], f))

    gc = np.arange(16)[:, None, None]
    j = np.arange(4)[None, :, None]
    p = np.arange(128)[None, None, :]
    permF = (4 * (gc * 128 + p) + j).reshape(-1)

    def slab_major(a, nk, nm):
        K2, M2 = a.shape
        ms = M2 // nm
        return np.ascontiguousarray(
            a.reshape(nk, 128, nm, ms).transpose(2, 1, 0, 3).reshape(nm * 128, nk * ms))

    W["wg_p"] = slab_major(np.asarray(inputs["L0_ffn_w_gate"], f)[:, permF], 16, 16)
    W["wu_p"] = slab_major(np.asarray(inputs["L0_ffn_w_up"], f)[:, permF], 16, 16)
    W["wd_p"] = slab_major(np.asarray(inputs["L0_ffn_w_down"], f)[permF, :], 64, 16)
    W["winT"] = slab_major(W["winT"], 16, 48)
    W["wq_sl"] = slab_major(W["wqT"], 16, 16)
    W["wgt_sl"] = slab_major(W["wgT"], 16, 16)
    W["wmb_sl"] = slab_major(W["wmbT"], 16, 16)
    W["wout_sl"] = slab_major(W["woutT"], 16, 16)

    rot = D // 2
    inv_freq = 1.0 / (10000.0 ** (np.arange(0, rot, 2, dtype=np.float64) / rot))
    freqs = np.arange(S, dtype=np.float64)[:, None] * inv_freq[None, :]
    W["costab"] = np.ascontiguousarray(np.cos(freqs).T.astype(f))
    W["sintab"] = np.ascontiguousarray(np.sin(freqs).T.astype(f))

    W["ident"] = np.eye(128, dtype=f)
    W["identb"] = np.eye(128, dtype=ml_dtypes.bfloat16)
    W["ones"] = np.ones((128, 128), dtype=f)
    oh = np.zeros((4, 512), dtype=f)
    for r in range(4):
        oh[r, r * 128:(r + 1) * 128] = 1.0
    W["onehot4"] = oh
    return W


def build_nc(taps=(), phases=ALL_PHASES):
    nc = bass.Bass("TRN2", target_bir_lowering=False)
    g = {}

    def din(name, shape, dtype=F32):
        g[name] = nc.dram_tensor(name, shape, dtype, kind="ExternalInput")

    din("x", [S, D])
    din("wkT", [D, 512]); din("wvT", [D, 512])
    din("bq_c", [128, 16]); din("bk_c", [128, 4]); din("bv_c", [128, 4])
    din("bg_c", [128, 16]); din("bmb_c", [128, 16])
    din("mem", [16, 768, 128]); din("normT", [768, 16])
    din("winT", [48 * 128, D]); din("bin_c", [128, 48])
    din("bout_c", [128, 16])
    din("wq_sl", [D, D]); din("wgt_sl", [D, D]); din("wmb_sl", [D, D])
    din("wout_sl", [D, D])
    din("wg_p", [D, 8192]); din("wu_p", [D, 8192]); din("wd_p", [D, 8192])
    din("costab", [512, S]); din("sintab", [512, S])
    din("ident", [128, 128]); din("identb", [128, 128], BF16)
    din("ones", [128, 128]); din("onehot4", [4, 512])

    out = nc.dram_tensor("out", [S, D], F32, kind="ExternalOutput")

    XN1 = nc.dram_tensor("XN1", [16, 128, S], BF16)
    XN2B = nc.dram_tensor("XN2B", [16, 128, S], BF16)
    QR = nc.dram_tensor("QR", [16, 128, S], BF16)
    XR1 = nc.dram_tensor("XR1", [16, 128, S], F32)
    XR2 = nc.dram_tensor("XR2", [16, 128, S], F32)
    XR3 = nc.dram_tensor("XR3", [16, 128, S], F32)
    ACC = nc.dram_tensor("ACC", [24, 128, 516], F32)
    ACC8 = nc.dram_tensor("ACC8", [8, 24, 128, 516], F32)
    ACC_R = nc.dram_tensor("ACC_R", [24, 128, 516], F32)
    CWG = nc.dram_tensor("CWG", [D, 8192], BF16)
    CWU = nc.dram_tensor("CWU", [D, 8192], BF16)
    CWD = nc.dram_tensor("CWD", [D, 8192], BF16)
    CWIN = nc.dram_tensor("CWIN", [48 * 128, D], BF16)
    CWQ = nc.dram_tensor("CWQ", [D, D], BF16)
    CWGT = nc.dram_tensor("CWGT", [D, D], BF16)
    CWMB = nc.dram_tensor("CWMB", [D, D], BF16)
    CWOUT = nc.dram_tensor("CWOUT", [D, D], BF16)
    scrmap = {"XN1": XN1, "XN2B": XN2B, "QR": QR, "XR1": XR1, "XR2": XR2,
              "XR3": XR3, "ACC": ACC, "ACC_R": ACC_R, "CWQ": CWQ}

    ctr = [0]
    dctr = [0]
    es = ExitStack()
    sem = es.enter_context(nc.semaphore("seq"))
    dsem = es.enter_context(nc.semaphore("dseq"))
    gsem = es.enter_context(nc.semaphore("gseq"))
    fsem = es.enter_context(nc.semaphore("fchain"))
    ftc = [0]
    asem = es.enter_context(nc.semaphore("achain"))
    atc = [0]
    vsem = es.enter_context(nc.semaphore("vchain"))
    vtc = [0]

    def cinc(inst, n=1):
        inst.then_inc(fsem, n)
        ftc[0] += n
        return ftc[0]

    def aw(e):
        e.wait_ge(asem, atc[0])

    def ainc(inst):
        inst.then_inc(asem, 1)
        atc[0] += 1
        return atc[0]

    def vw_(e):
        e.wait_ge(vsem, vtc[0])

    def vinc(inst):
        inst.then_inc(vsem, 1)
        vtc[0] += 1
        return vtc[0]
    GSEM[0] = gsem
    GSEM[1] = [0]
    slots = [(es.enter_context(nc.semaphore("slot%d" % i)), [0]) for i in range(22)]

    def sput(eng, i, out, in_):
        h, cc = slots[i]
        eng.dma_start(out=out, in_=in_).then_inc(h, 16)
        cc[0] += 16
        return cc[0]

    def swt(eng, i, val):
        if val:
            eng.wait_ge(slots[i][0], val)

    def drain_all(eng):
        for h, cc in slots:
            if cc[0]:
                eng.wait_ge(h, cc[0])
        eng.wait_ge(dsem, dctr[0])
        if GSEM[1][0]:
            eng.wait_ge(gsem, GSEM[1][0])

    ident = es.enter_context(nc.sbuf_tensor("c_ident", [128, 128], F32))
    identb = es.enter_context(nc.sbuf_tensor("c_identb", [128, 128], BF16))
    ones = es.enter_context(nc.sbuf_tensor("c_ones", [128, 128], F32))
    oh4 = es.enter_context(nc.sbuf_tensor("c_oh4", [4, 512], F32))
    bias = es.enter_context(nc.sbuf_tensor("c_bias", [128, 96], F32))
    binc = es.enter_context(nc.sbuf_tensor("c_binc", [128, 48], F32))
    BQ, BK, BV, BG, BMB, BOUT, NBK, NBQ = 0, 16, 20, 24, 40, 56, 72, 76

    with nc.Block() as blk:
        st = Stepper(blk, sem, ctr, dsem, dctr)

        def ld_const(t, e):
            t.dma_start(out=ident[:], in_=g["ident"].ap(), inc=16)
            t.dma_start(out=identb[:], in_=g["identb"].ap(), inc=16)
            t.dma_start(out=ones[:], in_=g["ones"].ap(), inc=16)
            t.dma_start(out=oh4[:], in_=g["onehot4"].ap(), inc=16)
            t.dma_start(out=bias[:, BQ:BQ + 16], in_=g["bq_c"].ap(), inc=16)
            t.dma_start(out=bias[:, BK:BK + 4], in_=g["bk_c"].ap(), inc=16)
            t.dma_start(out=bias[:, BV:BV + 4], in_=g["bv_c"].ap(), inc=16)
            t.dma_start(out=bias[:, BG:BG + 16], in_=g["bg_c"].ap(), inc=16)
            t.dma_start(out=bias[:, BMB:BMB + 16], in_=g["bmb_c"].ap(), inc=16)
            t.dma_start(out=bias[:, BOUT:BOUT + 16], in_=g["bout_c"].ap(), inc=16)
            t.dma_start(out=binc[:], in_=g["bin_c"].ap(), inc=16)

        st.step(sync=ld_const)

        def negb(t, e):
            t.tensor_scalar_mul(bias[:, NBK:NBK + 4], bias[:, BK:BK + 4], -1.0)
            t.tensor_scalar_mul(bias[:, NBQ:NBQ + 16], bias[:, BQ:BQ + 16], -1.0)

        st.step(vector=negb)

    if "P0" in phases:
        with nc.Block() as blk, nc.sbuf_tensor("cbuf", [128, 2 * 8192], BF16) as cbuf:
            st = Stepper(blk, sem, ctr, dsem, dctr)
            jobs = [(g["wg_p"], CWG, D, 8192, 2), (g["wu_p"], CWU, D, 8192, 2),
                    (g["wd_p"], CWD, D, 8192, 2), (g["winT"], CWIN, 48 * 128, D, 8),
                    (g["wq_sl"], CWQ, D, D, 8), (g["wgt_sl"], CWGT, D, D, 8),
                    (g["wmb_sl"], CWMB, D, D, 8), (g["wout_sl"], CWOUT, D, D, 8)]
            for src, dst, rows_, cols, rblk in jobs:
                for r0 in range(0, rows_, 128 * rblk):
                    nb = rblk * cols

                    gv = [None]

                    def ld(t, e, src=src, r0=r0, cols=cols, rblk=rblk, nb=nb, gv=gv):
                        gv[0] = t.gdma(cbuf.ap()[:, 0:nb].rearrange("p (r c) -> p r c", r=rblk),
                                       src.ap()[r0:r0 + 128 * rblk, :]
                                       .rearrange("(r p) c -> p r c", p=128))

                    def sv(t, e, dst=dst, r0=r0, cols=cols, rblk=rblk, nb=nb, gv=gv):
                        t.gwait(gv[0])
                        t.dma_start(out=dst.ap()[r0:r0 + 128 * rblk, :]
                                    .rearrange("(r p) c -> p r c", p=128),
                                    in_=cbuf.ap()[:, 0:nb].rearrange("p (r c) -> p r c", r=rblk),
                                    inc=16)

                    st.step(gpsimd=ld)
                    st.step(sync=sv)

    ases = ExitStack()
    memsb = ases.enter_context(nc.sbuf_tensor("memsb", [128, 16 * 6 * 128], BF16))
    normsb = ases.enter_context(nc.sbuf_tensor("normsb", [128, 6 * 16], BF16))

    def mem_tile(h, fc):
        o = (h * 6 + fc) * 128
        return memsb[:, o:o + 128]

    with nc.Block() as blk:
        st = Stepper(blk, sem, ctr, dsem, dctr)

        gvm = [None]

        def ld_mem(t, e):
            t.gdma(memsb.ap().rearrange("p (h fc d) -> p h fc d", h=16, fc=6),
                   g["mem"].ap().rearrange("h (fc p) d -> p h fc d", p=128))
            gvm[0] = t.gdma(normsb.ap().rearrange("p (fc h) -> p fc h", fc=6),
                            g["normT"].ap().rearrange("(fc p) h -> p fc h", p=128))

        st.step(gpsimd=ld_mem)

        def ld_mem_drain(t, e):
            t.gwait(gvm[0])
            t.nop()

        st.step(gpsimd=ld_mem_drain)

    TC = 512
    if "A1" in phases:
        with ExitStack() as ph:
            x_tm = ph.enter_context(nc.sbuf_tensor("a1_xtm", [128, 4 * D], F32))
            scr2k = ph.enter_context(nc.sbuf_tensor("a1_scr2k", [128, 2 * D], F32))
            stats = ph.enter_context(nc.sbuf_tensor("a1_stats", [128, 16], F32))
            xn_tm = ph.enter_context(nc.sbuf_tensor("a1_xntm", [128, 4 * D], BF16))
            xn_fm = ph.enter_context(nc.sbuf_tensor("a1_xnfm", [128, 16 * TC], BF16))
            wk_sb = ph.enter_context(nc.sbuf_tensor("a1_wk", [128, 16 * 512], BF16))
            wv_sb = ph.enter_context(nc.sbuf_tensor("a1_wv", [128, 16 * 512], BF16))
            wmb_sl = ph.enter_context(nc.sbuf_tensor("a1_wmb", [128, 2 * 2048], BF16))
            k_fm = ph.enter_context(nc.sbuf_tensor("a1_kfm", [128, 4 * TC], F32))
            v_fm = ph.enter_context(nc.sbuf_tensor("a1_vfm", [128, 4 * TC], F32))
            mb_fm = xn_tm
            k_tm = ph.enter_context(nc.sbuf_tensor("a1_ktm", [128, 4 * 128], F32))
            xx = ph.enter_context(nc.sbuf_tensor("a1_xx", [128, 4 * 260], F32))
            u_tm = ph.enter_context(nc.sbuf_tensor("a1_utm", [128, 4 * 768], BF16))
            u_fm = ph.enter_context(nc.sbuf_tensor("a1_ufm", [128, 6 * TC], BF16))
            ss6 = ph.enter_context(nc.sbuf_tensor("a1_ss6", [128, 4 * 8], F32))
            num_sb = ph.enter_context(nc.sbuf_tensor("a1_num", [128, TC], F32))
            vninv = ph.enter_context(nc.sbuf_tensor("a1_vninv", [128, TC], F32))
            p1 = ph.enter_context(nc.sbuf_tensor("a1_p1", [128, TC], F32))
            wmv_fm = ph.enter_context(nc.sbuf_tensor("a1_wmvfm", [128, TC], BF16))
            rtm = ph.enter_context(nc.sbuf_tensor("a1_rtm", [128, 4 * 520], BF16))
            astg = ph.enter_context(nc.sbuf_tensor("a1_astg", [128, 2 * 516], F32))

            pT0 = ph.enter_context(nc.psum_tensor("a1_pT0", [128, 1024], BF16))
            pT1 = ph.enter_context(nc.psum_tensor("a1_pT1", [128, 1024], BF16))
            pP0 = ph.enter_context(nc.psum_tensor("a1_pP0", [128, 512], F32))
            pP1 = ph.enter_context(nc.psum_tensor("a1_pP1", [128, 512], F32))
            pN0 = ph.enter_context(nc.psum_tensor("a1_pN0", [128, 512], F32))
            pN1 = ph.enter_context(nc.psum_tensor("a1_pN1", [128, 512], F32))
            pSm = ph.enter_context(nc.psum_tensor("a1_pSm", [128, 512], F32))
            pBC = ph.enter_context(nc.psum_tensor("a1_pBC", [128, 512], F32))

            def R(r):
                return scr2k[0:1, r * 512:(r + 1) * 512]

            def R4(r):
                return scr2k[0:4, r * 512:(r + 1) * 512]

            vload = {}
            vspill = {}
            for c in range(S // TC):
                with nc.Block() as blk:
                    st = Stepper(blk, sem, ctr, dsem, dctr)

                    def ld_x(t, e, c=c):
                        vload["x"] = sput(e, 0, x_tm.ap().rearrange("p (tt d) -> p tt d", tt=4),
                                          g["x"].ap()[c * TC:(c + 1) * TC, :]
                                          .rearrange("(tt p) d -> p tt d", p=128))
                        vload["wmb0"] = sput(e, 1, wmb_sl.ap()[:, 0:2048], CWMB.ap()[0:128, :])

                    def ld_kv(t, e, c=c):
                        if c == 0:
                            t.gdma(wk_sb.ap().rearrange("p (kc m) -> p kc m", kc=16),
                                   g["wkT"].ap().rearrange("(kc p) m -> p kc m", p=128))
                            vload["kv"] = t.gdma(wv_sb.ap().rearrange("p (kc m) -> p kc m", kc=16),
                                                 g["wvT"].ap().rearrange("(kc p) m -> p kc m", p=128))

                    st.step(sync=ld_x, gpsimd=ld_kv)

                    def ln_sums(t, e):
                        swt(e, 0, vload["x"])
                        for tt in range(4):
                            t.reduce_sum(stats[:, tt:tt + 1], x_tm[:, tt * D:(tt + 1) * D], axis=AX.X)

                    def ln_sq2(t, e):
                        swt(e, 0, vload["x"])
                        for tt in range(4):
                            t.activation(xn_tm[:, tt * D:(tt + 1) * D],
                                         x_tm[:, tt * D:(tt + 1) * D], AF.Square,
                                         accum_out=stats[:, 4 + tt:5 + tt])

                    st.step(vector=ln_sums, scalar=ln_sq2)

                    def ln_mv(t, e):
                        t.tensor_scalar_mul(stats[:, 8:12], stats[:, 0:4], 1.0 / D)
                        t.tensor_scalar_mul(stats[:, 4:8], stats[:, 4:8], 1.0 / D)

                    st.step(vector=ln_mv)

                    def ln_m2(t, e):
                        t.tensor_mul(stats[:, 12:16], stats[:, 8:12], stats[:, 8:12])

                    st.step(vector=ln_m2)

                    def ln_var(t, e):
                        t.tensor_sub(stats[:, 4:8], stats[:, 4:8], stats[:, 12:16])

                    st.step(vector=ln_var)

                    def ln_eps(t, e):
                        t.tensor_scalar_add(stats[:, 4:8], stats[:, 4:8], LN_EPS)

                    st.step(vector=ln_eps)

                    def ln_rc(t, e):
                        t.reciprocal(stats[:, 12:16], stats[:, 4:8])

                    st.step(vector=ln_rc)

                    def ln_sq(t, e):
                        t.activation(stats[:, 12:16], stats[:, 12:16], AF.Sqrt)

                    st.step(scalar=ln_sq)

                    def ln_norm(t, e):
                        for tt in range(4):
                            t.tensor_scalar(xn_tm[:, tt * D:(tt + 1) * D], x_tm[:, tt * D:(tt + 1) * D],
                                            stats[:, 8 + tt:9 + tt], stats[:, 12 + tt:13 + tt],
                                            ALU.subtract, ALU.mult)

                    st.step(vector=ln_norm)

                    for grp in range(4):
                        def tr(t, e, grp=grp):
                            for k in range(4):
                                fc = grp * 4 + k
                                pt = (pT0, pT1)[k // 2]
                                for tt in range(4):
                                    mmi = e.transpose(
                                        pt[:, (k % 2) * 512 + tt * 128: (k % 2) * 512 + (tt + 1) * 128],
                                        xn_tm[:, tt * D + fc * 128: tt * D + (fc + 1) * 128],
                                        identb[:])
                            mmi.then_inc(sem, 1)
                            ctr[0] += 1

                        def ev(t, e, grp=grp):
                            swt(e, 3, vspill.get("xn"))
                            t.tensor_copy(xn_fm[:, (grp * 4) * TC:(grp * 4 + 2) * TC], pT0[:])
                            t.tensor_copy(xn_fm[:, (grp * 4 + 2) * TC:(grp * 4 + 4) * TC], pT1[:])

                        st.step(tensor=tr)
                        st.step(vector=ev)

                    def sp_xn(t, e, c=c):
                        vspill["xn"] = sput(e, 3, XN1.ap()[:, :, c * TC:(c + 1) * TC]
                                            .rearrange("fc p t -> p fc t"),
                                            xn_fm.ap().rearrange("p (fc t) -> p fc t", fc=16))

                    st.step(sync=sp_xn)

                    for name, wsb, nco, dst, bcol in (("k", wk_sb, 4, k_fm, BK),
                                                      ("v", wv_sb, 4, v_fm, BV)):
                        for oc in range(nco):
                            pp = (pP0, pP1)[oc % 2]

                            def mm(t, e, wsb=wsb, oc=oc, pp=pp):
                                t.gwait(vload.get("kv"))
                                for kc in range(16):
                                    mmi = e.matmul(
                                        pp[:], wsb[:, kc * 512 + oc * 128: kc * 512 + (oc + 1) * 128],
                                        xn_fm[:, kc * TC:(kc + 1) * TC],
                                        start=(kc == 0), stop=(kc == 15))
                                mmi.then_inc(sem, 1)
                                ctr[0] += 1

                            def ev(t, e, oc=oc, pp=pp, dst=dst, bcol=bcol):
                                t.activation(dst[:, oc * TC:(oc + 1) * TC], pp[:], AF.Identity,
                                             bias=bias[:, bcol + oc:bcol + oc + 1])

                            st.step(tensor=mm)
                            st.step(scalar=ev)

                    for oc in range(16):
                        sl = wmb_sl[:, (oc % 2) * 2048:(oc % 2 + 1) * 2048]
                        pp = (pP0, pP1)[oc % 2]

                        def mm(t, e, sl=sl, pp=pp, oc=oc):
                            swt(e, 1 + (oc % 2), vload["wmb" + str(oc)] if oc else vload["wmb0"])
                            for kc in range(16):
                                mmi = e.matmul(pp[:], sl[:, kc * 128:(kc + 1) * 128],
                                                      xn_fm[:, kc * TC:(kc + 1) * TC],
                                                      start=(kc == 0), stop=(kc == 15))
                            mmi.then_inc(sem, 1)
                            ctr[0] += 1

                        def pf(t, e, oc=oc):
                            if oc + 1 < 16:
                                vload["wmb" + str(oc + 1)] = sput(
                                    e, 1 + ((oc + 1) % 2),
                                    wmb_sl.ap()[:, ((oc + 1) % 2) * 2048:((oc + 1) % 2 + 1) * 2048],
                                    CWMB.ap()[(oc + 1) * 128:(oc + 2) * 128, :])

                        def ev(t, e, oc=oc, pp=pp):
                            t.activation(mb_fm[:, oc * TC:(oc + 1) * TC], pp[:], AF.Sigmoid,
                                         bias=bias[:, BMB + oc:BMB + oc + 1])

                        st.step(tensor=mm, sync=pf)
                        st.step(scalar=ev)

                    for gi in range(4):
                        def ktr(t, e, gi=gi):
                            for tt in range(4):
                                mmi = e.transpose(
                                    pSm[:, tt * 128:(tt + 1) * 128],
                                    k_fm[:, gi * TC + tt * 128: gi * TC + (tt + 1) * 128],
                                    ident[:])
                            mmi.then_inc(sem, 1)
                            ctr[0] += 1

                        def kev(t, e):
                            t.tensor_copy(k_tm[:], pSm[:])

                        st.step(tensor=ktr)
                        st.step(vector=kev)

                        def relu12(t, e):
                            t.activation(xx.ap().rearrange("p (tt cc) -> p tt cc", tt=4)[:, :, 3:131],
                                         k_tm.ap().rearrange("p (tt cc) -> p tt cc", tt=4), AF.Relu)
                            t.activation(xx.ap().rearrange("p (tt cc) -> p tt cc", tt=4)[:, :, 131:259],
                                         k_tm.ap().rearrange("p (tt cc) -> p tt cc", tt=4), AF.Relu,
                                         scale=-1.0)

                        st.step(scalar=relu12)

                        def wrap(t, e):
                            t.tensor_copy(xx.ap().rearrange("p (tt cc) -> p tt cc", tt=4)[:, :, 0:3],
                                          xx.ap().rearrange("p (tt cc) -> p tt cc", tt=4)[:, :, 256:259])

                        st.step(vector=wrap)

                        def dp(t, e):
                            for tt in range(4):
                                xo = tt * 260
                                for jj, j in enumerate((1, 2, 3)):
                                    for cc in range(2):
                                        fo = tt * 768 + (jj * 2 + cc) * 128
                                        t.tensor_mul(
                                            u_tm[:, fo:fo + 128],
                                            xx[:, xo + 3 + cc * 128: xo + 3 + (cc + 1) * 128],
                                            xx[:, xo + 3 + cc * 128 - j: xo + 3 + (cc + 1) * 128 - j])

                        st.step(vector=dp)

                        def ssq(t, e):
                            for tt in range(4):
                                t.activation(u_fm[:, tt * 768:(tt + 1) * 768],
                                             u_tm[:, tt * 768:(tt + 1) * 768],
                                             AF.Square, accum_out=ss6[:, tt * 8 + 6:tt * 8 + 7])

                        st.step(scalar=ssq)

                        def sstr(t, e):
                            for tt in range(4):
                                mmi = e.transpose(pSm[0:1, tt * 128:(tt + 1) * 128],
                                                         ss6[:, tt * 8 + 6:tt * 8 + 7], ident[:])
                            mmi.then_inc(sem, 1)
                            ctr[0] += 1

                        def ssev(t, e):
                            t.tensor_copy(R(0), pSm[0:1, 0:512])

                        st.step(tensor=sstr)
                        st.step(vector=ssev)

                        def nrow(t, e):
                            t.activation(R(1), R(0), AF.Sqrt)

                        st.step(scalar=nrow)

                        def ninv1(t, e):
                            t.tensor_scalar_max(R(1), R(1), 1e-12)

                        st.step(vector=ninv1)

                        def ninv2(t, e):
                            t.reciprocal(R(2), R(1))

                        st.step(vector=ninv2)

                        for grp in range(2):
                            nfc = 4 if grp == 0 else 2

                            def utr(t, e, grp=grp, nfc=nfc):
                                for k in range(nfc):
                                    fc = grp * 4 + k
                                    pt = (pT0, pT1)[k // 2]
                                    for tt in range(4):
                                        mmi = e.transpose(
                                            pt[:, (k % 2) * 512 + tt * 128: (k % 2) * 512 + (tt + 1) * 128],
                                            u_tm[:, tt * 768 + fc * 128: tt * 768 + (fc + 1) * 128],
                                            identb[:])
                                mmi.then_inc(sem, 1)
                                ctr[0] += 1

                            def uev(t, e, grp=grp, nfc=nfc):
                                t.tensor_copy(u_fm[:, (grp * 4) * TC:(grp * 4 + 2) * TC], pT0[:])
                                if nfc == 4:
                                    t.tensor_copy(u_fm[:, (grp * 4 + 2) * TC:(grp * 4 + 4) * TC], pT1[:])

                            st.step(tensor=utr)
                            st.step(vector=uev)

                        def dnm(t, e, gi=gi):
                            for fc in range(6):
                                mmi = e.matmul(
                                    pSm[0:4, :], normsb[:, fc * 16 + gi * 4: fc * 16 + gi * 4 + 4],
                                    u_fm[:, fc * TC:(fc + 1) * TC],
                                    start=(fc == 0), stop=(fc == 5))
                            mmi.then_inc(sem, 1)
                            ctr[0] += 1

                        def dnmev(t, e):
                            t.tensor_copy(R4(4), pSm[0:4, :])

                        st.step(tensor=dnm)
                        st.step(vector=dnmev)

                        def bcn(t, e):
                            t.matmul(pBC[0:4, :], ones[0:1, 0:4], R(2), start=True, stop=True)

                        st.step(tensor=bcn)

                        def den1(t, e):
                            t.tensor_mul(R4(5), R4(4), pBC[0:4, :])

                        st.step(vector=den1)

                        def den2(t, e):
                            t.tensor_scalar_add(R4(5), R4(5), EPS_A)

                        st.step(vector=den2)

                        def den3(t, e):
                            t.reciprocal(R4(6), R4(5))

                        st.step(vector=den3)

                        def aa1(t, e):
                            t.tensor_mul(R4(6), R4(6), pBC[0:4, :])

                        st.step(vector=aa1)

                        def aa2(t, e):
                            t.tensor_mul(R4(6), R4(6), pBC[0:4, :])

                        st.step(vector=aa2)

                        def cf2(t, e):
                            t.tensor_scalar(R4(7), R4(5), 1.0, -1.0, ALU.subtract, ALU.mult)

                        st.step(vector=cf2)

                        def cf3(t, e):
                            t.tensor_scalar(R4(7), R4(7), 0.0, 1.0, ALU.max, ALU.min)

                        st.step(vector=cf3)

                        def bcn2(t, e):
                            t.matmul(pBC[0:4, :], ones[0:1, 0:4], R(2), start=True, stop=True)

                        st.step(tensor=bcn2)

                        def cp1(t, e):
                            t.tensor_mul(R4(7), R4(7), pBC[0:4, :])

                        st.step(vector=cp1)

                        def cptr(t, e):
                            for tt in range(4):
                                mmi = e.transpose(pSm[:, 448 + tt * 4:448 + (tt + 1) * 4],
                                                  R4(7)[:, tt * 128:(tt + 1) * 128], ident[0:4, 0:4])
                            mmi.then_inc(sem, 1)
                            ctr[0] += 1

                        def cpev(t, e):
                            for tt in range(4):
                                t.tensor_copy(rtm[:, tt * 520 + 512: tt * 520 + 516],
                                              pSm[:, 448 + tt * 4:448 + (tt + 1) * 4])

                        st.step(tensor=cptr)
                        st.step(vector=cpev)

                        def bcv(t, e):
                            t.matmul(pBC[:], ones[0:1, :], R(2), start=True, stop=True)

                        st.step(tensor=bcv)

                        def vn(t, e, gi=gi):
                            t.tensor_mul(vninv[:], v_fm[:, gi * TC:(gi + 1) * TC], pBC[:])

                        st.step(vector=vn)

                        for r in range(4):
                            h = gi * 4 + r
                            pn = (pN0, pN1)[r % 2]

                            def nmm(t, e, h=h, pn=pn):
                                for fc in range(6):
                                    mmi = e.matmul(pn[:], mem_tile(h, fc),
                                                          u_fm[:, fc * TC:(fc + 1) * TC],
                                                          start=(fc == 0), stop=(fc == 5))
                                mmi.then_inc(sem, 1)
                                ctr[0] += 1

                            def nev(t, e, r=r, pn=pn):
                                t.copy(num_sb[:], pn[:])

                            st.step(tensor=nmm)
                            st.step(scalar=nev)

                            def bca(t, e, r=r):
                                t.matmul(pBC[:], oh4[:, r * 128:(r + 1) * 128], R4(6),
                                         start=True, stop=True)

                            st.step(tensor=bca)

                            def pp1(t, e, r=r):
                                t.tensor_mul(p1[:], num_sb[:], pBC[:])

                            st.step(vector=pp1)

                            def pp3(t, e, r=r):
                                t.tensor_sub(p1[:], vninv[:], p1[:])

                            st.step(vector=pp3)

                            def wm(t, e, h=h, r=r):
                                t.tensor_mul(wmv_fm[:], p1[:], mb_fm[:, h * TC:(h + 1) * TC])

                            st.step(vector=wm)

                            pt = (pT0, pT1)[r % 2]

                            def wtr(t, e, r=r, pt=pt):
                                for tt in range(4):
                                    mmi = e.transpose(
                                        pt[:, tt * 128:(tt + 1) * 128],
                                        wmv_fm[:, tt * 128:(tt + 1) * 128],
                                        identb[:])
                                mmi.then_inc(sem, 1)
                                ctr[0] += 1

                            def wev(t, e, r=r, pt=pt):
                                for tt in range(4):
                                    t.tensor_copy(rtm[:, tt * 520 + r * 128: tt * 520 + (r + 1) * 128],
                                                  pt[:, tt * 128:(tt + 1) * 128])

                            st.step(tensor=wtr)
                            st.step(vector=wev)

                        for fc in range(6):
                            def dmm(t, e, fc=fc):
                                for tt in range(4):
                                    lt = u_tm[:, tt * 768 + fc * 128: tt * 768 + (fc + 1) * 128]
                                    mmi = e.matmul(pP0[:], lt, rtm[:, tt * 520: tt * 520 + 512],
                                                          start=(tt == 0), stop=(tt == 3))
                                for tt in range(4):
                                    lt = u_tm[:, tt * 768 + fc * 128: tt * 768 + (fc + 1) * 128]
                                    mmi = e.matmul(pP1[:, 0:4], lt,
                                                          rtm[:, tt * 520 + 512: tt * 520 + 516],
                                                          start=(tt == 0), stop=(tt == 3))
                                mmi.then_inc(sem, 1)
                                ctr[0] += 1

                            i6 = gi * 6 + fc

                            def dev(t, e, i6=i6):
                                o = (i6 % 2) * 516
                                swt(e, 4 + (i6 % 2), vspill.get("dlast" + str(i6 % 2)))
                                t.tensor_copy(astg[:, o:o + 512], pP0[:])
                                t.tensor_copy(astg[:, o + 512:o + 516], pP1[:, 0:4])

                            def dwr(t, e, i6=i6, c=c):
                                o = (i6 % 2) * 516
                                vspill["dlast" + str(i6 % 2)] = sput(e, 4 + (i6 % 2),
                                                                     ACC8.ap()[c, i6, :, :],
                                                                     astg[:, o:o + 516])

                            st.step(tensor=dmm)
                            st.step(vector=dev)
                            st.step(sync=dwr)

    if "MID" in phases:
        with ExitStack() as ph:
            s0 = ph.enter_context(nc.sbuf_tensor("mid_s0", [128, 24 * 516], F32))
            s1 = ph.enter_context(nc.sbuf_tensor("mid_s1", [128, 24 * 516], F32))
            with nc.Block() as blk:
                st = Stepper(blk, sem, ctr, dsem, dctr)

                def l0(t, e):
                    drain_all(e)
                    t.dma_start(out=s0.ap().rearrange("p (i cc2) -> p i cc2", i=24),
                                in_=ACC8.ap()[0].rearrange("i p cc2 -> p i cc2"), inc=16)

                st.step(sync=l0)
                for c in range(1, 8):
                    def lc(t, e, c=c):
                        t.dma_start(out=s1.ap().rearrange("p (i cc2) -> p i cc2", i=24),
                                    in_=ACC8.ap()[c].rearrange("i p cc2 -> p i cc2"), inc=16)

                    st.step(scalar=lc)

                    def ac(t, e):
                        t.tensor_add(s0[:], s0[:], s1[:])

                    st.step(vector=ac)

                def sv(t, e):
                    t.dma_start(out=ACC.ap().rearrange("i p cc2 -> p i cc2"),
                                in_=s0.ap().rearrange("p (i cc2) -> p i cc2", i=24), inc=16)

                st.step(sync=sv)

        with nc.Block() as blk:
            st = Stepper(blk, sem, ctr, dsem, dctr)

            def cc(t, e):
                t.collective_compute(
                    "AllReduce", ALU.add,
                    replica_groups=[list(range(NCORE))],
                    ins=[ACC.ap().opt()], outs=[ACC_R.ap().opt()])

            st.step(gpsimd=cc)

        with ExitStack() as ph:
            accsb = ph.enter_context(nc.sbuf_tensor("mid_acc", [128, 24 * 516], F32))
            memf = ph.enter_context(nc.sbuf_tensor("mid_memf", [128, 16 * 6 * 128], F32))
            normf = ph.enter_context(nc.sbuf_tensor("mid_normf", [128, 96], F32))
            with nc.Block() as blk:
                st = Stepper(blk, sem, ctr, dsem, dctr)

                def ld(t, e):
                    t.dma_start(out=accsb.ap().rearrange("p (i cc) -> p i cc", i=24),
                                in_=ACC_R.ap().rearrange("i p cc -> p i cc"), inc=16)
                    t.dma_start(out=memf.ap().rearrange("p (h fc d) -> p h fc d", h=16, fc=6),
                                in_=g["mem"].ap().rearrange("h (fc p) d -> p h fc d", p=128), inc=16)
                    t.dma_start(out=normf.ap().rearrange("p (fc h) -> p fc h", fc=6),
                                in_=g["normT"].ap().rearrange("(fc p) h -> p fc h", p=128), inc=16)

                st.step(sync=ld)

                def addm(t, e):
                    for gi in range(4):
                        for fc in range(6):
                            i = gi * 6 + fc
                            memv = memf.ap().rearrange("p (h f2 d) -> p h f2 d", h=16, f2=6)[:, 4 * gi:4 * gi + 4, fc, :]
                            outv = memsb.ap().rearrange("p (h f2 d) -> p h f2 d", h=16, f2=6)[:, 4 * gi:4 * gi + 4, fc, :]
                            accv = accsb.ap()[:, i * 516:i * 516 + 512].rearrange("p (r d) -> p r d", r=4)
                            t.tensor_add(outv, memv, accv)
                    for gi in range(4):
                        accn = accsb.ap().rearrange("p (i cc) -> p i cc", i=24)[:, 6 * gi:6 * gi + 6, 512:516]
                        normv = normf.ap().rearrange("p (fc h) -> p fc h", fc=6)[:, :, 4 * gi:4 * gi + 4]
                        outn = normsb.ap().rearrange("p (fc h) -> p fc h", fc=6)[:, :, 4 * gi:4 * gi + 4]
                        t.tensor_add(outn, normv, accn)

                st.step(vector=addm)

    if "A2" in phases:
        with ExitStack() as ph:
            x_tm = ph.enter_context(nc.sbuf_tensor("a2_xtm", [128, 4 * D], F32))
            x_fm = ph.enter_context(nc.sbuf_tensor("a2_xfm", [128, 16 * TC], F32))
            xn1 = ph.enter_context(nc.sbuf_tensor("a2_xn1", [128, 16 * TC], BF16))
            scr2k = ph.enter_context(nc.sbuf_tensor("a2_scr", [128, 2 * D], F32))
            wq_sl = ph.enter_context(nc.sbuf_tensor("a2_wq", [128, 2 * 2048], BF16))
            wg_sl = ph.enter_context(nc.sbuf_tensor("a2_wg", [128, 2 * 2048], BF16))
            q_fm = ph.enter_context(nc.sbuf_tensor("a2_qfm", [128, TC], F32))
            xx = ph.enter_context(nc.sbuf_tensor("a2_xx", [128, 4 * 260], F32))
            u_tm = ph.enter_context(nc.sbuf_tensor("a2_utm", [128, 2 * 4 * 768], BF16))
            u_fm = ph.enter_context(nc.sbuf_tensor("a2_ufm", [128, 6 * TC], BF16))
            ss6 = ph.enter_context(nc.sbuf_tensor("a2_ss6", [128, 4 * 8], F32))
            num_sb = ph.enter_context(nc.sbuf_tensor("a2_num", [128, 2 * TC], F32))
            gate_all = ph.enter_context(nc.sbuf_tensor("a2_gate", [128, 16 * TC], BF16))
            o_sb = ph.enter_context(nc.sbuf_tensor("a2_o", [128, TC], F32))

            pF0 = ph.enter_context(nc.psum_tensor("a2_pF0", [128, 512], F32))
            pF1 = ph.enter_context(nc.psum_tensor("a2_pF1", [128, 512], F32))
            pB0 = ph.enter_context(nc.psum_tensor("a2_pB0", [128, 1024], BF16))
            pB1 = ph.enter_context(nc.psum_tensor("a2_pB1", [128, 1024], BF16))
            pN0 = ph.enter_context(nc.psum_tensor("a2_pN0", [128, 512], F32))
            pB2 = ph.enter_context(nc.psum_tensor("a2_pB2", [128, 1024], BF16))
            pSm = ph.enter_context(nc.psum_tensor("a2_pSm", [128, 512], F32))
            pBC = ph.enter_context(nc.psum_tensor("a2_pBC", [128, 512], F32))

            def R(r):
                return scr2k[0:1, r * 512:(r + 1) * 512]

            va2 = {}
            for c in range(S // TC):
                with nc.Block() as blk:
                    st = Stepper(blk, sem, ctr, dsem, dctr)

                    def ld_x(t, e, c=c):
                        if c == 0:
                            drain_all(e)
                        va2["x"] = sput(e, 0, x_tm.ap().rearrange("p (tt d) -> p tt d", tt=4),
                                        g["x"].ap()[c * TC:(c + 1) * TC, :]
                                        .rearrange("(tt p) d -> p tt d", p=128))
                        va2["xn1"] = sput(e, 6, xn1.ap().rearrange("p (fc t2) -> p fc t2", fc=16),
                                          XN1.ap()[:, :, c * TC:(c + 1) * TC]
                                          .rearrange("fc p t2 -> p fc t2"))
                        va2["wg0"] = sput(e, 7, wg_sl.ap()[:, 0:2048], CWGT.ap()[0:128, :])

                    st.step(sync=ld_x)

                    XEV = {}
                    for f in range(17):
                        def evf(t, e, f=f):
                            if f >= 1:
                                vw_(e)
                                pt = (pF0, pSm)[(f - 1) % 2]
                                inst = e.tensor_copy(x_fm[:, (f - 1) * TC:f * TC], pt[:])
                                XEV[f - 1] = vinc(inst)

                        def trf(t, e, f=f):
                            if f < 16:
                                if f == 0:
                                    swt(e, 0, va2["x"])
                                if f >= 2:
                                    e.wait_ge(vsem, XEV[f - 2])
                                pt = (pF0, pSm)[f % 2]
                                for tt in range(4):
                                    mmi = e.transpose(pt[:, tt * 128:(tt + 1) * 128],
                                                      x_tm[:, tt * D + f * 128: tt * D + (f + 1) * 128],
                                                      ident[:])
                                mmi.then_inc(sem, 1)
                                ctr[0] += 1

                        st.step(vector=evf, tensor=trf)

                    GEV = {}
                    for h2 in range(16):
                        def gmm(t, e, h2=h2):
                            swt(e, 7 + (h2 % 2), va2["wg" + str(h2)])
                            if h2 >= 2:
                                e.wait_ge(asem, GEV[h2 - 2])
                            elif h2 == 1:
                                e.wait_ge(vsem, XEV[15])
                            if h2 == 0:
                                swt(e, 6, va2["xn1"])
                            gsl = wg_sl[:, (h2 % 2) * 2048:(h2 % 2 + 1) * 2048]
                            pg = (pN0, pSm)[h2 % 2]
                            for kc in range(16):
                                mmi = e.matmul(pg[:], gsl[:, kc * 128:(kc + 1) * 128],
                                               xn1[:, kc * TC:(kc + 1) * TC],
                                               start=(kc == 0), stop=(kc == 15))
                            mmi.then_inc(sem, 1)
                            ctr[0] += 1

                        def gpf(t, e, h2=h2):
                            if h2 + 1 < 16:
                                va2["wg" + str(h2 + 1)] = sput(
                                    e, 7 + ((h2 + 1) % 2),
                                    wg_sl.ap()[:, ((h2 + 1) % 2) * 2048:((h2 + 1) % 2 + 1) * 2048],
                                    CWGT.ap()[(h2 + 1) * 128:(h2 + 2) * 128, :])
                            if h2 == 15:
                                va2["wq0"] = sput(e, 9, wq_sl.ap()[:, 0:2048], CWQ.ap()[0:128, :])

                        def gevf(t, e, h2=h2):
                            if h2 >= 1:
                                aw(e)
                                pg = (pN0, pSm)[(h2 - 1) % 2]
                                inst = e.activation(gate_all[:, (h2 - 1) * TC:h2 * TC], pg[:],
                                                    AF.Sigmoid, bias=bias[:, BG + h2 - 1:BG + h2])
                                GEV[h2 - 1] = ainc(inst)

                        st.step(scalar=gevf, tensor=gmm, sync=gpf)

                    QEV = {}
                    RELU = {}
                    DPV = {}
                    SSQ = {}
                    SSEV = {}
                    UEV = {}
                    NEV = {}
                    NROW = {}
                    DEN4 = {}
                    MH1 = {}
                    MH2 = {}
                    MH4 = {}

                    for k in range(39):
                        if k % 2 == 0:
                            def scl(t, e, k=k):
                                if k == 0:
                                    aw(e)
                                    inst = e.activation(gate_all[:, 15 * TC:16 * TC], pSm[:],
                                                        AF.Sigmoid, bias=bias[:, BG + 15:BG + 16])
                                    GEV[15] = ainc(inst)
                                h = (k - 2) // 2
                                if 0 <= h < 16:
                                    aw(e)
                                    if h >= 1:
                                        e.wait_ge(vsem, DPV[h - 1])
                                    pr = pSm.ap().rearrange("p (tt cc) -> p tt cc", tt=4)
                                    xr = xx.ap().rearrange("p (tt cc) -> p tt cc", tt=4)
                                    e.activation(xr[:, :, 3:131], pr, AF.Relu)
                                    inst = e.activation(xr[:, :, 131:259], pr, AF.Relu, scale=-1.0)
                                    RELU[h] = ainc(inst)
                                h = (k - 4) // 2
                                if 0 <= h < 16:
                                    aw(e)
                                    e.wait_ge(vsem, max(DPV[h], UEV[h - 1]) if h >= 1 else DPV[h])
                                    ub = (h % 2) * 3072
                                    for tt in range(4):
                                        inst = e.activation(u_fm[:, tt * 768:(tt + 1) * 768],
                                                            u_tm[:, ub + tt * 768:ub + (tt + 1) * 768],
                                                            AF.Square,
                                                            accum_out=ss6[:, tt * 8 + 6:tt * 8 + 7])
                                    SSQ[h] = ainc(inst)
                                h = (k - 6) // 2
                                if 0 <= h < 16:
                                    aw(e)
                                    e.wait_ge(vsem, SSEV[h])
                                    inst = e.activation(R(1), R(0), AF.Sqrt)
                                    NROW[h] = ainc(inst)
                                    aw(e)
                                    if h >= 2:
                                        e.wait_ge(vsem, MH2[h - 2])
                                    inst = e.copy(num_sb[:, (h % 2) * TC:(h % 2 + 1) * TC], pF0[:])
                                    NEV[h] = ainc(inst)

                            def vec(t, e, k=k):
                                h = (k - 2) // 2
                                if 0 <= h < 16:
                                    vw_(e)
                                    e.wait_ge(asem, RELU[h])
                                    xr = xx.ap().rearrange("p (tt cc) -> p tt cc", tt=4)
                                    inst = e.tensor_copy(xr[:, :, 0:3], xr[:, :, 256:259])
                                    vinc(inst)
                                    vw_(e)
                                    ub = (h % 2) * 3072
                                    ut = u_tm.ap()[:, ub:ub + 3072].rearrange("p (tt f) -> p tt f", tt=4)
                                    for jj, jv in enumerate((1, 2, 3)):
                                        for cc in range(2):
                                            inst = e.tensor_tensor(
                                                ut[:, :, (jj * 2 + cc) * 128:(jj * 2 + cc + 1) * 128],
                                                xr[:, :, 3 + cc * 128: 3 + (cc + 1) * 128],
                                                xr[:, :, 3 + cc * 128 - jv: 3 + (cc + 1) * 128 - jv],
                                                ALU.mult)
                                    DPV[h] = vinc(inst)
                                h = (k - 6) // 2
                                if 0 <= h < 16:
                                    vw_(e)
                                    e.wait_ge(asem, NROW[h])
                                    inst = e.tensor_scalar_max(R(1), R(1), 1e-12)
                                    vinc(inst)
                                    vw_(e)
                                    inst = e.reciprocal(R(2), R(1))
                                    vinc(inst)
                                    vw_(e)
                                    inst = e.tensor_mul(R(3), pF1[0:1, :], R(2))
                                    vinc(inst)
                                    vw_(e)
                                    inst = e.tensor_scalar_add(R(3), R(3), EPS_A)
                                    vinc(inst)
                                    vw_(e)
                                    inst = e.reciprocal(R(3), R(3))
                                    vinc(inst)
                                    vw_(e)
                                    inst = e.tensor_mul(R(3), R(3), R(2))
                                    DEN4[h] = vinc(inst)
                                h = (k - 8) // 2
                                if 0 <= h < 16:
                                    vw_(e)
                                    e.wait_ge(asem, NEV[h])
                                    ns = num_sb[:, (h % 2) * TC:(h % 2 + 1) * TC]
                                    inst = e.tensor_mul(ns, ns, pBC[:])
                                    MH1[h] = vinc(inst)
                                    vw_(e)
                                    swt(e, 11, va2.get("o" + str((h - 1) % 16)))
                                    inst = e.tensor_mul(o_sb[:], gate_all[:, h * TC:(h + 1) * TC], ns)
                                    MH2[h] = vinc(inst)
                                    vw_(e)
                                    inst = e.tensor_add(o_sb[:], o_sb[:], x_fm[:, h * TC:(h + 1) * TC])
                                    vinc(inst)
                                    vw_(e)
                                    inst = e.tensor_add(o_sb[:], o_sb[:], xn1[:, h * TC:(h + 1) * TC])
                                    MH4[h] = vinc(inst)

                            def ten(t, e, k=k):
                                h = k // 2
                                if h < 16:
                                    swt(e, 9 + (h % 2), va2["wq" + str(h)])
                                    if h == 0:
                                        e.wait_ge(asem, GEV[14])
                                    else:
                                        e.wait_ge(asem, QEV[h - 1])
                                    qsl = wq_sl[:, (h % 2) * 2048:(h % 2 + 1) * 2048]
                                    for kc in range(16):
                                        mmi = e.matmul(pN0[:], qsl[:, kc * 128:(kc + 1) * 128],
                                                       xn1[:, kc * TC:(kc + 1) * TC],
                                                       start=(kc == 0), stop=(kc == 15))
                                    mmi.then_inc(sem, 1)
                                    ctr[0] += 1
                                h = (k - 4) // 2
                                if 0 <= h < 16:
                                    e.wait_ge(asem, SSQ[h])
                                    if h >= 1:
                                        e.wait_ge(vsem, DEN4[h - 1])
                                    for tt in range(4):
                                        mmi = e.transpose(pF1[0:1, tt * 128:(tt + 1) * 128],
                                                          ss6[:, tt * 8 + 6:tt * 8 + 7], ident[:])
                                    mmi.then_inc(sem, 1)
                                    ctr[0] += 1
                                    e.wait_ge(vsem, max(DPV[h], UEV[h - 1]) if h >= 1 else DPV[h])
                                    ub = (h % 2) * 3072
                                    for fc in range(6):
                                        pt = (pB0, pB0, pB1, pB1, pB2, pB2)[fc]
                                        for tt in range(4):
                                            mmi = e.transpose(
                                                pt[:, (fc % 2) * 512 + tt * 128:(fc % 2) * 512 + (tt + 1) * 128],
                                                u_tm[:, ub + tt * 768 + fc * 128: ub + tt * 768 + (fc + 1) * 128],
                                                identb[:])
                                    mmi.then_inc(sem, 1)
                                    ctr[0] += 1

                            def syn(t, e, k=k):
                                h = k // 2
                                if h + 1 < 16:
                                    va2["wq" + str(h + 1)] = sput(
                                        e, 9 + ((h + 1) % 2),
                                        wq_sl.ap()[:, ((h + 1) % 2) * 2048:((h + 1) % 2 + 1) * 2048],
                                        CWQ.ap()[(h + 1) * 128:(h + 2) * 128, :])
                                h = (k - 8) // 2
                                if 0 <= h < 16:
                                    e.wait_ge(vsem, MH4[h])
                                    va2["o" + str(h)] = sput(e, 11, XR1.ap()[h, :, c * TC:(c + 1) * TC],
                                                             o_sb[:])

                            st.step(scalar=scl, vector=vec, tensor=ten, sync=syn)
                        else:
                            def scl(t, e, k=k):
                                h = (k - 1) // 2
                                if h < 16:
                                    aw(e)
                                    inst = e.activation(q_fm[:], pN0[:], AF.Identity,
                                                        bias=bias[:, BQ + h:BQ + h + 1])
                                    QEV[h] = ainc(inst)

                            def vec(t, e, k=k):
                                h = (k - 5) // 2
                                if 0 <= h < 16:
                                    vw_(e)
                                    if h >= 1:
                                        e.wait_ge(asem, NROW[h - 1])
                                    inst = e.tensor_copy(R(0), pF1[0:1, 0:512])
                                    SSEV[h] = vinc(inst)
                                    vw_(e)
                                    e.wait_ge(asem, SSQ[h])
                                    inst = e.tensor_copy(u_fm[:, 0:2 * TC], pB0[:])
                                    inst = e.tensor_copy(u_fm[:, 2 * TC:4 * TC], pB1[:])
                                    inst = e.tensor_copy(u_fm[:, 4 * TC:6 * TC], pB2[:])
                                    UEV[h] = vinc(inst)

                            def ten(t, e, k=k):
                                h = (k - 1) // 2
                                if h < 16:
                                    e.wait_ge(asem, max(QEV[h], RELU[h - 1] if h >= 1 else GEV[15]))
                                    for tt in range(4):
                                        mmi = e.transpose(pSm[:, tt * 128:(tt + 1) * 128],
                                                          q_fm[:, tt * 128:(tt + 1) * 128], ident[:])
                                    mmi.then_inc(sem, 1)
                                    ctr[0] += 1
                                h = (k - 5) // 2
                                if 0 <= h < 16:
                                    wv = max(UEV[h], SSEV[h], XEV[14] if h == 0 else 0)
                                    e.wait_ge(vsem, wv)
                                    if h >= 1:
                                        e.wait_ge(asem, NEV[h - 1])
                                    for fc in range(6):
                                        mmi = e.matmul(pF0[:], mem_tile(h, fc),
                                                       u_fm[:, fc * TC:(fc + 1) * TC],
                                                       start=(fc == 0), stop=(fc == 5))
                                    for fc in range(6):
                                        mmi = e.matmul(pF1[0:1, :], normsb[:, fc * 16 + h: fc * 16 + h + 1],
                                                       u_fm[:, fc * TC:(fc + 1) * TC],
                                                       start=(fc == 0), stop=(fc == 5))
                                    mmi.then_inc(sem, 1)
                                    ctr[0] += 1
                                h = (k - 7) // 2
                                if 0 <= h < 16:
                                    e.wait_ge(vsem, max(DEN4[h], MH1[h - 1] if h >= 1 else 0))
                                    mmi = e.matmul(pBC[:], ones[0:1, :], R(3), start=True, stop=True)
                                    mmi.then_inc(sem, 1)
                                    ctr[0] += 1

                            st.step(scalar=scl, vector=vec, tensor=ten)

    ases.close()

    TCF = 512

    def w1buf2(w, i):
        return w[:, (i % 2) * 8192:(i % 2 + 1) * 8192]

    evMM = es.enter_context(nc.semaphore("F_evMM"))
    evMM2 = es.enter_context(nc.semaphore("F_evMM2"))
    evSilu = es.enter_context(nc.semaphore("F_evSilu"))
    evAbs = es.enter_context(nc.semaphore("F_evAbs"))
    evHm = es.enter_context(nc.semaphore("F_evHm"))
    evHS = es.enter_context(nc.semaphore("F_evHS"))
    evDev = es.enter_context(nc.semaphore("F_evDev"))
    evOtr = es.enter_context(nc.semaphore("F_evOtr"))
    evOev = es.enter_context(nc.semaphore("F_evOev"))
    svF = es.enter_context(nc.semaphore("F_svF"))
    svF_base = [0]
    vps = [0, 0]
    vps2 = [0, 0]
    f_ebase = [0]
    f_obase = [0]

    def phase_F(src, dst, pref):
        with ExitStack() as ph:
            xt0 = ph.enter_context(nc.sbuf_tensor(pref + "_xt0", [128, 512], F32))
            xt1 = ph.enter_context(nc.sbuf_tensor(pref + "_xt1", [128, 512], F32))
            x2t = ph.enter_context(nc.sbuf_tensor(pref + "_x2", [128, 512], F32))
            rowst = ph.enter_context(nc.sbuf_tensor(pref + "_rows", [1, 4 * 512], F32))
            xn = ph.enter_context(nc.sbuf_tensor(pref + "_xn", [128, 16 * TCF], BF16))
            wg1 = ph.enter_context(nc.sbuf_tensor(pref + "_wg1", [128, 2 * 16 * 512], BF16))
            wu1 = ph.enter_context(nc.sbuf_tensor(pref + "_wu1", [128, 2 * 16 * 512], BF16))
            wd1 = wg1
            sg = ph.enter_context(nc.sbuf_tensor(pref + "_sg", [128, 4 * 512], F32))
            hh = ph.enter_context(nc.sbuf_tensor(pref + "_hh", [128, 4 * 512], F32))
            gx = ph.enter_context(nc.sbuf_tensor(pref + "_gx", [128, 10 * 512], BF16))
            ab = gx[:, 6 * 512:10 * 512]
            hs = ph.enter_context(nc.sbuf_tensor(pref + "_hs", [128, 64 * TCF], BF16))
            osub = sg[:, 0:1024]
            ostg = (ph.enter_context(nc.sbuf_tensor(pref + "_ostg", [128, 2 * 512], F32))
                    if dst is None else None)

            pA = ph.enter_context(nc.psum_tensor(pref + "_pA", [128, 512], F32))
            pB = ph.enter_context(nc.psum_tensor(pref + "_pB", [128, 512], F32))
            pC = ph.enter_context(nc.psum_tensor(pref + "_pC", [128, 512], F32))
            pD = ph.enter_context(nc.psum_tensor(pref + "_pD", [128, 512], F32))
            pE = ph.enter_context(nc.psum_tensor(pref + "_pE", [128, 512], F32))
            pF_ = ph.enter_context(nc.psum_tensor(pref + "_pF", [128, 512], F32))
            pG = ph.enter_context(nc.psum_tensor(pref + "_pG", [128, 512], F32))
            pH = ph.enter_context(nc.psum_tensor(pref + "_pH", [128, 512], F32))
            pGU = [pA, pB, pC, pD, pE, pF_, pG, pH]

            def RW(r):
                return rowst[0:1, r * 512:(r + 1) * 512]


            v_fwr = [None]
            for c in range(S // TCF):
                with nc.Block() as blk:
                    st = Stepper(blk, sem, ctr, dsem, dctr)
                    vx = {}
                    vw = {}
                    vd = {}
                    vo = {}
                    vst = {}

                    m0 = ctr[0]
                    a0 = atc[0]
                    s12 = slots[12][1][0]
                    s13 = slots[13][1][0]
                    LD1 = {fc: (s12 if fc % 2 == 0 else s13) + 16 * (fc // 2 + 1)
                           for fc in range(16)}
                    MMS = {fc: m0 + 2 * fc + 1 for fc in range(16)}
                    MMQ = {fc: m0 + 2 * fc + 2 for fc in range(16)}
                    SQ1 = {fc: a0 + fc + 1 for fc in range(16)}

                    def sqb(fc):
                        return (x2t, osub[:, 0:512])[fc % 2]

                    def l1_syn(t, e, c=c):
                        if c == 0:
                            drain_all(e)
                        sput(e, 14, w1buf2(wg1, 0), CWG.ap()[0:128, :])
                        vw[0] = sput(e, 14, w1buf2(wu1, 0), CWU.ap()[0:128, :])
                        for fc in range(16):
                            if fc >= 2:
                                e.wait_ge(sem, MMS[fc - 2])
                                e.wait_ge(asem, SQ1[fc - 2])
                            sput(e, 12 + (fc % 2), (xt0, xt1)[fc % 2][:],
                                 src.ap()[fc, :, c * TCF:(c + 1) * TCF])

                    def l1_act(t, e):
                        for fc in range(16):
                            aw(e)
                            swt(e, 12 + (fc % 2), LD1[fc])
                            if fc == 1 and dst is not None:
                                swt(e, 20, vps[0])
                                swt(e, 21, vps[1])
                            if fc >= 2:
                                e.wait_ge(sem, MMQ[fc - 2])
                            inst = e.activation(sqb(fc)[:], (xt0, xt1)[fc % 2][:], AF.Square)
                            ainc(inst)

                    def l1_pe(t, e):
                        for fc in range(16):
                            e.wait_ge(sem, m0 + 2 * fc)
                            swt(e, 12 + (fc % 2), LD1[fc])
                            mmi = e.matmul(pA[0:1, :], ones[:, 0:1], (xt0, xt1)[fc % 2][:],
                                           start=(fc == 0), stop=(fc == 15))
                            mmi.then_inc(sem, 1)
                            ctr[0] += 1
                            e.wait_ge(asem, SQ1[fc])
                            mmi = e.matmul(pC[0:1, :], ones[:, 0:1], sqb(fc)[:],
                                           start=(fc == 0), stop=(fc == 15))
                            mmi.then_inc(sem, 1)
                            ctr[0] += 1

                    st.step(scalar=l1_act, tensor=l1_pe, sync=l1_syn)

                    def l2_vec(t, e):
                        vw_(e)
                        inst = e.tensor_scalar_mul(RW(0), pA[0:1, :], 1.0 / D)
                        inst = e.tensor_scalar_mul(RW(1), pC[0:1, :], 1.0 / D)
                        vinc(inst)
                        vw_(e)
                        inst = e.tensor_mul(RW(2), RW(0), RW(0))
                        vinc(inst)
                        vw_(e)
                        inst = e.tensor_sub(RW(1), RW(1), RW(2))
                        vinc(inst)
                        vw_(e)
                        inst = e.tensor_scalar_add(RW(1), RW(1), LN_EPS)
                        vinc(inst)
                        vw_(e)
                        inst = e.reciprocal(RW(2), RW(1))
                        vinc(inst)

                    st.step(vector=l2_vec)
                    V4 = vtc[0]

                    def l3_act(t, e):
                        aw(e)
                        e.wait_ge(vsem, V4)
                        inst = e.activation(RW(2), RW(2), AF.Sqrt)
                        ainc(inst)

                    st.step(scalar=l3_act)
                    V5 = atc[0]

                    def l3_vec(t, e):
                        vw_(e)
                        e.wait_ge(asem, V5)
                        inst = e.tensor_mul(RW(3), RW(0), RW(2))
                        vinc(inst)

                    def l3_pe(t, e):
                        e.wait_ge(asem, V5)
                        e.wait_ge(vsem, vtc[0])
                        t.matmul(pA[:], ones[0:1, :], RW(2), start=True, stop=True)
                        t.matmul(pC[:], ones[0:1, :], RW(3), start=True, stop=True)

                    st.step(vector=l3_vec, tensor=l3_pe)
                    bcs_v = ctr[0]

                    v0 = vtc[0]
                    s12b = slots[12][1][0]
                    s13b = slots[13][1][0]
                    LD2 = {fc: (s12b if fc % 2 == 0 else s13b) + 16 * (fc // 2 + 1)
                           for fc in range(16)}
                    NM1 = {fc: v0 + 2 * fc + 1 for fc in range(16)}
                    NM2 = {fc: v0 + 2 * fc + 2 for fc in range(16)}

                    def l4_syn(t, e, c=c):
                        for fc in range(16):
                            if fc >= 2:
                                e.wait_ge(vsem, NM1[fc - 2])
                            else:
                                e.wait_ge(sem, m0 + 32)
                                e.wait_ge(asem, SQ1[15])
                            sput(e, 12 + (fc % 2), (xt0, xt1)[fc % 2][:],
                                 src.ap()[fc, :, c * TCF:(c + 1) * TCF])

                    def l4_vec(t, e):
                        for fc in range(16):
                            vw_(e)
                            swt(e, 12 + (fc % 2), LD2[fc])
                            if fc == 0:
                                e.wait_ge(sem, bcs_v)
                                e.wait_ge(asem, SQ1[15])
                            inst = e.tensor_mul(sqb(fc)[:], (xt0, xt1)[fc % 2][:], pA[:])
                            vinc(inst)
                            vw_(e)
                            inst = e.tensor_sub(xn[:, fc * TCF:(fc + 1) * TCF], sqb(fc)[:], pC[:])
                            vinc(inst)

                    st.step(vector=l4_vec, sync=l4_syn)
                    ln_done = NM2[15]

                vout = {}
                if not ASYNC_F:
                  with nc.Block() as blk:
                    st = Stepper(blk, sem, ctr, dsem, dctr)
                    hv = {}
                    devv = {}
                    otrv = {}
                    oevv = {}

                    def tinc(inst):
                        inst.then_inc(fsem, 1)
                        ftc[0] += 1
                        return ftc[0]

                    def em_silu(e, s1):
                        p = s1 % 2
                        e.wait_ge(fsem, ftc[0])
                        for j in range(2):
                            jj = 2 * p + j
                            inst = e.activation(sg[:, jj * 512:(jj + 1) * 512],
                                                pGU[4 * p + j][:], AF.Silu)
                        return tinc(inst)

                    def em_hmul(e, s1):
                        p = s1 % 2
                        e.wait_ge(fsem, ftc[0])
                        for j in range(2):
                            jj = 2 * (s1 % 2) + j
                            inst = e.tensor_mul(hh[:, jj * 512:(jj + 1) * 512],
                                                sg[:, jj * 512:(jj + 1) * 512],
                                                pGU[4 * p + 2 + j][:])
                        hv[s1] = tinc(inst)
                        return hv[s1]

                    def em_habs(e):
                        e.wait_ge(fsem, ftc[0])
                        for jj in range(4):
                            inst = e.activation(gx[:, (6 + jj) * 512:(7 + jj) * 512],
                                                hh[:, jj * 512:(jj + 1) * 512], AF.Square)
                        return tinc(inst)

                    def em_tail24(e, gcx):
                        e.wait_ge(fsem, ftc[0])
                        pairs = [(0, 1), (0, 2), (0, 3), (1, 2), (1, 3), (2, 3)]
                        for i2, (a, b) in enumerate(pairs):
                            inst = e.tensor_tensor(gx[:, i2 * 512:(i2 + 1) * 512],
                                                   gx[:, (6 + a) * 512:(7 + a) * 512],
                                                   gx[:, (6 + b) * 512:(7 + b) * 512], ALU.is_ge)
                        tinc(inst)
                        e.wait_ge(fsem, ftc[0])
                        inst = e.tensor_add(gx[:, 6 * 512:7 * 512], gx[:, 0:512], gx[:, 512:1024])
                        inst = e.tensor_add(gx[:, 7 * 512:8 * 512], gx[:, 3 * 512:4 * 512], gx[:, 4 * 512:5 * 512])
                        inst = e.tensor_add(gx[:, 8 * 512:9 * 512], gx[:, 512:1024], gx[:, 3 * 512:4 * 512])
                        inst = e.tensor_add(gx[:, 9 * 512:10 * 512], gx[:, 2 * 512:3 * 512], gx[:, 4 * 512:5 * 512])
                        tinc(inst)
                        e.wait_ge(fsem, ftc[0])
                        inst = e.tensor_add(gx[:, 6 * 512:7 * 512], gx[:, 6 * 512:7 * 512], gx[:, 2 * 512:3 * 512])
                        inst = e.tensor_sub(gx[:, 7 * 512:8 * 512], gx[:, 0:512], gx[:, 7 * 512:8 * 512])
                        inst = e.tensor_sub(gx[:, 8 * 512:9 * 512], gx[:, 8 * 512:9 * 512], gx[:, 5 * 512:6 * 512])
                        inst = e.tensor_add(gx[:, 9 * 512:10 * 512], gx[:, 9 * 512:10 * 512], gx[:, 5 * 512:6 * 512])
                        tinc(inst)
                        e.wait_ge(fsem, ftc[0])
                        inst = e.tensor_single_scalar(gx[:, 6 * 512:7 * 512], gx[:, 6 * 512:7 * 512], 1.5, ALU.is_ge)
                        inst = e.tensor_single_scalar(gx[:, 7 * 512:8 * 512], gx[:, 7 * 512:8 * 512], -0.5, ALU.is_le)
                        inst = e.tensor_single_scalar(gx[:, 8 * 512:9 * 512], gx[:, 8 * 512:9 * 512], 0.5, ALU.is_le)
                        inst = e.tensor_single_scalar(gx[:, 9 * 512:10 * 512], gx[:, 9 * 512:10 * 512], 1.5, ALU.is_le)
                        tinc(inst)
                        e.wait_ge(fsem, ftc[0])
                        for jj in range(4):
                            inst = e.tensor_mul(hs[:, (gcx * 4 + jj) * TCF:(gcx * 4 + jj + 1) * TCF],
                                                hh[:, jj * 512:(jj + 1) * 512],
                                                gx[:, (6 + jj) * 512:(7 + jj) * 512])
                        return tinc(inst)

                    for s in range(32):
                        gc, h = s // 2, s % 2

                        def gumm(t, e, s=s, gc=gc, h=h):
                            if h == 0:
                                swt(e, 14 + (gc % 2), vw[gc])
                            if s < 2:
                                e.wait_ge(vsem, ln_done)
                            if s >= 2:
                                e.wait_ge(fsem, hv[s - 2])
                            wgb = w1buf2(wg1, gc)
                            wub = w1buf2(wu1, gc)
                            p = s % 2
                            for j in range(2):
                                jj = 2 * h + j
                                for kc in range(16):
                                    e.matmul(pGU[4 * p + j][:],
                                             wgb[:, kc * 512 + jj * 128: kc * 512 + (jj + 1) * 128],
                                             xn[:, kc * TCF:(kc + 1) * TCF],
                                             start=(kc == 0), stop=(kc == 15))
                            for j in range(2):
                                jj = 2 * h + j
                                for kc in range(16):
                                    mmi = e.matmul(pGU[4 * p + 2 + j][:],
                                                   wub[:, kc * 512 + jj * 128: kc * 512 + (jj + 1) * 128],
                                                   xn[:, kc * TCF:(kc + 1) * TCF],
                                                   start=(kc == 0), stop=(kc == 15))
                            mmi.then_inc(sem, 1)
                            ctr[0] += 1

                        def pfw(t, e, gc=gc, h=h):
                            if h == 0 and gc + 1 < 16:
                                sput(e, 14 + ((gc + 1) % 2), w1buf2(wg1, gc + 1),
                                     CWG.ap()[(gc + 1) * 128:(gc + 2) * 128, :])
                                vw[gc + 1] = sput(e, 14 + ((gc + 1) % 2), w1buf2(wu1, gc + 1),
                                                  CWU.ap()[(gc + 1) * 128:(gc + 2) * 128, :])

                        def tact(t, e, s=s):
                            if s >= 1:
                                em_silu(e, s - 1)
                            if s >= 3 and s % 2 == 1:
                                em_habs(e)

                        def tdve(t, e, s=s):
                            if s >= 3 and s % 2 == 1:
                                em_tail24(e, (s - 3) // 2)
                            if s >= 1:
                                em_hmul(e, s - 1)

                        st.step(scalar=tact, vector=tdve, tensor=gumm, sync=pfw)

                    def tfa1(t, e):
                        em_silu(e, 31)

                    def tfv1(t, e):
                        em_hmul(e, 31)

                    def ldwd0(t, e, c=c):
                        vd[0] = sput(e, 16, wd1[:, 0:8192], CWD.ap()[0:128, :])
                        if dst is not None:
                            swt(e, 20, vps[0])
                            swt(e, 21, vps[1])
                        vo[0] = sput(e, 18, osub[:, 0:512], src.ap()[0, :, c * TCF:(c + 1) * TCF])

                    st.step(scalar=tfa1, vector=tfv1, sync=ldwd0)

                    def tfa2(t, e):
                        em_habs(e)

                    def tfv2(t, e):
                        em_tail24(e, 15)

                    st.step(scalar=tfa2, vector=tfv2)
                    tail_done = ftc[0]

                    for oc in range(16):
                        def dve_fn(t, e, oc=oc):
                            if oc >= 1:
                                swt(e, 18 + ((oc - 1) % 2), vo[oc - 1])
                                e.wait_ge(fsem, ftc[0])
                                ob = osub[:, ((oc - 1) % 2) * 512:((oc - 1) % 2 + 1) * 512]
                                inst = e.tensor_add(ob, ob, (pE, pF_)[(oc - 1) % 2][:])
                                devv[oc - 1] = tinc(inst)
                            if dst is None and oc >= 2:
                                swt(e, 20 + ((oc - 2) % 2), vout.get(oc - 4, vps2[(oc - 2) % 2]))
                                e.wait_ge(fsem, otrv[oc - 2])
                                o = ((oc - 2) % 2) * 512
                                for tt in range(4):
                                    pp = (pG, pH)[tt // 2]
                                    inst = e.tensor_copy(ostg[:, o + tt * 128: o + (tt + 1) * 128],
                                                         pp[:, (tt % 2) * 128:(tt % 2 + 1) * 128])
                                oevv[oc - 2] = tinc(inst)

                        def dmm(t, e, oc=oc):
                            swt(e, 16 + (oc % 2), vd[oc])
                            if oc < 2:
                                e.wait_ge(fsem, tail_done)
                            else:
                                e.wait_ge(fsem, devv[oc - 2])
                            wdb = wd1[:, (oc % 2) * 8192:(oc % 2 + 1) * 8192]
                            pp = (pE, pF_)[oc % 2]
                            for kc in range(64):
                                mmi = e.matmul(pp[:], wdb[:, kc * 128:(kc + 1) * 128],
                                               hs[:, kc * TCF:(kc + 1) * TCF],
                                               start=(kc == 0), stop=(kc == 63))
                            mmi.then_inc(sem, 1)
                            ctr[0] += 1
                            if dst is None and oc >= 1:
                                e.wait_ge(fsem, devv[oc - 1])
                                if oc >= 2:
                                    e.wait_ge(fsem, oevv[oc - 2])
                                for tt in range(4):
                                    pp2 = (pG, pH)[tt // 2]
                                    inst = e.transpose(pp2[:, (tt % 2) * 128:(tt % 2 + 1) * 128],
                                                       osub[:, ((oc - 1) % 2) * 512 + tt * 128:
                                                            ((oc - 1) % 2) * 512 + (tt + 1) * 128],
                                                       ident[:])
                                otrv[oc - 1] = tinc(inst)

                        def sync_fn(t, e, oc=oc, c=c):
                            if oc + 1 < 16:
                                vd[oc + 1] = sput(e, 16 + ((oc + 1) % 2),
                                                  wd1[:, ((oc + 1) % 2) * 8192:((oc + 1) % 2 + 1) * 8192],
                                                  CWD.ap()[(oc + 1) * 128:(oc + 2) * 128, :])
                            if dst is not None:
                                if oc >= 1:
                                    e.wait_ge(fsem, devv[oc - 1])
                                    vst[oc - 1] = sput(e, 20 + ((oc - 1) % 2),
                                                       dst.ap()[oc - 1, :, c * TCF:(c + 1) * TCF],
                                                       osub[:, ((oc - 1) % 2) * 512:((oc - 1) % 2 + 1) * 512])
                                if oc + 1 < 16:
                                    swt(e, 20 + ((oc + 1) % 2), vst.get(oc - 1, vps[(oc + 1) % 2]))
                                    vo[oc + 1] = sput(e, 18 + ((oc + 1) % 2),
                                                      osub[:, ((oc + 1) % 2) * 512:((oc + 1) % 2 + 1) * 512],
                                                      src.ap()[oc + 1, :, c * TCF:(c + 1) * TCF])
                            else:
                                if oc + 1 < 16:
                                    if oc >= 1:
                                        e.wait_ge(fsem, otrv[oc - 1])
                                    vo[oc + 1] = sput(e, 18 + ((oc + 1) % 2),
                                                      osub[:, ((oc + 1) % 2) * 512:((oc + 1) % 2 + 1) * 512],
                                                      src.ap()[oc + 1, :, c * TCF:(c + 1) * TCF])
                                if oc >= 2:
                                    e.wait_ge(fsem, oevv[oc - 2])
                                    o = ((oc - 2) % 2) * 512
                                    for tt in range(4):
                                        vout[oc - 2] = sput(e, 20 + ((oc - 2) % 2),
                                                            out.ap()[c * TCF + tt * 128: c * TCF + (tt + 1) * 128,
                                                                     (oc - 2) * 128:(oc - 1) * 128],
                                                            ostg[:, o + tt * 128: o + (tt + 1) * 128])

                        st.step(vector=dve_fn, tensor=dmm, sync=sync_fn)

                    if dst is not None:
                        def dve_t(t, e):
                            swt(e, 18 + (15 % 2), vo[15])
                            e.wait_ge(fsem, ftc[0])
                            ob = osub[:, 512:1024]
                            inst = e.tensor_add(ob, ob, pF_[:])
                            devv[15] = tinc(inst)

                        def sync_t(t, e, c=c):
                            e.wait_ge(fsem, devv[15])
                            vst[15] = sput(e, 20 + 1,
                                           dst.ap()[15, :, c * TCF:(c + 1) * TCF],
                                           osub[:, 512:1024])

                        st.step(vector=dve_t, sync=sync_t)
                        vps[0] = vst[14]
                        vps[1] = vst[15]
                    else:
                        def dve_t1(t, e):
                            swt(e, 18 + 1, vo[15])
                            e.wait_ge(fsem, ftc[0])
                            ob = osub[:, 512:1024]
                            inst = e.tensor_add(ob, ob, pF_[:])
                            devv[15] = tinc(inst)
                            swt(e, 20 + 0, vout.get(12, vps2[0]))
                            e.wait_ge(fsem, otrv[14])
                            for tt in range(4):
                                inst = e.tensor_copy(ostg[:, tt * 128:(tt + 1) * 128],
                                                     (pG, pH)[tt // 2][:, (tt % 2) * 128:(tt % 2 + 1) * 128])
                            oevv[14] = tinc(inst)

                        def pe_t1(t, e):
                            e.wait_ge(fsem, devv[15])
                            e.wait_ge(fsem, oevv[14])
                            for tt in range(4):
                                inst = e.transpose((pG, pH)[tt // 2][:, (tt % 2) * 128:(tt % 2 + 1) * 128],
                                                   osub[:, 512 + tt * 128:512 + (tt + 1) * 128],
                                                   ident[:])
                            otrv[15] = tinc(inst)

                        def sync_t1(t, e, c=c):
                            e.wait_ge(fsem, oevv[14])
                            for tt in range(4):
                                vout[14] = sput(e, 20 + 0,
                                                out.ap()[c * TCF + tt * 128: c * TCF + (tt + 1) * 128,
                                                         14 * 128:15 * 128],
                                                ostg[:, tt * 128:(tt + 1) * 128])

                        st.step(vector=dve_t1, tensor=pe_t1, sync=sync_t1)

                        def dve_t2(t, e):
                            swt(e, 20 + 1, vout.get(13, vps2[1]))
                            e.wait_ge(fsem, otrv[15])
                            for tt in range(4):
                                inst = e.tensor_copy(ostg[:, 512 + tt * 128:512 + (tt + 1) * 128],
                                                     (pG, pH)[tt // 2][:, (tt % 2) * 128:(tt % 2 + 1) * 128])
                            oevv[15] = tinc(inst)

                        def sync_t2(t, e, c=c):
                            e.wait_ge(fsem, oevv[15])
                            for tt in range(4):
                                vout[15] = sput(e, 20 + 1,
                                                out.ap()[c * TCF + tt * 128: c * TCF + (tt + 1) * 128,
                                                         15 * 128:16 * 128],
                                                ostg[:, 512 + tt * 128:512 + (tt + 1) * 128])

                        st.step(vector=dve_t2, sync=sync_t2)
                        vps2[0] = vout[14]
                        vps2[1] = vout[15]

                if ASYNC_F:
                  with nc.Block() as blk:
                    base_mm = 16 * (f_ebase[0] + c)
                    base_hm = base_mm
                    base_hs = base_mm
                    base_mm2 = base_mm
                    base_dev = base_mm
                    base_otr = f_obase[0]
                    if dst is None:
                        f_obase[0] += 16

                    @blk.sync
                    def _(eng, c=c):
                        for gc in range(1, 16):
                            eng.wait_ge(evMM, base_mm + gc)
                            sput(eng, 14 + (gc % 2), w1buf2(wg1, gc),
                                 CWG.ap()[gc * 128:(gc + 1) * 128, :])
                            vw[gc] = sput(eng, 14 + (gc % 2), w1buf2(wu1, gc),
                                          CWU.ap()[gc * 128:(gc + 1) * 128, :])
                        eng.wait_ge(evMM, base_mm + 16)
                        eng.wait_ge(evHm, base_hm + 16)
                        vd[0] = sput(eng, 16, wd1[:, 0:8192], CWD.ap()[0:128, :])
                        if dst is not None:
                            swt(eng, 20, vps[0])
                            swt(eng, 21, vps[1])
                        else:
                            eng.wait_ge(evOtr, base_otr)
                        vo[0] = sput(eng, 18, osub[:, 0:512],
                                     src.ap()[0, :, c * TCF:(c + 1) * TCF])
                        for oc in range(16):
                            if oc + 1 < 16:
                                vd[oc + 1] = sput(eng, 16 + ((oc + 1) % 2),
                                                  wd1[:, ((oc + 1) % 2) * 8192:((oc + 1) % 2 + 1) * 8192],
                                                  CWD.ap()[(oc + 1) * 128:(oc + 2) * 128, :])
                                if dst is not None:
                                    swt(eng, 20 + ((oc + 1) % 2), vst.get(oc - 1, vps[(oc + 1) % 2]))
                                else:
                                    eng.wait_ge(evOtr, base_otr + oc)
                                vo[oc + 1] = sput(eng, 18 + ((oc + 1) % 2),
                                                  osub[:, ((oc + 1) % 2) * 512:((oc + 1) % 2 + 1) * 512],
                                                  src.ap()[oc + 1, :, c * TCF:(c + 1) * TCF])
                            if dst is not None:
                                eng.wait_ge(evDev, base_dev + oc + 1)
                                vst[oc] = sput(eng, 20 + (oc % 2),
                                               dst.ap()[oc, :, c * TCF:(c + 1) * TCF],
                                               osub[:, (oc % 2) * 512:(oc % 2 + 1) * 512])
                            else:
                                if oc > 0 and not DBG["no_out"]:
                                    if not DBG["no_oev_wait"]:
                                        eng.wait_ge(evOev, base_otr + oc)
                                    o = ((oc - 1) % 2) * 512
                                    for tt in range(4):
                                        vout[oc - 1] = sput(
                                            eng, 20 + ((oc - 1) % 2),
                                            out.ap()[c * TCF + tt * 128: c * TCF + (tt + 1) * 128,
                                                     (oc - 1) * 128: oc * 128],
                                            ostg[:, o + tt * 128: o + (tt + 1) * 128])
                        if dst is None:
                            if not DBG["no_out"]:
                                if not DBG["no_oev_wait"]:
                                    eng.wait_ge(evOev, base_otr + 16)
                                o = (15 % 2) * 512
                                for tt in range(4):
                                    vout[15] = sput(
                                        eng, 21,
                                        out.ap()[c * TCF + tt * 128: c * TCF + (tt + 1) * 128,
                                                 15 * 128: 16 * 128],
                                        ostg[:, o + tt * 128: o + (tt + 1) * 128])
                                vps2[0] = vout[14]
                                vps2[1] = vout[15]
                        else:
                            vps[0] = vst[14]
                            vps[1] = vst[15]

                    @blk.tensor
                    def _(eng, c=c):
                        for gc in range(16):
                            swt(eng, 14 + (gc % 2), vw[gc])
                            if gc > 0:
                                eng.wait_ge(evHm, base_hm + gc)
                            wgb = w1buf2(wg1, gc)
                            wub = w1buf2(wu1, gc)
                            for jj in range(4):
                                for kc in range(16):
                                    eng.matmul(pGU[jj][:],
                                               wgb[:, kc * 512 + jj * 128: kc * 512 + (jj + 1) * 128],
                                               xn[:, kc * TCF:(kc + 1) * TCF],
                                               start=(kc == 0), stop=(kc == 15))
                            for jj in range(4):
                                for kc in range(16):
                                    mmi = eng.matmul(pGU[4 + jj][:],
                                                     wub[:, kc * 512 + jj * 128: kc * 512 + (jj + 1) * 128],
                                                     xn[:, kc * TCF:(kc + 1) * TCF],
                                                     start=(kc == 0), stop=(kc == 15))
                            mmi.then_inc(evMM, 1)
                        eng.wait_ge(evHS, base_hs + 16)
                        for oc in range(16):
                            swt(eng, 16 + (oc % 2), vd[oc])
                            if oc >= 2:
                                eng.wait_ge(evDev, base_dev + oc - 1)
                            wdb = wd1[:, (oc % 2) * 8192:(oc % 2 + 1) * 8192]
                            pp = (pE, pF_)[oc % 2]
                            for kc in range(64):
                                mmi = eng.matmul(pp[:], wdb[:, kc * 128:(kc + 1) * 128],
                                                 hs[:, kc * TCF:(kc + 1) * TCF],
                                                 start=(kc == 0), stop=(kc == 63))
                            mmi.then_inc(evMM2, 1)
                            if dst is None and oc > 0:
                                eng.wait_ge(evDev, base_dev + oc)
                                eng.wait_ge(evOev, base_otr + oc - 1)
                                for tt in range(4):
                                    ppx = (pG, pH)[tt // 2]
                                    mmi = eng.transpose(
                                        ppx[:, (tt % 2) * 128:(tt % 2 + 1) * 128],
                                        osub[:, ((oc - 1) % 2) * 512 + tt * 128:
                                             ((oc - 1) % 2) * 512 + (tt + 1) * 128], ident[:])
                                mmi.then_inc(evOtr, 1)
                        if dst is None:
                            eng.wait_ge(evDev, base_dev + 16)
                            eng.wait_ge(evOev, base_otr + 15)
                            for tt in range(4):
                                ppx = (pG, pH)[tt // 2]
                                mmi = eng.transpose(
                                    ppx[:, (tt % 2) * 128:(tt % 2 + 1) * 128],
                                    osub[:, (15 % 2) * 512 + tt * 128:
                                         (15 % 2) * 512 + (tt + 1) * 128], ident[:])
                            mmi.then_inc(evOtr, 1)

                    @blk.scalar
                    def _(eng, c=c):
                        for gc in range(16):
                            eng.wait_ge(evMM, base_mm + gc + 1)
                            if gc > 0:
                                eng.wait_ge(evHm, base_hm + gc)
                            for jj in range(4):
                                ins = eng.activation(sg[:, jj * 512:(jj + 1) * 512], pGU[jj][:], AF.Silu)
                            ins.then_inc(evSilu, 1)
                            eng.wait_ge(evHm, base_hm + gc + 1)
                            if gc > 0:
                                eng.wait_ge(evHS, base_hs + gc)
                            for jj in range(4):
                                ins = eng.activation(gx[:, (6 + jj) * 512:(7 + jj) * 512],
                                                     hh[:, jj * 512:(jj + 1) * 512], AF.Square)
                            ins.then_inc(evAbs, 1)

                    @blk.vector
                    def _(eng, c=c):
                        svn = [0]

                        def edge(ins):
                            ins.then_inc(svF, 1)
                            svn[0] += 1
                            eng.wait_ge(svF, svn[0] + svF_base[0])

                        for gc in range(16):
                            eng.wait_ge(evSilu, base_mm + gc + 1)
                            if gc > 0:
                                eng.wait_ge(evHS, base_hs + gc)
                            for jj in range(4):
                                ins = eng.tensor_mul(hh[:, jj * 512:(jj + 1) * 512],
                                                     sg[:, jj * 512:(jj + 1) * 512], pGU[4 + jj][:])
                            ins.then_inc(evHm, 1)
                            eng.wait_ge(evAbs, base_mm + gc + 1)
                            pairs = [(0, 1), (0, 2), (0, 3), (1, 2), (1, 3), (2, 3)]
                            for i, (a, b) in enumerate(pairs):
                                ins = eng.tensor_tensor(gx[:, i * 512:(i + 1) * 512],
                                                        gx[:, (6 + a) * 512:(7 + a) * 512],
                                                        gx[:, (6 + b) * 512:(7 + b) * 512], ALU.is_ge)
                            edge(ins)
                            eng.tensor_add(gx[:, 6 * 512:7 * 512], gx[:, 0:512], gx[:, 512:1024])
                            eng.tensor_add(gx[:, 7 * 512:8 * 512], gx[:, 3 * 512:4 * 512], gx[:, 4 * 512:5 * 512])
                            eng.tensor_add(gx[:, 8 * 512:9 * 512], gx[:, 512:1024], gx[:, 3 * 512:4 * 512])
                            ins = eng.tensor_add(gx[:, 9 * 512:10 * 512], gx[:, 2 * 512:3 * 512], gx[:, 4 * 512:5 * 512])
                            edge(ins)
                            eng.tensor_add(gx[:, 6 * 512:7 * 512], gx[:, 6 * 512:7 * 512], gx[:, 2 * 512:3 * 512])
                            eng.tensor_sub(gx[:, 7 * 512:8 * 512], gx[:, 0:512], gx[:, 7 * 512:8 * 512])
                            eng.tensor_sub(gx[:, 8 * 512:9 * 512], gx[:, 8 * 512:9 * 512], gx[:, 5 * 512:6 * 512])
                            ins = eng.tensor_add(gx[:, 9 * 512:10 * 512], gx[:, 9 * 512:10 * 512], gx[:, 5 * 512:6 * 512])
                            edge(ins)
                            eng.tensor_single_scalar(gx[:, 6 * 512:7 * 512], gx[:, 6 * 512:7 * 512], 1.5, ALU.is_ge)
                            eng.tensor_single_scalar(gx[:, 7 * 512:8 * 512], gx[:, 7 * 512:8 * 512], -0.5, ALU.is_le)
                            eng.tensor_single_scalar(gx[:, 8 * 512:9 * 512], gx[:, 8 * 512:9 * 512], 0.5, ALU.is_le)
                            ins = eng.tensor_single_scalar(gx[:, 9 * 512:10 * 512], gx[:, 9 * 512:10 * 512], 1.5, ALU.is_le)
                            edge(ins)
                            for jj in range(4):
                                ins = eng.tensor_mul(hs[:, (gc * 4 + jj) * TCF:(gc * 4 + jj + 1) * TCF],
                                                     hh[:, jj * 512:(jj + 1) * 512],
                                                     gx[:, (6 + jj) * 512:(7 + jj) * 512])
                            ins.then_inc(evHS, 1)
                        svF_base[0] += svn[0]
                        for oc in range(16):
                            eng.wait_ge(evMM2, base_mm2 + oc + 1)
                            swt(eng, 18 + (oc % 2), vo[oc])
                            ob = osub[:, (oc % 2) * 512:(oc % 2 + 1) * 512]
                            ins = eng.tensor_add(ob, ob, (pE, pF_)[oc % 2][:])
                            ins.then_inc(evDev, 1)
                            if dst is None:
                                if oc > 0:
                                    eng.wait_ge(evOtr, base_otr + oc)
                                    if not DBG["no_out"]:
                                        swt(eng, 20 + ((oc - 1) % 2), vout.get(oc - 3, vps2[(oc - 1) % 2]))
                                    o = ((oc - 1) % 2) * 512
                                    for tt in range(4):
                                        ppx = (pG, pH)[tt // 2]
                                        ins = eng.tensor_copy(ostg[:, o + tt * 128: o + (tt + 1) * 128],
                                                              ppx[:, (tt % 2) * 128:(tt % 2 + 1) * 128])
                                    ins.then_inc(evOev, 1)
                        if dst is None:
                            eng.wait_ge(evOtr, base_otr + 16)
                            if not DBG["no_out"]:
                                swt(eng, 21, vout.get(13, vps2[1]))
                            o = (15 % 2) * 512
                            for tt in range(4):
                                ppx = (pG, pH)[tt // 2]
                                ins = eng.tensor_copy(ostg[:, o + tt * 128: o + (tt + 1) * 128],
                                                      ppx[:, (tt % 2) * 128:(tt % 2 + 1) * 128])
                            ins.then_inc(evOev, 1)

    if "F1" in phases:
        phase_F(XR1, XR2, "f1")
        f_ebase[0] += 8

    if "FX" in phases:
        phase_F(XR2, XR3, "fx")
        f_ebase[0] += 8

    if "FZ" in phases:
        phase_F(XR1, None, "fz")
        f_ebase[0] += 8

    TCB = 512
    asb = es.enter_context(nc.sbuf_tensor("b_a", [128, 18], F32))
    if "B1" in phases:
        with ExitStack() as ph:
            xt0 = ph.enter_context(nc.sbuf_tensor("b1_xt0", [128, 512], F32))
            xt1 = ph.enter_context(nc.sbuf_tensor("b1_xt1", [128, 512], F32))
            x2t = ph.enter_context(nc.sbuf_tensor("b1_x2", [128, 512], F32))
            rowst = ph.enter_context(nc.sbuf_tensor("b1_rows", [1, 4 * 512], F32))
            xn = ph.enter_context(nc.sbuf_tensor("b1_xn", [128, 16 * TCB], BF16))
            wsl = ph.enter_context(nc.sbuf_tensor("b1_wsl", [128, 4 * 2048], BF16))
            qro = ph.enter_context(nc.sbuf_tensor("b1_qro", [128, 8 * TCB], F32))
            kro = ph.enter_context(nc.sbuf_tensor("b1_kro", [128, 8 * TCB], F32))
            kps = ph.enter_context(nc.sbuf_tensor("b1_kps", [128, 8 * TCB], F32))
            vb = ph.enter_context(nc.sbuf_tensor("b1_v", [128, 16 * TCB], BF16))
            cst = ph.enter_context(nc.sbuf_tensor("b1_cs", [128, 8 * TCB], F32))
            qstg = ph.enter_context(nc.sbuf_tensor("b1_qstg", [128, 2 * TCB], BF16))
            ksq = ph.enter_context(nc.sbuf_tensor("b1_ksq", [128, 4 * 512], F32))
            fkv = ph.enter_context(nc.sbuf_tensor("b1_fkv", [128, 512], F32))

            pA = ph.enter_context(nc.psum_tensor("b1_pA", [128, 512], F32))
            pB = ph.enter_context(nc.psum_tensor("b1_pB", [128, 512], F32))
            pC = ph.enter_context(nc.psum_tensor("b1_pC", [128, 512], F32))
            pE = ph.enter_context(nc.psum_tensor("b1_pE", [128, 512], F32))
            pF_ = ph.enter_context(nc.psum_tensor("b1_pF", [128, 512], F32))

            def RW(r):
                return rowst[0:1, r * 512:(r + 1) * 512]

            with nc.Block() as blk:
                st = Stepper(blk, sem, ctr, dsem, dctr)

                def az(t, e):
                    t.memset(asb[:], 0.0)

                def drn(t, e):
                    drain_all(e)
                    t.nop()

                st.step(vector=az, sync=drn)

            for c in range(S // TCB):
                with nc.Block() as blk:
                    st = Stepper(blk, sem, ctr, dsem, dctr)

                    for fc in range(16):
                        xt = (xt0, xt1)[fc % 2]

                        def ldx(t, e, fc=fc, xt=xt, c=c):
                            t.dma_start(out=xt[:], in_=XR2.ap()[fc, :, c * TCB:(c + 1) * TCB], inc=16)

                        st.step(sync=ldx)

                        def mms(t, e, fc=fc, xt=xt):
                            t.matmul(pA[0:1, :], ones[:, 0:1], xt[:],
                                     start=(fc == 0), stop=(fc == 15))

                        def sq(t, e, xt=xt):
                            t.activation(x2t[:], xt[:], AF.Square)

                        st.step(tensor=mms, scalar=sq)

                        def mmq(t, e, fc=fc):
                            t.matmul(pC[0:1, :], ones[:, 0:1], x2t[:],
                                     start=(fc == 0), stop=(fc == 15))

                        st.step(tensor=mmq)

                    def evs(t, e):
                        t.tensor_scalar_mul(RW(0), pA[0:1, :], 1.0 / D)
                        t.tensor_scalar_mul(RW(1), pC[0:1, :], 1.0 / D)

                    st.step(vector=evs)

                    def var1(t, e):
                        t.tensor_mul(RW(2), RW(0), RW(0))

                    st.step(vector=var1)

                    def var2(t, e):
                        t.tensor_sub(RW(1), RW(1), RW(2))

                    st.step(vector=var2)

                    def var3(t, e):
                        t.tensor_scalar_add(RW(1), RW(1), LN_EPS)

                    st.step(vector=var3)

                    def var4(t, e):
                        t.reciprocal(RW(2), RW(1))

                    st.step(vector=var4)

                    def var5(t, e):
                        t.activation(RW(2), RW(2), AF.Sqrt)

                    st.step(scalar=var5)

                    def var6(t, e):
                        t.tensor_mul(RW(3), RW(0), RW(2))

                    st.step(vector=var6)

                    def bcs(t, e):
                        t.matmul(pA[:], ones[0:1, :], RW(2), start=True, stop=True)
                        mmi = e.matmul(pC[:], ones[0:1, :], RW(3), start=True, stop=True)
                        mmi.then_inc(sem, 1)
                        ctr[0] += 1

                    st.step(tensor=bcs)

                    for fc in range(16):
                        xt = (xt0, xt1)[fc % 2]

                        def ldx2(t, e, fc=fc, xt=xt, c=c):
                            t.dma_start(out=xt[:], in_=XR2.ap()[fc, :, c * TCB:(c + 1) * TCB], inc=16)

                        st.step(sync=ldx2)

                        def nm1(t, e, xt=xt):
                            t.tensor_mul(x2t[:], xt[:], pA[:])

                        st.step(vector=nm1)

                        def nm2(t, e, fc=fc):
                            t.tensor_sub(xn[:, fc * TCB:(fc + 1) * TCB], x2t[:], pC[:])

                        st.step(vector=nm2)

                    def spxn(t, e, c=c):
                        t.dma_start(out=XN2B.ap()[:, :, c * TCB:(c + 1) * TCB]
                                    .rearrange("fc p t2 -> p fc t2"),
                                    in_=xn.ap().rearrange("p (fc t2) -> p fc t2", fc=16), inc=16)

                    st.step(sync=spxn)

                    def ldcs(t, e, c=c):
                        t.dma_start(out=cst.ap()[:, 0:4 * TCB].rearrange("p (pc t2) -> p pc t2", pc=4),
                                    in_=g["costab"].ap()[:, c * TCB:(c + 1) * TCB]
                                    .rearrange("(pc p) t2 -> p pc t2", p=128), inc=16)
                        t.dma_start(out=cst.ap()[:, 4 * TCB:8 * TCB].rearrange("p (pc t2) -> p pc t2", pc=4),
                                    in_=g["sintab"].ap()[:, c * TCB:(c + 1) * TCB]
                                    .rearrange("(pc p) t2 -> p pc t2", p=128), inc=16)

                    st.step(sync=ldcs)

                    def b1dst(oc):
                        if oc < 8:
                            return qro[:, oc * TCB:(oc + 1) * TCB], False
                        if oc < 16:
                            return qstg[:, (oc % 2) * TCB:(oc % 2 + 1) * TCB], True
                        if oc < 24:
                            return kro[:, (oc - 16) * TCB:(oc - 15) * TCB], False
                        if oc < 32:
                            return kps[:, (oc - 24) * TCB:(oc - 23) * TCB], False
                        return vb[:, (oc - 32) * TCB:(oc - 31) * TCB], False

                    def ldw0(t, e):
                        for j in range(2):
                            t.dma_start(out=wsl.ap()[:, j * 2048:(j + 1) * 2048],
                                        in_=CWIN.ap()[j * 128:(j + 1) * 128, :], inc=16)

                    st.step(sync=ldw0)

                    for op in range(24):
                        oc0, oc1 = 2 * op, 2 * op + 1

                        def pfw(t, e, op=op):
                            for j in range(2):
                                oc = 2 * op + 2 + j
                                if oc < 48:
                                    t.dma_start(out=wsl.ap()[:, (oc % 4) * 2048:(oc % 4 + 1) * 2048],
                                                in_=CWIN.ap()[oc * 128:(oc + 1) * 128, :], inc=16)

                        def qkvmm(t, e, oc0=oc0, oc1=oc1):
                            for oc, pp in ((oc0, pE), (oc1, pF_)):
                                sl = wsl[:, (oc % 4) * 2048:(oc % 4 + 1) * 2048]
                                for kc in range(16):
                                    mmi = e.matmul(pp[:], sl[:, kc * 128:(kc + 1) * 128],
                                                   xn[:, kc * TCB:(kc + 1) * TCB],
                                                   start=(kc == 0), stop=(kc == 15))
                            mmi.then_inc(sem, 1)
                            ctr[0] += 1

                        st.step(tensor=qkvmm, sync=pfw)

                        def ev(t, e, oc0=oc0, oc1=oc1):
                            d0, _ = b1dst(oc0)
                            d1, _ = b1dst(oc1)
                            t.activation(d0, pE[:], AF.Identity, bias=binc[:, oc0:oc0 + 1])
                            t.activation(d1, pF_[:], AF.Identity, bias=binc[:, oc1:oc1 + 1])

                        st.step(scalar=ev)

                        if 8 <= oc0 < 16:
                            def wr(t, e, oc0=oc0, oc1=oc1, c=c):
                                t.dma_start(out=QR.ap()[oc0, :, c * TCB:(c + 1) * TCB],
                                            in_=qstg[:, (oc0 % 2) * TCB:(oc0 % 2 + 1) * TCB], inc=16)
                                t.dma_start(out=QR.ap()[oc1, :, c * TCB:(c + 1) * TCB],
                                            in_=qstg[:, (oc1 % 2) * TCB:(oc1 % 2 + 1) * TCB], inc=16)

                            st.step(sync=wr)

                    for rot in (qro, kro):
                        for e_ in range(4):
                            sl = slice(e_ * TCB, (e_ + 1) * TCB)
                            slo = slice((e_ + 4) * TCB, (e_ + 5) * TCB)
                            cs = slice(e_ * TCB, (e_ + 1) * TCB)
                            sn = slice((4 + e_) * TCB, (5 + e_) * TCB)

                            def r1(t, e, rot=rot, sl=sl, slo=slo, cs=cs, sn=sn):
                                t.tensor_mul(x2t[:], rot[:, sl], cst[:, cs])
                                t.tensor_mul(ksq[:, 0:512], rot[:, slo], cst[:, sn])
                                t.tensor_mul(fkv[:], rot[:, slo], cst[:, cs])
                                t.tensor_mul(xt0[:], rot[:, sl], cst[:, sn])

                            st.step(vector=r1)

                            def r2(t, e, rot=rot, sl=sl, slo=slo):
                                t.tensor_sub(rot[:, sl], x2t[:], ksq[:, 0:512])
                                t.tensor_add(rot[:, slo], fkv[:], xt0[:])

                            st.step(vector=r2)

                    for op in range(4):
                        def wq(t, e, op=op):
                            t.tensor_copy(qstg[:, 0:TCB], qro[:, (2 * op) * TCB:(2 * op + 1) * TCB])
                            t.tensor_copy(qstg[:, TCB:2 * TCB], qro[:, (2 * op + 1) * TCB:(2 * op + 2) * TCB])

                        st.step(vector=wq)

                        def wq2(t, e, op=op, c=c):
                            t.dma_start(out=QR.ap()[2 * op, :, c * TCB:(c + 1) * TCB],
                                        in_=qstg[:, 0:TCB], inc=16)
                            t.dma_start(out=QR.ap()[2 * op + 1, :, c * TCB:(c + 1) * TCB],
                                        in_=qstg[:, TCB:2 * TCB], inc=16)

                        st.step(sync=wq2)

                    for q4 in range(4):
                        def sq1(t, e, q4=q4):
                            for j in range(4):
                                i = q4 * 4 + j
                                srck = kro if i < 8 else kps
                                off = i if i < 8 else i - 8
                                t.activation(ksq[:, j * 512:(j + 1) * 512],
                                             srck[:, off * TCB:(off + 1) * TCB], AF.Square)

                        st.step(scalar=sq1)

                        def mmn(t, e, q4=q4):
                            for j in range(4):
                                i = q4 * 4 + j
                                mmi = e.matmul(pA[0:1, :], ones[:, 0:1], ksq[:, j * 512:(j + 1) * 512],
                                               start=(i == 0), stop=(i == 15))
                            mmi.then_inc(sem, 1)
                            ctr[0] += 1

                        st.step(tensor=mmn)

                    def nrm1(t, e):
                        t.tensor_copy(RW(0), pA[0:1, :])

                    st.step(vector=nrm1)

                    def nrm2(t, e):
                        t.activation(RW(0), RW(0), AF.Sqrt)

                    st.step(scalar=nrm2)

                    def nrm3(t, e):
                        t.tensor_scalar_max(RW(0), RW(0), 1e-12)

                    st.step(vector=nrm3)

                    def nrm4(t, e):
                        t.reciprocal(RW(0), RW(0))

                    st.step(vector=nrm4)

                    def bci(t, e):
                        t.matmul(pA[:], ones[0:1, :], RW(0), start=True, stop=True)

                    st.step(tensor=bci)

                    for p2 in range(8):
                        i0, i1 = 2 * p2, 2 * p2 + 1

                        def fk1(t, e, i0=i0, i1=i1):
                            for j, i in enumerate((i0, i1)):
                                srck = kro if i < 8 else kps
                                off = i if i < 8 else i - 8
                                t.tensor_mul(ksq[:, (2 + j) * 512:(3 + j) * 512],
                                             srck[:, off * TCB:(off + 1) * TCB], pA[:])

                        st.step(vector=fk1)

                        def fk2(t, e, i0=i0, i1=i1):
                            for j, i in enumerate((i0, i1)):
                                t.tensor_mul(ksq[:, j * 512:(j + 1) * 512],
                                             ksq[:, (2 + j) * 512:(3 + j) * 512],
                                             vb[:, i * TCB:(i + 1) * TCB])

                        st.step(vector=fk2)

                        def fk3(t, e):
                            t.reduce_sum(asb[:, 16:17], ksq[:, 0:512], axis=AX.X)
                            t.reduce_sum(asb[:, 17:18], ksq[:, 512:1024], axis=AX.X)

                        st.step(vector=fk3)

                        def fk4(t, e, i0=i0, i1=i1):
                            t.tensor_add(asb[:, i0:i0 + 1], asb[:, i0:i0 + 1], asb[:, 16:17])
                            t.tensor_add(asb[:, i1:i1 + 1], asb[:, i1:i1 + 1], asb[:, 17:18])

                        st.step(vector=fk4)

    if "B2" in phases:
        with ExitStack() as ph:
            qrc2 = [ph.enter_context(nc.sbuf_tensor("b2_qr%d" % i, [128, 16 * TCB], BF16))
                    for i in range(2)]
            yb2 = [ph.enter_context(nc.sbuf_tensor("b2_y%d" % i, [128, 16 * TCB], BF16))
                   for i in range(2)]
            wsl = ph.enter_context(nc.sbuf_tensor("b2_wsl", [128, 4 * 2048], BF16))
            xnt = ph.enter_context(nc.sbuf_tensor("b2_xnt", [128, 2 * TCB], BF16))
            xrt = ph.enter_context(nc.sbuf_tensor("b2_xrt", [128, 2 * TCB], F32))
            ot = ph.enter_context(nc.sbuf_tensor("b2_ot", [128, 2 * TCB], F32))
            pE = ph.enter_context(nc.psum_tensor("b2_pE", [128, 512], F32))
            pF_ = ph.enter_context(nc.psum_tensor("b2_pF", [128, 512], F32))

            NC_ = S // TCB
            qld = [None, None]
            pend = [None]

            for c in range(NC_):
                qrc, yb = qrc2[c % 2], yb2[c % 2]
                qrn, ybn = qrc2[(c + 1) % 2], yb2[(c + 1) % 2]
                with nc.Block() as blk:
                    st = Stepper(blk, sem, ctr, dsem, dctr)

                    if c == 0:
                        def ldq(t, e):
                            t.dma_start(out=qrc.ap().rearrange("p (fc t2) -> p fc t2", fc=16),
                                        in_=QR.ap()[:, :, 0:TCB]
                                        .rearrange("fc p t2 -> p fc t2"), inc=16)

                        def ldw0(t, e):
                            for j in range(2):
                                t.dma_start(out=wsl.ap()[:, j * 2048:(j + 1) * 2048],
                                            in_=CWOUT.ap()[j * 128:(j + 1) * 128, :], inc=16)

                        st.step(sync=ldq, scalar=ldw0)

                        def ymul(t, e):
                            for fc in range(16):
                                t.tensor_single_scalar(yb[:, fc * TCB:(fc + 1) * TCB],
                                                       qrc[:, fc * TCB:(fc + 1) * TCB],
                                                       asb[:, fc:fc + 1], ALU.mult)

                        st.step(vector=ymul)

                    for op in range(8):
                        oc0, oc1 = 2 * op, 2 * op + 1

                        def pf(t, e, op=op):
                            for j in range(2):
                                oc = (2 * op + 2 + j) % 16
                                t.dma_start(
                                    out=wsl.ap()[:, (oc % 4) * 2048:(oc % 4 + 1) * 2048],
                                    in_=CWOUT.ap()[oc * 128:(oc + 1) * 128, :], inc=16)

                        def ldxr(t, e, oc0=oc0, oc1=oc1, c=c, op=op, pp=pend[0],
                                 qrn=qrn):
                            if pp is not None:
                                po0, po1, pc = pp
                                t.dma_start(out=XR3.ap()[po0, :, pc * TCB:(pc + 1) * TCB],
                                            in_=ot[:, 0:TCB], inc=16)
                                t.dma_start(out=XR3.ap()[po1, :, pc * TCB:(pc + 1) * TCB],
                                            in_=ot[:, TCB:2 * TCB], inc=16)
                            if op == 0 and c + 1 < NC_:
                                qld[(c + 1) % 2] = t.ldma(
                                    out=qrn.ap().rearrange("p (fc t2) -> p fc t2", fc=16),
                                    in_=QR.ap()[:, :, (c + 1) * TCB:(c + 2) * TCB]
                                    .rearrange("fc p t2 -> p fc t2"))
                            for j, oc in enumerate((oc0, oc1)):
                                t.dma_start(out=xrt[:, j * TCB:(j + 1) * TCB],
                                            in_=XR2.ap()[oc, :, c * TCB:(c + 1) * TCB], inc=16)
                                t.dma_start(out=xnt[:, j * TCB:(j + 1) * TCB],
                                            in_=XN2B.ap()[oc, :, c * TCB:(c + 1) * TCB], inc=16)

                        def omm(t, e, oc0=oc0, oc1=oc1, yb=yb):
                            for oc, pp in ((oc0, pE), (oc1, pF_)):
                                sl = wsl[:, (oc % 4) * 2048:(oc % 4 + 1) * 2048]
                                for kc in range(16):
                                    mmi = e.matmul(pp[:], sl[:, kc * 128:(kc + 1) * 128],
                                                   yb[:, kc * TCB:(kc + 1) * TCB],
                                                   start=(kc == 0), stop=(kc == 15))
                            mmi.then_inc(sem, 1)
                            ctr[0] += 1

                        engs = dict(tensor=omm, scalar=pf, sync=ldxr)
                        if c + 1 < NC_ and op >= 4:
                            def ymn(t, e, op=op, c=c, qrn=qrn, ybn=ybn):
                                if op == 4:
                                    t.dwait(qld[(c + 1) % 2])
                                for fc in range(4 * (op - 4), 4 * (op - 3)):
                                    t.tensor_single_scalar(
                                        ybn[:, fc * TCB:(fc + 1) * TCB],
                                        qrn[:, fc * TCB:(fc + 1) * TCB],
                                        asb[:, fc:fc + 1], ALU.mult)

                            engs["vector"] = ymn
                        st.step(**engs)
                        pend[0] = (oc0, oc1, c)

                        def oev(t, e, oc0=oc0, oc1=oc1):
                            t.activation(ot[:, 0:TCB], pE[:], AF.Identity,
                                         bias=bias[:, BOUT + oc0:BOUT + oc0 + 1])
                            t.activation(ot[:, TCB:2 * TCB], pF_[:], AF.Identity,
                                         bias=bias[:, BOUT + oc1:BOUT + oc1 + 1])

                        def xadd(t, e):
                            t.tensor_add(xrt[:], xrt[:], xnt[:])

                        st.step(scalar=oev, vector=xadd)

                        def oadd(t, e):
                            t.tensor_add(ot[:], ot[:], xrt[:])

                        st.step(vector=oadd)

                    if c + 1 == NC_:
                        def owr_last(t, e, pp=pend[0]):
                            po0, po1, pc = pp
                            t.dma_start(out=XR3.ap()[po0, :, pc * TCB:(pc + 1) * TCB],
                                        in_=ot[:, 0:TCB], inc=16)
                            t.dma_start(out=XR3.ap()[po1, :, pc * TCB:(pc + 1) * TCB],
                                        in_=ot[:, TCB:2 * TCB], inc=16)

                        st.step(sync=owr_last)

    if "F2" in phases:
        phase_F(XR3, None, "f2")
        f_ebase[0] += 8

    with nc.Block() as blk:
        st = Stepper(blk, sem, ctr, dsem, dctr)
        for name in taps:
            src = scrmap[name]

            def cp(t, e, src=src, name=name):
                drain_all(e)
                tt_ = nc.dram_tensor("tap_" + name, list(src.shape), src.dtype,
                                     kind="ExternalOutput")
                t.dma_start(out=tt_.ap(), in_=src.ap(), inc=16)

            st.step(sync=cp)

        def fin(t, e):
            drain_all(e)
            t.nop()

        st.step(sync=fin)

    es.close()
    global LAST_CTRS
    LAST_CTRS = (ctr[0], dctr[0])
    return nc


INPUT_NAMES = ["x", "wkT", "wvT", "bq_c", "bk_c", "bv_c", "bg_c", "bmb_c",
               "mem", "normT", "winT", "bin_c", "bout_c", "wq_sl", "wgt_sl",
               "wmb_sl", "wout_sl", "wg_p", "wu_p", "wd_p", "costab", "sintab",
               "ident", "identb", "ones", "onehot4"]


def kernel(**inputs):
    W = host_prep(inputs)
    x_all = np.ascontiguousarray(np.asarray(inputs["L0_attn_a_hidden_states"], np.float32))
    nc = build_nc()
    in_maps = []
    for b in range(NCORE):
        m = {k: W[k] for k in INPUT_NAMES if k != "x"}
        m["x"] = np.ascontiguousarray(x_all[b])
        in_maps.append(m)
    res = run_bass_kernel_spmd(nc, in_maps, core_ids=list(range(NCORE)))
    return np.stack([r["out"] for r in res.results], axis=0)
